# version 15
# baseline (speedup 1.0000x reference)
import contextlib
import numpy as np
import ml_dtypes
import concourse.bass as bass
import concourse.mybir as mybir
from concourse.bass_utils import run_bass_kernel_spmd

F32 = mybir.dt.float32
F32R = mybir.dt.float32r
BF16 = mybir.dt.bfloat16
AF = mybir.ActivationFunctionType
ALU = mybir.AluOpType
AX = mybir.AxisListType

NCORE = 8
B = 16
T = 2048
D = 1024
NB = B // NCORE
NT = NB * T
DC = D // 128
TT = 512
NTT = NT // TT
EPS = 1e-5

H0, DK0 = 16, 64
H1, HD1 = 16, 64
NM, DMIX, DDEC = 5, 32, 64
F_HEADS, F_DH, F_WIN = 8, 64, 512
F_INNER = F_HEADS * F_DH
F_HID = 2730
F_HIDP = 2816
F_NW = T // F_WIN

ENGINES = ("pe", "dve", "act", "pool", "sp")
_UNM = [0]


def unm(base):
    _UNM[0] += 1
    return f"{base}_{_UNM[0]}"

SECT = {"pe": "tensor", "dve": "vector", "act": "scalar", "pool": "gpsimd",
        "sp": "sync"}


class PB:
    POOL = 18

    def __init__(self, nc, es):
        self.nc = nc
        self.es = es
        self.ops = {e: [] for e in ENGINES}
        self.cnt = {}
        self.semh = {}
        self.hw = {e: {} for e in ENGINES}
        self.dma_sems = set()
        self.alias = {}
        self.alias_n = {}
        self.phase = "init"
        for e in ENGINES:
            self._sem("s_" + e)

    def _canon(self, qeng, name):
        key = (qeng, name)
        if key not in self.alias:
            i = self.alias_n.get(qeng, 0)
            assert i < self.POOL, f"stream pool exhausted on {qeng}"
            self.alias_n[qeng] = i + 1
            self.alias[key] = f"sq_{qeng}_{i}"
        return self.alias[key]

    def _sem(self, name):
        if name not in self.semh:
            self.semh[name] = self.es.enter_context(self.nc.semaphore(name))
            self.cnt[name] = 0
        return name

    def _waits(self, eng, after):
        ws = []
        hw = self.hw[eng]
        for tok in after:
            if tok is None:
                continue
            sn, v = tok
            if v <= 0 or hw.get(sn, 0) >= v:
                continue
            if sn in self.dma_sems and v < self.cnt[sn]:
                v = self.cnt[sn]
                if hw.get(sn, 0) >= v:
                    continue
            ws.append((sn, v))
            hw[sn] = v
        return ws

    def op(self, eng, fn, after=()):
        sn = "s_" + eng
        if eng == "pe":
            after = [t for t in after if t is not None and t[0] != sn]
        ws = self._waits(eng, after)
        self.cnt[sn] += 1
        tok = (sn, self.cnt[sn])
        self.ops[eng].append((ws, fn, (sn, 1), self.phase))
        return tok

    def dma(self, qeng, stream, out, in_, after=()):
        sn = self._sem(self._canon(qeng, stream))
        self.dma_sems.add(sn)
        ws = self._waits(qeng, after)
        self.cnt[sn] += 16
        tok = (sn, self.cnt[sn])

        def fn(e, out=out, in_=in_):
            return e.dma_start(out=out, in_=in_)
        self.ops[qeng].append((ws, fn, (sn, 16), self.phase))
        return tok

    def cur(self, stream, qeng=None):
        if qeng is None:
            hits = [v for (q, n), v in self.alias.items() if n == stream]
            assert len(hits) == 1, f"ambiguous stream {stream}: {hits}"
            sn = hits[0]
        else:
            sn = self._canon(qeng, stream)
        return (sn, self.cnt[sn])

    def barrier(self):
        toks = [(sn, c) for sn, c in self.cnt.items() if c > 0]
        for e in ENGINES:
            ws = self._waits(e, toks)
            if ws:
                self.ops[e].append((ws, None, None, self.phase))
        self.alias = {}
        self.alias_n = {}

    def emit(self):
        block = self.es.enter_context(self.nc.Block())
        for eng in ENGINES:
            lst = self.ops[eng]
            semh = self.semh

            nc = self.nc

            def body(e, lst=lst):
                import contextlib as _cl
                cur_ph = None
                stk = _cl.ExitStack()
                for (ws, fn, inc, phname) in lst:
                    if phname != cur_ph:
                        stk.close()
                        stk = _cl.ExitStack()
                        stk.enter_context(nc.named_scope(phname))
                        cur_ph = phname
                    for (sn, v) in ws:
                        e.wait_ge(semh[sn], v)
                    if fn is not None:
                        inst = fn(e)
                        if inc is not None:
                            inst.then_inc(semh[inc[0]], inc[1])
                stk.close()
            getattr(block, SECT[eng])(body)


class Buf:
    __slots__ = ("ap", "wtok", "rtoks")

    def __init__(self, ap):
        self.ap = ap
        self.wtok = None
        self.rtoks = []

    def deps_w(self):
        d = list(self.rtoks)
        if self.wtok is not None:
            d.append(self.wtok)
        return d

    def wrote(self, tok):
        self.wtok = tok
        self.rtoks = []

    def read(self, tok):
        self.rtoks.append(tok)


def _tril_maskT():
    return np.triu(np.ones((128, 128), dtype=np.float32))


def _rot_perm():
    P = np.zeros((128, 128), dtype=np.float32)
    for hb in (0, 64):
        for j in range(32):
            P[hb + j + 32, hb + j] = -1.0
        for j in range(32, 64):
            P[hb + j - 32, hb + j] = 1.0
    return P.astype(ml_dtypes.bfloat16)


def _rotary_tables():
    n = T
    half = np.arange(0, F_DH, 2, dtype=np.float64)
    inv_freq = 1.0 / 10000.0 ** (half / F_DH)
    t = np.arange(n, dtype=np.float64)
    freqs = t[:, None] * inv_freq[None, :]
    freqs = np.concatenate([freqs, freqs], axis=1)
    base = (half + 0.4 * F_DH) / (1.4 * F_DH)
    power = (t - n // 2) / float(F_WIN // 2)
    scale = base[None, :] ** power[:, None]
    scale = np.concatenate([scale, scale], axis=1)
    cos, sin = np.cos(freqs), np.sin(freqs)
    qcs = (cos * scale).T
    qss = (sin * scale).T
    kcs = (cos / scale).T
    kss = (sin / scale).T

    def two(x):
        return np.concatenate([x, x], axis=0).astype(ml_dtypes.bfloat16)
    return two(qcs), two(qss), two(kcs), two(kss)


def _locattn_masks():
    s_ = np.arange(1024)
    mq = np.zeros((128, 4 * 1024), dtype=np.float32)
    for r in range(4):
        for tl in range(128):
            i_rel = 512 + r * 128 + tl
            bad = ~((0 <= (i_rel - s_)) & ((i_rel - s_) <= 512))
            mq[tl, r * 1024:(r + 1) * 1024] = np.where(bad, -1e30, 0.0)
    mt = np.zeros((128, 8 * 512), dtype=np.float32)
    tloc = np.arange(512)
    for ss in range(8):
        for sl in range(128):
            j_rel = ss * 128 + sl
            i_rel = 512 + tloc
            bad = ~((0 <= (i_rel - j_rel)) & ((i_rel - j_rel) <= 512))
            mt[sl, ss * 512:(ss + 1) * 512] = np.where(bad, -1e30, 0.0)
    return mq.astype(ml_dtypes.bfloat16), mt.astype(ml_dtypes.bfloat16)


def _perpart(v):
    v = np.asarray(v, dtype=np.float32).reshape(-1)
    n = v.shape[0] // 128
    return np.ascontiguousarray(v.reshape(n, 128).T)


def const_to_sbuf(ctx, name, np_arr, dtype):
    nc, pb, es = ctx["nc"], ctx["pb"], ctx["es"]
    h = nc.inline_tensor(np_arr, name)
    tile = es.enter_context(nc.sbuf_tensor(name + "_sb", list(np_arr.shape), dtype))
    tok = pb.dma("sp", "s_const", tile[:], h.ap())
    b = Buf(tile[:])
    b.wrote(tok)
    return b


def ph_layernorm(ctx, x_ap, xn_ap, z_ap=None, g_ap=None, b_ap=None):
    nc, pb, es = ctx["nc"], ctx["pb"], ctx["es"]
    ones_r = ctx["ones_f32r"]
    with contextlib.ExitStack() as ph:
        sb = lambda n, s, d: ph.enter_context(nc.sbuf_tensor(unm(n), s, d))
        xt = [Buf(sb(f"ln_x{i}", [128, DC * TT], F32R)[:]) for i in range(2)]
        sq = [Buf(sb(f"ln_sq{i}", [128, DC * TT], F32R)[:]) for i in range(2)]
        M = [Buf(sb(f"ln_m{i}", [128, TT], F32)[:]) for i in range(2)]
        RS = [Buf(sb(f"ln_rs{i}", [128, TT], F32)[:]) for i in range(2)]
        VAR = [Buf(sb(f"ln_v{i}", [128, TT], F32)[:]) for i in range(2)]
        SCR = [[Buf(sb(f"ln_s{i}{e}", [128, TT], F32)[:]) for e in range(2)]
               for i in range(2)]
        xnb = [Buf(sb(f"ln_xn{i}", [128, DC * TT], BF16)[:]) for i in range(2)]
        znb = [Buf(sb(f"ln_zn{i}", [128, DC * TT], F32)[:]) for i in range(2)] \
            if z_ap is not None else None
        psum = ph.enter_context(nc.psum_tensor(unm("ln_ps"), [128, 4 * TT], F32))
        ps = [Buf(psum[:, i * TT:(i + 1) * TT]) for i in range(4)]

        for t in range(NTT):
            ib = t % 2
            X, SQ, Mb, RSb, Vb, XN = xt[ib], sq[ib], M[ib], RS[ib], VAR[ib], xnb[ib]
            ZN = znb[ib] if znb is not None else None
            p1, p2 = ps[2 * ib], ps[2 * ib + 1]
            ltoks = []
            for c in range(DC):
                tok = pb.dma("sp", f"s_lnx{t % 3}",
                             X.ap[:, c * TT:(c + 1) * TT],
                             x_ap[c * 128:(c + 1) * 128, t * TT:(t + 1) * TT]
                             .bitcast(F32R),
                             after=X.deps_w())
                ltoks.append(tok)
            X.wrote(ltoks[-1])
            stoks = []
            for c in range(DC):
                tok = pb.op("act", (lambda e, c=c, X=X, SQ=SQ: e.activation(
                    SQ.ap[:, c * TT:(c + 1) * TT],
                    X.ap.bitcast(F32)[:, c * TT:(c + 1) * TT], AF.Square)),
                    after=[X.wtok] + SQ.deps_w())
                stoks.append(tok)
            SQ.wrote(stoks[-1])
            mtok = None
            for c in range(DC):
                mtok = pb.op("pe", (lambda e, c=c, X=X, p1=p1: e.matmul(
                    p1.ap, ones_r.ap, X.ap[:, c * TT:(c + 1) * TT],
                    start=(c == 0), stop=(c == DC - 1))),
                    after=[X.wtok, ones_r.wtok] + (p1.deps_w() if c == 0 else []))
            p1.wrote(mtok)
            for c in range(DC):
                mtok = pb.op("pe", (lambda e, c=c, SQ=SQ, p2=p2: e.matmul(
                    p2.ap, ones_r.ap, SQ.ap[:, c * TT:(c + 1) * TT],
                    start=(c == 0), stop=(c == DC - 1))),
                    after=[stoks[-1], ones_r.wtok] + (p2.deps_w() if c == 0 else []))
            p2.wrote(mtok)
            tok = pb.op("dve", (lambda e, Mb=Mb, p1=p1: e.tensor_scalar_mul(
                Mb.ap, p1.ap, 1.0 / D)), after=[p1.wtok] + Mb.deps_w())
            Mb.wrote(tok); p1.read(tok)
            tok = pb.op("dve", (lambda e, Vb=Vb, Mb=Mb: e.tensor_mul(
                Vb.ap, Mb.ap, Mb.ap)), after=[Mb.wtok] + Vb.deps_w())
            Vb.wrote(tok); Mb.read(tok)
            tok = pb.op("dve", (lambda e, Vb=Vb, p2=p2: e.scalar_tensor_tensor(
                Vb.ap, p2.ap, 1.0 / D, Vb.ap, ALU.mult, ALU.subtract)),
                after=[p2.wtok, Vb.wtok])
            Vb.wrote(tok); p2.read(tok)
            cp = ctx["cpool"]
            ci = ctx["cidx"][EPS]
            tok = pb.op("act", (lambda e, RSb=RSb, Vb=Vb, cp=cp, ci=ci:
                                e.activation(RSb.ap, Vb.ap, AF.Ln,
                                             bias=cp.ap[:, ci:ci + 1])),
                        after=[Vb.wtok, cp.wtok] + RSb.deps_w())
            RSb.wrote(tok); Vb.read(tok); cp.read(tok)
            tok = pb.op("act", (lambda e, RSb=RSb: e.activation(
                RSb.ap, RSb.ap, AF.Exp, scale=-0.5)), after=[RSb.wtok])
            RSb.wrote(tok)
            ztoks = [None] * DC
            xtoks = [None] * DC
            for c in range(DC):
                ei = c % 2
                eng = "dve" if ei == 0 else "pool"
                Sc = SCR[ib][ei]
                sl = slice(c * TT, (c + 1) * TT)
                tok = pb.op(eng, (lambda e, sl=sl, Sc=Sc, Mb=Mb, X=X:
                                  e.tensor_sub(Sc.ap,
                                               X.ap.bitcast(F32)[:, sl], Mb.ap)),
                            after=[X.wtok, Mb.wtok] + Sc.deps_w())
                Sc.wrote(tok); Mb.read(tok); X.read(tok)
                if g_ap is not None:
                    tok = pb.op(eng, (lambda e, Sc=Sc, RSb=RSb:
                                      e.tensor_mul(Sc.ap, Sc.ap, RSb.ap)),
                                after=[Sc.wtok, RSb.wtok])
                    Sc.wrote(tok); RSb.read(tok)
                    dst = ZN if ZN is not None else XN
                    tok2 = pb.op("dve", (lambda e, c=c, sl=sl, Sc=Sc, dst=dst:
                                       e.tensor_scalar(
                                           dst.ap[:, sl], Sc.ap,
                                           g_ap.ap[:, c:c + 1], b_ap.ap[:, c:c + 1],
                                           ALU.mult, ALU.add)),
                                 after=[Sc.wtok, g_ap.wtok, b_ap.wtok] + dst.deps_w())
                    dst.wrote(tok2); Sc.read(tok2)
                    g_ap.read(tok2); b_ap.read(tok2)
                else:
                    dst = ZN if ZN is not None else XN
                    tok2 = pb.op(eng, (lambda e, sl=sl, Sc=Sc, RSb=RSb, dst=dst:
                                       e.tensor_mul(dst.ap[:, sl], Sc.ap, RSb.ap)),
                                 after=[Sc.wtok, RSb.wtok] + dst.deps_w())
                    dst.wrote(tok2); Sc.read(tok2); RSb.read(tok2)
                if ZN is not None:
                    ztoks[c] = tok2
                    tok3 = pb.op(eng, (lambda e, sl=sl, XN=XN, ZN=ZN:
                                       e.tensor_copy(XN.ap[:, sl], ZN.ap[:, sl])),
                                 after=[tok2] + XN.deps_w())
                    XN.wrote(tok3); ZN.read(tok3)
                    xtoks[c] = tok3
                else:
                    xtoks[c] = tok2
            for c in range(DC):
                sl = slice(c * TT, (c + 1) * TT)
                pb.dma("act", f"s_lnst{t % 3}",
                       xn_ap[c * 128:(c + 1) * 128, t * TT:(t + 1) * TT],
                       XN.ap[:, sl], after=[xtoks[c]])
                if ZN is not None:
                    pb.dma("sp", f"s_lnzst{t % 3}",
                           z_ap[c * 128:(c + 1) * 128, t * TT:(t + 1) * TT],
                           ZN.ap[:, sl], after=[ztoks[c]])
            XN.rtoks.append(pb.cur(f"s_lnst{t % 3}"))
            if ZN is not None:
                ZN.rtoks.append(pb.cur(f"s_lnzst{t % 3}"))
        pb.barrier()


def ph_ffn0_half(ctx, xn_ap, x_ap, w1_ap, b1t_ap, w2_ap, b2t_ap, acc_ap, half):
    nc, pb, es = ctx["nc"], ctx["pb"], ctx["es"]
    NH = 2048 // 128
    h0 = half * 2048
    with contextlib.ExitStack() as ph:
        sb = lambda n, s, d: ph.enter_context(nc.sbuf_tensor(unm(n), s, d))
        w1 = Buf(sb("f0_w1", [128, DC * 2048], BF16)[:])
        w2 = Buf(sb("f0_w2", [128, NH * 1024], BF16)[:])
        b1 = Buf(sb("f0_b1", [128, NH], F32)[:])
        b2 = Buf(sb("f0_b2", [128, DC], F32)[:])
        xn = [Buf(sb(f"f0_xn{i}", [128, DC * TT], BF16)[:]) for i in range(2)]
        xr = [Buf(sb(f"f0_x{i}", [128, DC * TT], F32)[:]) for i in range(2)]
        hb = [Buf(sb(f"f0_h{i}", [128, NH * TT], BF16)[:]) for i in range(2)]
        ob = [Buf(sb(f"f0_o{i}", [128, DC * TT], F32)[:]) for i in range(1)]
        psum = ph.enter_context(nc.psum_tensor(unm("f0_ps"), [128, 8 * TT], F32))
        ps = [Buf(psum[:, i * TT:(i + 1) * TT]) for i in range(8)]

        w1toks = [None] * DC
        w2toks = [None] * NH

        for k in range(DC):
            pb.dma("pool", f"s_w1_{k}", w1.ap[:, k * 2048:(k + 1) * 2048],
                   w1_ap[k * 128:(k + 1) * 128, h0:h0 + 2048],
                   after=w1.deps_w())
        for k in range(NH):
            pb.dma("pool", f"s_w2_{k % 8}", w2.ap[:, k * 1024:(k + 1) * 1024],
                   w2_ap[h0 + k * 128:h0 + (k + 1) * 128, :],
                   after=w2.deps_w())
        w1toks = [pb.cur(f"s_w1_{k}") for k in range(DC)]
        w2toks = [pb.cur(f"s_w2_{k % 8}") for k in range(NH)]
        pb.dma("sp", "s_b", b1.ap, b1t_ap[:, half * NH:(half + 1) * NH],
               after=b1.deps_w())
        if half == 0:
            pb.dma("sp", "s_b", b2.ap, b2t_ap, after=b2.deps_w())
        b1.wrote(pb.cur("s_b"))
        b2.wrote(pb.cur("s_b"))

        for t in range(NTT):
            ib = t % 2
            XN, XR, HB, OB = xn[ib], xr[ib], hb[ib], ob[0]
            xtoks = []
            for c in range(DC):
                tok = pb.dma("sp", f"s_fxn{t % 3}",
                             XN.ap[:, c * TT:(c + 1) * TT],
                             xn_ap[c * 128:(c + 1) * 128, t * TT:(t + 1) * TT],
                             after=XN.deps_w())
                xtoks.append(tok)
            XN.wrote(xtoks[-1])
            rtoks = []
            rsrc = x_ap if half == 0 else acc_ap
            for c in range(DC):
                tok = pb.dma("act", f"s_fx{t % 3}",
                             XR.ap[:, c * TT:(c + 1) * TT],
                             rsrc[c * 128:(c + 1) * 128, t * TT:(t + 1) * TT],
                             after=XR.deps_w())
                rtoks.append(tok)
            XR.wrote(rtoks[-1])
            htoks = []
            for hc in range(NH):
                p = ps[hc % 4]
                mtok = None
                for k in range(DC):
                    mtok = pb.op("pe", (lambda e, k=k, hc=hc, p=p, XN=XN: e.matmul(
                        p.ap, w1.ap[:, k * 2048 + hc * 128: k * 2048 + hc * 128 + 128],
                        XN.ap[:, k * TT:(k + 1) * TT],
                        start=(k == 0), stop=(k == DC - 1))),
                        after=[XN.wtok, w1toks[k]] + (p.deps_w() if k == 0 else []))
                p.wrote(mtok)
                w1.read(mtok); XN.read(mtok)
                tok = pb.op("act", (lambda e, hc=hc, p=p, HB=HB: e.activation(
                    HB.ap[:, hc * TT:(hc + 1) * TT], p.ap, AF.Gelu,
                    bias=b1.ap[:, hc:hc + 1])),
                    after=[p.wtok, b1.wtok] + HB.deps_w())
                htoks.append(tok)
                p.read(tok); b1.read(tok)
            HB.wrote(htoks[-1])
            hread = [[] for _ in range(NH)]
            for oc in range(DC):
                p = ps[4 + oc % 4]
                mtok = None
                for k in range(NH):
                    mtok = pb.op("pe", (lambda e, k=k, oc=oc, p=p, HB=HB: e.matmul(
                        p.ap, w2.ap[:, k * 1024 + oc * 128: k * 1024 + oc * 128 + 128],
                        HB.ap[:, k * TT:(k + 1) * TT],
                        start=(k == 0), stop=(k == NH - 1))),
                        after=[htoks[k], w2toks[k]] + (p.deps_w() if k == 0 else []))
                    hread[k].append(mtok)
                p.wrote(mtok)
                w2.read(mtok)
                if half == 0:
                    tok = pb.op("dve", (lambda e, oc=oc, p=p, OB=OB, XR=XR:
                                        e.scalar_tensor_tensor(
                                            OB.ap[:, oc * TT:(oc + 1) * TT], p.ap,
                                            b2.ap[:, oc:oc + 1],
                                            XR.ap[:, oc * TT:(oc + 1) * TT],
                                            ALU.add, ALU.add)),
                                after=[p.wtok, XR.wtok, b2.wtok] + OB.deps_w())
                    b2.read(tok)
                else:
                    tok = pb.op("dve", (lambda e, oc=oc, p=p, OB=OB, XR=XR:
                                        e.tensor_add(
                                            OB.ap[:, oc * TT:(oc + 1) * TT], p.ap,
                                            XR.ap[:, oc * TT:(oc + 1) * TT])),
                                after=[p.wtok, XR.wtok] + OB.deps_w())
                OB.wrote(tok); p.read(tok); XR.read(tok)
                stok = pb.dma("act", f"s_fst{t % 3}",
                              acc_ap[oc * 128:(oc + 1) * 128, t * TT:(t + 1) * TT],
                              OB.ap[:, oc * TT:(oc + 1) * TT], after=[OB.wtok])
                OB.read(stok)
            for k in range(NH):
                HB.rtoks.extend(hread[k][-1:])
        pb.barrier()


def ph_ffn0(ctx, xn_ap, x_ap, w1_ap, b1t_ap, w2_ap, b2t_ap, acc_ap):
    ph_ffn0_half(ctx, xn_ap, x_ap, w1_ap, b1t_ap, w2_ap, b2t_ap, acc_ap, 0)
    ph_ffn0_half(ctx, xn_ap, x_ap, w1_ap, b1t_ap, w2_ap, b2t_ap, acc_ap, 1)


def ph_attn0_half(ctx, xn_ap, wqkv_ap, y_ap, half):
    nc, pb, es = ctx["nc"], ctx["pb"], ctx["es"]
    ident = ctx["ident_bf"]
    bdiag = ctx["bdiag_bf"]
    trilT = ctx["trilT_f32"]
    CH = 128
    NCH = T // CH
    with contextlib.ExitStack() as ph:
        sb = lambda n, s, d: ph.enter_context(nc.sbuf_tensor(unm(n), s, d))
        wq = Buf(sb("a0_wq", [128, DC * 512], BF16)[:])
        wk = Buf(sb("a0_wk", [128, DC * 512], BF16)[:])
        wv = Buf(sb("a0_wv", [128, DC * 512], BF16)[:])
        qf = Buf(sb("a0_qf", [128, 4 * NT], BF16)[:])
        kf = Buf(sb("a0_kf", [128, 4 * NT], BF16)[:])
        vtm = Buf(sb("a0_vtm", [128, (NT // 128) * 512], BF16)[:])
        yb = Buf(sb("a0_yb", [128, 4 * NT], BF16)[:])

        for k in range(DC):
            for nm, wb, c0 in (("q", wq, half * 512),
                               ("k", wk, 1024 + half * 512),
                               ("v", wv, 2048 + half * 512)):
                pb.dma("pool", f"s_a0w{k % 4}",
                       wb.ap[:, k * 512:(k + 1) * 512],
                       wqkv_ap[k * 128:(k + 1) * 128, c0:c0 + 512],
                       after=wb.deps_w())
        wtoks = {nm: [pb.cur(f"s_a0w{k % 4}") for k in range(DC)]
                 for nm in ("q", "k", "v")}

        with contextlib.ExitStack() as st1:
            sb1 = lambda n, s, d: st1.enter_context(nc.sbuf_tensor(unm(n), s, d))
            xn = [Buf(sb1("a0_xn", [128, DC * TT], BF16)[:]) for _ in range(1)]
            NSC = 2
            SQs = [Buf(sb1("a0_sq", [128, TT], BF16)[:]) for _ in range(NSC)]
            KRBs = [Buf(sb1("a0_krb", [128, TT], BF16)[:]) for _ in range(NSC)]
            MEANs = [Buf(sb1("a0_mean", [128, TT], F32)[:]) for _ in range(NSC)]
            VARs = [Buf(sb1("a0_var", [128, TT], F32)[:]) for _ in range(NSC)]
            RSTDs = [Buf(sb1("a0_rstd", [128, TT], F32)[:]) for _ in range(NSC)]
            SCRs = [Buf(sb1("a0_scr", [128, TT], F32)[:]) for _ in range(NSC)]
            psums = st1.enter_context(
                nc.psum_tensor(unm("a0_ps"), [128, 8 * TT], F32))
            pjq = [Buf(psums[:, i * TT:(i + 1) * TT]) for i in range(4)]
            pst = [Buf(psums[:, (4 + i) * TT:(5 + i) * TT]) for i in range(4)]
            kit = [0]
            pend1 = [None]
            pend2 = [None]

            def emit_k1(st):
                p, KRB, SQ = st["p"], st["KRB"], st["SQ"]
                MEAN, VAR = st["MEAN"], st["VAR"]
                p2, p3 = st["p2"], st["p3"]
                tok = pb.op("act", (lambda e: e.copy(
                    KRB.ap, p.ap)), after=[p.wtok] + KRB.deps_w())
                KRB.wrote(tok); p.read(tok)
                tok = pb.op("pool", (lambda e: e.tensor_mul(
                    SQ.ap, KRB.ap, KRB.ap)),
                    after=[KRB.wtok] + SQ.deps_w())
                SQ.wrote(tok); KRB.read(tok)
                tok = pb.op("pe", (lambda e: e.matmul(
                    p2.ap, bdiag.ap, KRB.ap, start=True, stop=True)),
                    after=[KRB.wtok, bdiag.wtok] + p2.deps_w())
                p2.wrote(tok); KRB.read(tok); bdiag.read(tok)
                tok = pb.op("pe", (lambda e: e.matmul(
                    p3.ap, bdiag.ap, SQ.ap, start=True, stop=True)),
                    after=[SQ.wtok, bdiag.wtok] + p3.deps_w())
                p3.wrote(tok); SQ.read(tok); bdiag.read(tok)
                tok = pb.op("dve", (lambda e: e.tensor_scalar_mul(
                    MEAN.ap, p2.ap, 1.0 / 64)),
                    after=[p2.wtok] + MEAN.deps_w())
                MEAN.wrote(tok); p2.read(tok)
                tok = pb.op("dve", (lambda e: e.tensor_mul(
                    VAR.ap, MEAN.ap, MEAN.ap)),
                    after=[MEAN.wtok] + VAR.deps_w())
                VAR.wrote(tok); MEAN.read(tok)
                tok = pb.op("dve", (lambda e: e.scalar_tensor_tensor(
                    VAR.ap, p3.ap, 1.0 / 64, VAR.ap,
                    ALU.mult, ALU.subtract)),
                    after=[p3.wtok, VAR.wtok])
                VAR.wrote(tok); p3.read(tok)

            def emit_k2(st):
                KRB, MEAN, VAR = st["KRB"], st["MEAN"], st["VAR"]
                RSTD, SCR, dstb, sl = st["RSTD"], st["SCR"], st["dstb"], st["sl"]
                cp_, ci = ctx["cpool"], ctx["cidx"][EPS]
                tok = pb.op("act", (lambda e, cp_=cp_, ci=ci: e.activation(
                    RSTD.ap, VAR.ap, AF.Ln, bias=cp_.ap[:, ci:ci + 1])),
                    after=[VAR.wtok, cp_.wtok] + RSTD.deps_w())
                RSTD.wrote(tok); VAR.read(tok); cp_.read(tok)
                tok = pb.op("act", (lambda e: e.activation(
                    RSTD.ap, RSTD.ap, AF.Exp, scale=-0.5)),
                    after=[RSTD.wtok])
                RSTD.wrote(tok)
                tok = pb.op("dve", (lambda e: e.tensor_sub(
                    SCR.ap, KRB.ap, MEAN.ap)),
                    after=[KRB.wtok, MEAN.wtok] + SCR.deps_w())
                SCR.wrote(tok); KRB.read(tok); MEAN.read(tok)
                tok = pb.op("dve", (lambda e, sl=sl: e.tensor_mul(
                    dstb.ap[:, sl], SCR.ap, RSTD.ap)),
                    after=[SCR.wtok, RSTD.wtok] + dstb.deps_w())
                dstb.rtoks.append(tok)
                SCR.read(tok); RSTD.read(tok)

            for t in range(NTT):
                XN = xn[0]
                xtoks = []
                for c in range(DC):
                    tok = pb.dma("sp", f"s_a0x{t % 3}",
                                 XN.ap[:, c * TT:(c + 1) * TT],
                                 xn_ap[c * 128:(c + 1) * 128,
                                       t * TT:(t + 1) * TT],
                                 after=XN.deps_w())
                    xtoks.append(tok)
                XN.wrote(xtoks[-1])
                for nm, wb, dstb in (("q", wq, qf), ("k", wk, kf)):
                    for hp in range(4):
                        p = pjq[hp]
                        mtok = None
                        for k in range(DC):
                            mtok = pb.op("pe", (lambda e, k=k, hp=hp, p=p, XN=XN,
                                                wb=wb: e.matmul(
                                p.ap,
                                wb.ap[:, k * 512 + hp * 128:k * 512 + hp * 128 + 128],
                                XN.ap[:, k * TT:(k + 1) * TT],
                                start=(k == 0), stop=(k == DC - 1))),
                                after=[XN.wtok, wtoks[nm][k]] +
                                      (p.deps_w() if k == 0 else []))
                        p.wrote(mtok)
                        wb.read(mtok); XN.read(mtok)
                        sl = slice(hp * NT + t * TT, hp * NT + (t + 1) * TT)
                        if nm == "q":
                            tok = pb.op("act", (lambda e, p=p, dstb=dstb, sl=sl:
                                                e.copy(dstb.ap[:, sl], p.ap)),
                                        after=[p.wtok] + dstb.deps_w())
                            dstb.rtoks.append(tok)
                            p.read(tok)
                        else:
                            it = kit[0]
                            kit[0] += 1
                            st = {"p": p, "KRB": KRBs[it % NSC],
                                  "SQ": SQs[it % NSC], "MEAN": MEANs[it % NSC],
                                  "VAR": VARs[it % NSC], "RSTD": RSTDs[it % NSC],
                                  "SCR": SCRs[it % NSC],
                                  "p2": pst[2 * (it % 2)],
                                  "p3": pst[2 * (it % 2) + 1],
                                  "dstb": dstb, "sl": sl}
                            if pend1[0] is not None:
                                emit_k1(pend1[0])
                            if pend2[0] is not None:
                                emit_k2(pend2[0])
                            pend2[0] = pend1[0]
                            pend1[0] = st
                if pend1[0] is not None:
                    emit_k1(pend1[0])
                if pend2[0] is not None:
                    emit_k2(pend2[0])
                if pend1[0] is not None:
                    emit_k2(pend1[0])
                pend1[0] = None
                pend2[0] = None
                for ts in range(4):
                    p = pjq[ts]
                    mtok = None
                    for k in range(DC):
                        mtok = pb.op("pe", (lambda e, k=k, ts=ts, p=p, XN=XN:
                                            e.matmul(
                            p.ap,
                            XN.ap[:, k * TT + ts * 128:k * TT + ts * 128 + 128],
                            wv.ap[:, k * 512:(k + 1) * 512],
                            start=(k == 0), stop=(k == DC - 1))),
                            after=[XN.wtok, wtoks["v"][k]] +
                                  (p.deps_w() if k == 0 else []))
                    p.wrote(mtok)
                    wv.read(mtok); XN.read(mtok)
                    tc = t * 4 + ts
                    tok = pb.op("act", (lambda e, p=p, tc=tc: e.copy(
                        vtm.ap[:, tc * 512:(tc + 1) * 512], p.ap)),
                        after=[p.wtok] + vtm.deps_w())
                    vtm.rtoks.append(tok)
                    p.read(tok)
            if pend1[0] is not None:
                emit_k1(pend1[0])
            if pend2[0] is not None:
                emit_k2(pend2[0])
            if pend1[0] is not None:
                emit_k2(pend1[0])
            pend1[0] = None
            pend2[0] = None
            for bb in (qf, kf, vtm):
                bb.wrote(bb.rtoks[-1])
                bb.rtoks = []
            pb.barrier()

        with contextlib.ExitStack() as st2:
            sb2 = lambda n, s, d: st2.enter_context(nc.sbuf_tensor(unm(n), s, d))
            stbf = Buf(sb2("a0_stbf", [128, 8 * 64], BF16)[:])
            stf32 = Buf(sb2("a0_stf32", [128, 8 * 64], F32)[:])
            abf = [Buf(sb2("a0_abf", [128, 128], BF16)[:]) for _ in range(4)]
            ktmc = [Buf(sb2("a0_ktmc", [128, 64], BF16)[:]) for _ in range(4)]
            pS = [Buf(st2.enter_context(
                nc.psum_tensor(unm("a0_pS"), [128, 512], F32))[:])
                for _ in range(2)]
            pA = [Buf(st2.enter_context(
                nc.psum_tensor(unm("a0_pA"), [128, 512], F32))[:, 0:128])
                for _ in range(2)]
            pO = [Buf(st2.enter_context(
                nc.psum_tensor(unm("a0_pO"), [128, 512], F32))[0:64, 0:128])
                for _ in range(2)]
            pT = [Buf(st2.enter_context(
                nc.psum_tensor(unm("a0_pT"), [128, 1024], BF16))[:, 0:64])
                for _ in range(2)]
            stS = [Buf(pS[par].ap[0:64, 0:64]) for par in (0, 1)]
            stF = [[Buf(stf32.ap[par * 64:(par + 1) * 64, i * 64:(i + 1) * 64])
                    for i in range(8)] for par in (0, 1)]
            stB = [[Buf(stbf.ap[par * 64:(par + 1) * 64, i * 64:(i + 1) * 64])
                    for i in range(8)] for par in (0, 1)]

            for hp in range(4):
                for b in range(NB):
                    for c in range(NCH):
                        for par in (0, 1):
                            base = par * 64
                            col = hp * NT + b * T + c * CH
                            q_sl = qf.ap[base:base + 64, col:col + CH]
                            k_sl = kf.ap[base:base + 64, col:col + CH]
                            pa = pA[par]
                            pt = pT[par]
                            kt = ktmc[2 * par + (c % 2)]
                            ab = abf[2 * par + (c % 2)]
                            tok = pb.op("pe", (lambda e, pa=pa, k_sl=k_sl,
                                               q_sl=q_sl: e.matmul(
                                pa.ap, k_sl, q_sl, start=True, stop=True)),
                                after=[kf.wtok, qf.wtok] + pa.deps_w())
                            pa.wrote(tok)
                            kf.read(tok); qf.read(tok)
                            tok = pb.op("dve", (lambda e, pa=pa, ab=ab:
                                                e.scalar_tensor_tensor(
                                ab.ap, pa.ap, 1.0, trilT.ap,
                                ALU.mult, ALU.mult)),
                                after=[pa.wtok, trilT.wtok] + ab.deps_w())
                            ab.wrote(tok); pa.read(tok); trilT.read(tok)
                            idsl = ident.ap[base:base + 64, base:base + 64]
                            tok = pb.op("pe", (lambda e, pt=pt, k_sl=k_sl,
                                               idsl=idsl: e.transpose(
                                pt.ap, k_sl, idsl)),
                                after=[kf.wtok, ident.wtok] + pt.deps_w())
                            pt.wrote(tok)
                            kf.read(tok); ident.read(tok)
                            tok = pb.op("dve", (lambda e, pt=pt, kt=kt:
                                                e.tensor_copy(kt.ap, pt.ap)),
                                after=[pt.wtok] + kt.deps_w())
                            kt.wrote(tok); pt.read(tok)
                        for par in (0, 1):
                            h = hp * 2 + par
                            sidx = b * 4 + hp
                            S = stS[par]
                            SF = stF[par][sidx]
                            SB = stB[par][sidx]
                            base = par * 64
                            col = hp * NT + b * T + c * CH
                            q_sl = qf.ap[base:base + 64, col:col + CH]
                            po = pO[par]
                            kt = ktmc[2 * par + (c % 2)]
                            ab = abf[2 * par + (c % 2)]
                            tci = b * (T // CH) + c
                            v_sl = vtm.ap[:, tci * 512 + h * 64:
                                          tci * 512 + h * 64 + 64]
                            first = (c == 0)
                            if not first:
                                tok = pb.op("pe", (lambda e, po=po, SB=SB,
                                                   q_sl=q_sl: e.matmul(
                                    po.ap, SB.ap, q_sl,
                                    start=True, stop=False)),
                                    after=[SB.wtok, qf.wtok] + po.deps_w())
                                po.wrote(tok)
                                SB.read(tok); qf.read(tok)
                            tok = pb.op("pe", (lambda e, po=po, v_sl=v_sl,
                                               ab=ab, first=first: e.matmul(
                                po.ap, v_sl, ab.ap,
                                start=first, stop=True)),
                                after=[vtm.wtok, ab.wtok] +
                                      (po.deps_w() if first else []))
                            po.wrote(tok)
                            vtm.read(tok); ab.read(tok)
                            ysl = yb.ap[base:base + 64, hp * NT + b * T + c * CH:
                                        hp * NT + b * T + c * CH + CH]
                            tok = pb.op("act", (lambda e, po=po, ysl=ysl:
                                                e.copy(ysl, po.ap)),
                                after=[po.wtok] + yb.deps_w())
                            yb.rtoks.append(tok)
                            po.read(tok)
                            tok = pb.op("pe", (lambda e, S=S, kt=kt, v_sl=v_sl:
                                               e.matmul(
                                S.ap, kt.ap, v_sl, start=True, stop=True)),
                                after=[kt.wtok, vtm.wtok] + S.deps_w())
                            S.wrote(tok)
                            kt.read(tok); vtm.read(tok)
                            if c < NCH - 1:
                                if first:
                                    tok = pb.op("dve", (lambda e, S=S, SF=SF:
                                                        e.tensor_copy(SF.ap, S.ap)),
                                        after=[S.wtok] + SF.deps_w())
                                else:
                                    tok = pb.op("dve", (lambda e, S=S, SF=SF:
                                                        e.tensor_add(SF.ap, SF.ap,
                                                                     S.ap)),
                                        after=[S.wtok, SF.wtok])
                                SF.wrote(tok); S.read(tok)
                                tok = pb.op("dve", (lambda e, SF=SF, SB=SB:
                                                    e.tensor_copy(SB.ap, SF.ap)),
                                    after=[SF.wtok] + SB.deps_w())
                                SB.wrote(tok); SF.read(tok)
            ylast = yb.rtoks[-1]
            for hp in range(4):
                tok = pb.dma("sp", "s_a0yst",
                             y_ap[half * 512 + hp * 128:half * 512 + (hp + 1) * 128, :],
                             yb.ap[:, hp * NT:(hp + 1) * NT],
                             after=[ylast])
                yb.read(tok)
            pb.barrier()


def ph_attn0(ctx, xn_ap, wqkv_ap, y_ap):
    ph_attn0_half(ctx, xn_ap, wqkv_ap, y_ap, 0)
    ph_attn0_half(ctx, xn_ap, wqkv_ap, y_ap, 1)


def ph_mix1(ctx, xn_ap, maa_x_ap, maa5_ap, w1maa_ap, w2m_ap, td_ap,
            w1dec_ap, w2dec_ap, xf_aps, glog_ap):
    nc, pb, es = ctx["nc"], ctx["pb"], ctx["es"]
    FNAMES = ["r", "k", "v", "w", "g"]
    with contextlib.ExitStack() as ph:
        sb = lambda n, s, d: ph.enter_context(nc.sbuf_tensor(unm(n), s, d))
        w1p = Buf(sb("m1_w1p", [128, DC * 320], BF16)[:])
        w2m = Buf(sb("m1_w2m", [128, 5 * 1024], BF16)[:])
        w1d = Buf(sb("m1_w1d", [128, DC * 64], BF16)[:])
        w2d = Buf(sb("m1_w2d", [64, 1024], BF16)[:])
        maax = Buf(sb("m1_maax", [128, 8], F32)[:])
        maa5 = Buf(sb("m1_maa5", [128, 40], F32)[:])
        tdt = Buf(sb("m1_td", [128, 8], F32)[:])
        xn = [Buf(sb("m1_xn", [128, DC * TT], BF16)[:]) for _ in range(2)]
        xp = [Buf(sb("m1_xp", [128, DC * TT], BF16)[:]) for _ in range(2)]
        dx = [Buf(sb("m1_dx", [128, DC * TT], BF16)[:]) for _ in range(2)]
        xxx = [Buf(sb("m1_xxx", [128, DC * TT], BF16)[:]) for _ in range(2)]
        m1b = Buf(sb("m1_m1", [128, TT], BF16)[:])
        m2b = Buf(sb("m1_m2", [128, TT], BF16)[:])
        m3b = Buf(sb("m1_m3", [64, TT], BF16)[:])
        xw = Buf(sb("m1_xw", [128, DC * TT], BF16)[:])
        tdcb = Buf(sb("m1_tdc", [64, TT], BF16)[:])
        tmp = [Buf(sb("m1_tmp", [128, TT], BF16)[:]) for _ in range(4)]
        xob = [Buf(sb("m1_xo", [128, TT], BF16)[:]) for _ in range(4)]
        glt = [Buf(sb("m1_gl", [128, TT], F32)[:]) for _ in range(4)]
        psum = ph.enter_context(nc.psum_tensor(unm("m1_ps"), [128, 8 * TT], F32))
        pmx = [Buf(psum[:, i * TT:(i + 1) * TT]) for i in range(3)]
        pf = [Buf(psum[:, (3 + i) * TT:(4 + i) * TT]) for i in range(2)]
        pd = Buf(psum[0:64, 5 * TT:6 * TT])
        pd2 = [Buf(psum[:, (6 + i) * TT:(7 + i) * TT]) for i in range(2)]

        for k in range(DC):
            pb.dma("pool", f"s_m1a{k}", w1p.ap[:, k * 320:(k + 1) * 320],
                   w1maa_ap[k * 128:(k + 1) * 128, :], after=w1p.deps_w())
            pb.dma("pool", f"s_m1a{k}", w1d.ap[:, k * 64:(k + 1) * 64],
                   w1dec_ap[k * 128:(k + 1) * 128, :], after=w1d.deps_w())
        pb.dma("pool", "s_m1b", w2m.ap, w2m_ap, after=w2m.deps_w())
        pb.dma("pool", "s_m1b", w2d.ap, w2dec_ap, after=w2d.deps_w())
        w1ptok = [pb.cur(f"s_m1a{k}") for k in range(DC)]
        wtok = pb.cur("s_m1b")
        w1p.wrote(w1ptok[-1]); w1d.wrote(w1ptok[-1])
        w2m.wrote(wtok); w2d.wrote(wtok)
        vt = pb.dma("sp", "s_m1v", maax.ap, maa_x_ap)
        vt = pb.dma("sp", "s_m1v", maa5.ap, maa5_ap)
        vt = pb.dma("sp", "s_m1v", tdt.ap, td_ap)
        maax.wrote(vt); maa5.wrote(vt); tdt.wrote(vt)

        for t in range(NTT):
            ib = t % 2
            XN, XP, DX, XXX = xn[ib], xp[ib], dx[ib], xxx[ib]
            for c in range(DC):
                pb.dma("sp", f"s_m1x{t % 3}", XN.ap[:, c * TT:(c + 1) * TT],
                       xn_ap[c * 128:(c + 1) * 128, t * TT:(t + 1) * TT],
                       after=XN.deps_w())
            xt = pb.cur(f"s_m1x{t % 3}")
            XN.wrote(xt)
            zs = []
            if t % 4 == 0:
                for c in range(DC):
                    zs.append(pb.op("pool", (lambda e, c=c, XP=XP: e.memset(
                        XP.ap[:, c * TT:c * TT + 1], 0.0)), after=XP.deps_w()))
                for c in range(DC):
                    pb.dma("act", f"s_m1p{t % 3}",
                           XP.ap[:, c * TT + 1:(c + 1) * TT],
                           xn_ap[c * 128:(c + 1) * 128,
                                 t * TT:(t + 1) * TT - 1],
                           after=XP.deps_w() + zs)
            else:
                for c in range(DC):
                    pb.dma("act", f"s_m1p{t % 3}",
                           XP.ap[:, c * TT:(c + 1) * TT],
                           xn_ap[c * 128:(c + 1) * 128,
                                 t * TT - 1:(t + 1) * TT - 1],
                           after=XP.deps_w())
            pt = pb.cur(f"s_m1p{t % 3}")
            XP.wrote(pt)
            XP.rtoks.extend(zs)
            dtoks = []
            for c in range(DC):
                eng = "dve" if c % 2 == 0 else "pool"
                sl = slice(c * TT, (c + 1) * TT)
                tok = pb.op(eng, (lambda e, sl=sl, DX=DX, XP=XP, XN=XN:
                                  e.tensor_sub(DX.ap[:, sl], XP.ap[:, sl],
                                               XN.ap[:, sl])),
                            after=[XP.wtok, XN.wtok] + XP.rtoks[-DC:] + DX.deps_w())
                DX.wrote(tok) if c == DC - 1 else None
                dtoks.append(tok)
                XP.read(tok); XN.read(tok)
                tok2 = pb.op("dve", (lambda e, c=c, sl=sl, DX=DX, XN=XN, XXX=XXX:
                                   e.scalar_tensor_tensor(
                                       XXX.ap[:, sl], DX.ap[:, sl],
                                       maax.ap[:, c:c + 1], XN.ap[:, sl],
                                       ALU.mult, ALU.add)),
                             after=[dtoks[c], maax.wtok, XN.wtok] + XXX.deps_w())
                XXX.wrote(tok2) if c == DC - 1 else None
                XXX.rtoks.append(tok2)
                maax.read(tok2); XN.read(tok2)
            dxtok = dtoks[-1]
            xxtok = XXX.rtoks[-1]
            mts = []
            for sbi, (c0, c1, dst) in enumerate(
                    ((0, 128, m1b), (128, 256, m2b), (256, 320, m3b))):
                p = pmx[sbi]
                mtok = None
                for k in range(DC):
                    mtok = pb.op("pe", (lambda e, k=k, c0=c0, c1=c1, p=p,
                                        XXX=XXX: e.matmul(
                        p.ap[0:c1 - c0, :],
                        w1p.ap[:, k * 320 + c0:k * 320 + c1],
                        XXX.ap[:, k * TT:(k + 1) * TT],
                        start=(k == 0), stop=(k == DC - 1))),
                        after=[xxtok, w1ptok[k]] + (p.deps_w() if k == 0 else []))
                p.wrote(mtok)
                w1p.read(mtok)
                tok = pb.op("act", (lambda e, p=p, dst=dst, n=c1 - c0:
                                    e.activation(dst.ap, p.ap[0:n, :], AF.Tanh)),
                            after=[p.wtok] + dst.deps_w())
                dst.wrote(tok); p.read(tok)
                mts.append(tok)
            rhs_f = [(m1b, 0), (m1b, 64), (m2b, 0), (m2b, 64), (m3b, 0)]
            for f in range(5):
                src_b, base = rhs_f[f]
                for c in range(DC):
                    p = pf[c % 2]
                    mtok = pb.op("pe", (lambda e, f=f, c=c, p=p, src_b=src_b,
                                        base=base: e.matmul(
                        p.ap,
                        w2m.ap[base:base + 32,
                               f * 1024 + c * 128:f * 1024 + (c + 1) * 128],
                        src_b.ap[base:base + 32, :],
                        start=True, stop=True)),
                        after=[src_b.wtok, w2m.wtok] + p.deps_w())
                    p.wrote(mtok)
                    src_b.read(mtok); w2m.read(mtok)
                    tb = tmp[c % 4]
                    tok = pb.op("dve", (lambda e, f=f, c=c, p=p, tb=tb, DX=DX:
                                        e.scalar_tensor_tensor(
                        tb.ap, p.ap, maa5.ap[:, f * 8 + c:f * 8 + c + 1],
                        DX.ap[:, c * TT:(c + 1) * TT], ALU.add, ALU.mult)),
                        after=[p.wtok, dxtok, maa5.wtok] + tb.deps_w())
                    tb.wrote(tok); p.read(tok); DX.read(tok); maa5.read(tok)
                    if f == 3:
                        dst_ap = xw.ap[:, c * TT:(c + 1) * TT]
                        dstb = xw
                    else:
                        dstb = xob[c % 4]
                        dst_ap = dstb.ap
                    tok2 = pb.op("pool", (lambda e, c=c, tb=tb, XN=XN,
                                          dst_ap=dst_ap: e.tensor_add(
                        dst_ap, tb.ap, XN.ap[:, c * TT:(c + 1) * TT])),
                        after=[tb.wtok, XN.wtok] + dstb.deps_w())
                    dstb.wrote(tok2)
                    tb.read(tok2); XN.read(tok2)
                    if f != 3:
                        fn = FNAMES[f]
                        stok = pb.dma("sp", f"s_m1s{c % 4}",
                                      xf_aps[fn][c * 128:(c + 1) * 128,
                                                 t * TT:(t + 1) * TT],
                                      dst_ap, after=[tok2])
                        dstb.read(stok)
            mtok = None
            for k in range(DC):
                mtok = pb.op("pe", (lambda e, k=k: e.matmul(
                    pd.ap, w1d.ap[:, k * 64:(k + 1) * 64],
                    xw.ap[:, k * TT:(k + 1) * TT],
                    start=(k == 0), stop=(k == DC - 1))),
                    after=[xw.wtok, w1ptok[k]] + (pd.deps_w() if k == 0 else []))
            pd.wrote(mtok)
            w1d.read(mtok); xw.read(mtok)
            tok = pb.op("act", (lambda e: e.activation(
                tdcb.ap, pd.ap, AF.Tanh)), after=[pd.wtok] + tdcb.deps_w())
            tdcb.wrote(tok); pd.read(tok)
            for c in range(DC):
                p = pd2[c % 2]
                mtok = pb.op("pe", (lambda e, c=c, p=p: e.matmul(
                    p.ap, w2d.ap[:, c * 128:(c + 1) * 128], tdcb.ap,
                    start=True, stop=True)),
                    after=[tdcb.wtok, w2d.wtok] + p.deps_w())
                p.wrote(mtok)
                tdcb.read(mtok); w2d.read(mtok)
                gl = glt[c % 4]
                tok = pb.op("act", (lambda e, c=c, p=p, gl=gl: e.activation(
                    gl.ap, p.ap, AF.Exp, bias=tdt.ap[:, c:c + 1])),
                    after=[p.wtok, tdt.wtok] + gl.deps_w())
                gl.wrote(tok); p.read(tok); tdt.read(tok)
                stok = pb.dma("act", f"s_m1g{c % 4}",
                              glog_ap[c * 128:(c + 1) * 128, t * TT:(t + 1) * TT],
                              gl.ap, after=[tok])
                gl.read(stok)
        pb.barrier()


def ph_qkvg1(ctx, xf_aps, glog_ap, wq_ap, wk_ap, wv_ap, wg_ap, bg_ap,
             q_ap, k_ap, vtm_ap, gate_ap):
    nc, pb, es = ctx["nc"], ctx["pb"], ctx["es"]
    with contextlib.ExitStack() as ph:
        sb = lambda n, s, d: ph.enter_context(nc.sbuf_tensor(unm(n), s, d))
        ws = {}
        for nm, ap in (("q", wq_ap), ("k", wk_ap), ("v", wv_ap), ("g", wg_ap)):
            ws[nm] = Buf(sb(f"qg_w{nm}", [128, DC * 1024], BF16)[:])
            for k in range(DC):
                pb.dma("pool", f"s_qgw{nm}{k % 2}",
                       ws[nm].ap[:, k * 1024:(k + 1) * 1024],
                       ap[k * 128:(k + 1) * 128, :], after=ws[nm].deps_w())
            ws[nm].wrote(pb.cur(f"s_qgw{nm}1"))
        wtoks = {nm: [pb.cur(f"s_qgw{nm}{k % 2}") for k in range(DC)]
                 for nm in ("q", "k", "v", "g")}
        bg = Buf(sb("qg_bg", [128, 8], F32)[:])
        bg.wrote(pb.dma("sp", "s_qgb", bg.ap, bg_ap))
        xt = {nm: [Buf(sb(f"qg_x{nm}", [128, DC * TT], BF16)[:])
                   for _ in range(2)] for nm in ("r", "k", "v", "g")}
        et = [Buf(sb("qg_e", [128, DC * TT], F32)[:]) for _ in range(2)]
        wf = [Buf(sb("qg_wf", [128, TT], F32)[:]) for _ in range(2)]
        vout = [Buf(sb("qg_vo", [128, 1024], BF16)[:]) for _ in range(2)]
        outt = [Buf(sb("qg_ot", [128, TT], BF16)[:]) for _ in range(4)]
        psum = ph.enter_context(nc.psum_tensor(unm("qg_ps"), [128, 8 * TT], F32))
        ps = [Buf(psum[:, i * TT:(i + 1) * TT]) for i in range(8)]

        for t in range(NTT):
            ib = t % 2
            for nm in ("r", "k", "v", "g"):
                X = xt[nm][ib]
                for c in range(DC):
                    pb.dma("sp" if nm in ("r", "k") else "act",
                           f"s_qgx{nm}{t % 3}", X.ap[:, c * TT:(c + 1) * TT],
                           xf_aps[nm][c * 128:(c + 1) * 128,
                                      t * TT:(t + 1) * TT], after=X.deps_w())
                X.wrote(pb.cur(f"s_qgx{nm}{t % 3}"))
            E = et[ib]
            for c in range(DC):
                pb.dma("act", f"s_qge{t % 3}", E.ap[:, c * TT:(c + 1) * TT],
                       glog_ap[c * 128:(c + 1) * 128, t * TT:(t + 1) * TT],
                       after=E.deps_w())
            E.wrote(pb.cur(f"s_qge{t % 3}"))
            ko = [Buf(None) for _ in range(4)]
            for nm, src_nm, dst_ap in (("q", "r", q_ap), ("k", "k", k_ap),
                                       ("g", "g", gate_ap)):
                X = xt[src_nm][ib]
                for oc in range(DC):
                    p = ps[oc % 4]
                    mtok = None
                    for k in range(DC):
                        mtok = pb.op("pe", (lambda e, k=k, oc=oc, p=p, X=X,
                                            nm=nm: e.matmul(
                            p.ap,
                            ws[nm].ap[:, k * 1024 + oc * 128:
                                      k * 1024 + oc * 128 + 128],
                            X.ap[:, k * TT:(k + 1) * TT],
                            start=(k == 0), stop=(k == DC - 1))),
                            after=[X.wtok, wtoks[nm][k]] +
                                  (p.deps_w() if k == 0 else []))
                    p.wrote(mtok)
                    ws[nm].read(mtok); X.read(mtok)
                    ob = outt[oc % 4]
                    if nm == "q":
                        tok = pb.op("act", (lambda e, p=p, ob=ob: e.activation(
                            ob.ap, p.ap, AF.Copy, scale=0.125)),
                            after=[p.wtok] + ob.deps_w())
                        ob.wrote(tok); p.read(tok)
                    elif nm == "g":
                        tok = pb.op("act", (lambda e, p=p, ob=ob, oc=oc:
                                            e.activation(
                            ob.ap, p.ap, AF.Silu, bias=bg.ap[:, oc:oc + 1])),
                            after=[p.wtok, bg.wtok] + ob.deps_w())
                        ob.wrote(tok); p.read(tok); bg.read(tok)
                    else:
                        W = wf[oc % 2]
                        tok = pb.op("act", (lambda e, oc=oc, W=W, E=E:
                                            e.activation(
                            W.ap, E.ap[:, oc * TT:(oc + 1) * TT], AF.Exp,
                            scale=-1.0)), after=[E.wtok] + W.deps_w())
                        W.wrote(tok); E.read(tok)
                        tok = pb.op("dve", (lambda e, W=W: e.tensor_scalar(
                            W.ap, W.ap, -1.0, 1.0, ALU.mult, ALU.add)),
                            after=[W.wtok])
                        W.wrote(tok)
                        tok = pb.op("dve", (lambda e, p=p, W=W, ob=ob:
                                            e.tensor_mul(ob.ap, p.ap, W.ap)),
                            after=[p.wtok, W.wtok] + ob.deps_w())
                        ob.wrote(tok); p.read(tok); W.read(tok)
                    stok = pb.dma("sp", f"s_qgs{oc % 4}",
                                  dst_ap[oc * 128:(oc + 1) * 128,
                                         t * TT:(t + 1) * TT],
                                  ob.ap, after=[ob.wtok])
                    ob.read(stok)
            X = xt["v"][ib]
            for ts_ in range(4):
                VO = vout[ts_ % 2]
                vtoks = []
                for c2 in range(2):
                    p = ps[4 + 2 * (ts_ % 2) + c2]
                    mtok = None
                    for k in range(DC):
                        mtok = pb.op("pe", (lambda e, k=k, ts_=ts_, c2=c2, p=p,
                                            X=X: e.matmul(
                            p.ap,
                            X.ap[:, k * TT + ts_ * 128:k * TT + ts_ * 128 + 128],
                            ws["v"].ap[:, k * 1024 + c2 * 512:
                                       k * 1024 + c2 * 512 + 512],
                            start=(k == 0), stop=(k == DC - 1))),
                            after=[X.wtok, wtoks["v"][k]] +
                                  (p.deps_w() if k == 0 else []))
                    p.wrote(mtok)
                    ws["v"].read(mtok); X.read(mtok)
                    tok = pb.op("act", (lambda e, p=p, c2=c2, VO=VO: e.copy(
                        VO.ap[:, c2 * 512:(c2 + 1) * 512], p.ap)),
                        after=[p.wtok] + VO.deps_w())
                    vtoks.append(tok)
                    p.read(tok)
                VO.wrote(vtoks[-1])
                VO.rtoks = [vtoks[0]]
                r0 = t * TT + ts_ * 128
                stok = pb.dma("act", f"s_qgvs{ts_ % 2}",
                              vtm_ap[r0:r0 + 128, :], VO.ap,
                              after=vtoks)
                VO.read(stok)
        pb.barrier()


def ph_scan1(ctx, q_ap, k_ap, vtm_ap, glog_ap, o_ap):
    nc, pb, es = ctx["nc"], ctx["pb"], ctx["es"]
    trilT = ctx["trilT_f32"]
    ident = ctx["ident_bf"]
    CH = 128
    NCH = T // CH
    with contextlib.ExitStack() as ph:
        sb = lambda n, s, d: ph.enter_context(nc.sbuf_tensor(unm(n), s, d))
        qs = [Buf(sb("sc_q", [128, T], BF16)[:]) for _ in range(4)]
        ks = [Buf(sb("sc_k", [128, T], BF16)[:]) for _ in range(4)]
        es_ = [Buf(sb("sc_e", [128, T], F32)[:]) for _ in range(4)]
        vs = [Buf(sb("sc_v", [128, T], BF16)[:]) for _ in range(4)]
        ob = [Buf(sb("sc_o", [128, T], BF16)[:]) for _ in range(4)]
        zer = Buf(sb("sc_zero", [128, CH], F32)[:])
        NSD = 6
        cums = [[Buf(sb("sc_cum", [128, CH], F32)[:]) for _ in range(NSD)]
                for _ in range(2)]
        f1bs = [[Buf(sb("sc_f1", [128, 1], F32)[:]) for _ in range(NSD)]
                for _ in range(2)]
        aabs = [[Buf(sb("sc_a", [128, CH], F32)[:]) for _ in range(NSD)]
                for _ in range(2)]
        ambs = [[Buf(sb("sc_am", [128, CH], F32)[:]) for _ in range(NSD)]
                for _ in range(2)]
        qts = [[Buf(sb("sc_qt", [128, CH], BF16)[:]) for _ in range(NSD)]
               for _ in range(2)]
        kts = [[Buf(sb("sc_kt", [128, CH], BF16)[:]) for _ in range(NSD)]
               for _ in range(2)]
        ktm = [[Buf(sb("sc_ktm", [128, 64], BF16)[:]) for _ in range(4)]
               for _ in range(2)]
        abf = [[Buf(sb("sc_abf", [128, CH], BF16)[:]) for _ in range(4)]
               for _ in range(2)]
        sfb = Buf(sb("sc_sf", [128, 16 * 64], F32)[:])
        sbb = Buf(sb("sc_sb", [128, 16 * 64], BF16)[:])
        stf = [[Buf(sfb.ap[par * 64:(par + 1) * 64, i * 64:(i + 1) * 64])
                for i in range(16)] for par in (0, 1)]
        stb = [[Buf(sbb.ap[par * 64:(par + 1) * 64, i * 64:(i + 1) * 64])
                for i in range(16)] for par in (0, 1)]
        psA_t = ph.enter_context(nc.psum_tensor(unm("sc_pA"), [128, 512], F32))
        psO_t = ph.enter_context(nc.psum_tensor(unm("sc_pO"), [128, 512], F32))
        psS_t = ph.enter_context(nc.psum_tensor(unm("sc_pS"), [128, 512], F32))
        psT_t = ph.enter_context(
            nc.psum_tensor(unm("sc_pT"), [128, 1024], BF16))
        pA = [[Buf(psA_t[:, (m * 2 + par) * CH:(m * 2 + par + 1) * CH])
               for par in (0, 1)] for m in range(2)]
        pO = [[Buf(psO_t[0:64, (m * 2 + par) * CH:(m * 2 + par + 1) * CH])
               for par in (0, 1)] for m in range(2)]
        pS = [[Buf(psS_t[0:64, (m * 2 + par) * 64:(m * 2 + par + 1) * 64])
               for par in (0, 1)] for m in range(2)]
        pT = [[Buf(psT_t[:, (m * 2 + par) * 64:(m * 2 + par + 1) * 64])
               for par in (0, 1)] for m in range(2)]
        ztok = pb.op("pool", (lambda e: e.memset(zer.ap, 0.0)))
        zer.wrote(ztok)

        def issue_loads(s):
            hp, b = s // NB, s % NB
            ib = s % 4
            Q, K, E, V = qs[ib], ks[ib], es_[ib], vs[ib]
            r0, c0 = hp * 128, b * T
            for (Bf, ap_, strm, qn) in ((Q, q_ap, "q", "sp"),
                                        (K, k_ap, "k", "act"),
                                        (E, glog_ap, "e", "act")):
                pb.dma(qn, f"s_sc{strm}{ib}", Bf.ap,
                       ap_[r0:r0 + 128, c0:c0 + T], after=Bf.deps_w())
                Bf.wrote(pb.cur(f"s_sc{strm}{ib}"))
            PSZ = min(512, T)
            for part in range(T // PSZ):
                pb.dma("sp", f"s_scv{ib}",
                       V.ap[:, part * PSZ:(part + 1) * PSZ].rearrange(
                           "p (a n) -> p a n", n=128),
                       vtm_ap[c0 + part * PSZ:c0 + (part + 1) * PSZ,
                              r0:r0 + 128].rearrange(
                           "(a p) n -> p a n", p=128),
                       after=V.deps_w())
            V.wrote(pb.cur(f"s_scv{ib}"))

        def emit_front(s, m, c):
            ib = s % 4
            Q, K, E = qs[ib], ks[ib], es_[ib]
            cum, f1b = cums[m][c % NSD], f1bs[m][c % NSD]
            aab, amb = aabs[m][c % NSD], ambs[m][c % NSD]
            qt, kt_ = qts[m][c % NSD], kts[m][c % NSD]
            csl = slice(c * CH, (c + 1) * CH)
            tok = pb.op("dve", (lambda e: e.tensor_tensor_scan(
                cum.ap, E.ap[:, csl], zer.ap,
                0.0, ALU.add, ALU.add)),
                after=[E.wtok, zer.wtok] + cum.deps_w())
            cum.wrote(tok); E.read(tok); zer.read(tok)
            tok = pb.op("dve", (lambda e: e.tensor_scalar_mul(
                f1b.ap, cum.ap[:, CH - 1:CH], 0.5)),
                after=[cum.wtok] + f1b.deps_w())
            f1b.wrote(tok); cum.read(tok)
            tok = pb.op("dve", (lambda e: e.tensor_scalar(
                cum.ap, cum.ap, f1b.ap[:, 0:1], None, ALU.subtract)),
                after=[cum.wtok, f1b.wtok])
            cum.wrote(tok); f1b.read(tok)
            tok = pb.op("dve", (lambda e: e.tensor_scalar(
                cum.ap, cum.ap, 40.0, -40.0, ALU.min, ALU.max)),
                after=[cum.wtok])
            cum.wrote(tok)
            tok = pb.op("act", (lambda e: e.activation(
                aab.ap, cum.ap, AF.Exp, scale=-1.0)),
                after=[cum.wtok] + aab.deps_w())
            aab.wrote(tok); cum.read(tok)
            tok = pb.op("act", (lambda e: e.activation(
                amb.ap, cum.ap, AF.Exp)),
                after=[cum.wtok] + amb.deps_w())
            amb.wrote(tok); cum.read(tok)
            tok = pb.op("act", (lambda e: e.activation(
                f1b.ap, f1b.ap, AF.Exp, scale=-1.0)),
                after=[f1b.wtok])
            f1b.wrote(tok)
            tok = pb.op("pool", (lambda e: e.tensor_mul(
                qt.ap, Q.ap[:, csl], aab.ap)),
                after=[Q.wtok, aab.wtok] + qt.deps_w())
            qt.wrote(tok); Q.read(tok); aab.read(tok)
            tok = pb.op("pool", (lambda e: e.tensor_mul(
                kt_.ap, K.ap[:, csl], amb.ap)),
                after=[K.wtok, amb.wtok] + kt_.deps_w())
            kt_.wrote(tok); K.read(tok); amb.read(tok)

        def emit_back(s, m, c):
            hp, b = s // NB, s % NB
            ib = s % 4
            V, O = vs[ib], ob[ib]
            f1b = f1bs[m][c % NSD]
            qt, kt_ = qts[m][c % NSD], kts[m][c % NSD]
            csl = slice(c * CH, (c + 1) * CH)
            for par in (0, 1):
                base = par * 64
                sidx = b * 8 + hp
                SF = stf[par][sidx]
                SBc = stb[par][sidx]
                pa, po = pA[m][par], pO[m][par]
                pS_, pt = pS[m][par], pT[m][par]
                km = ktm[m][2 * par + (c % 2)]
                ab = abf[m][2 * par + (c % 2)]
                first = (c == 0)
                if not first:
                    tok = pb.op("dve", (lambda e, SF=SF, SBc=SBc,
                                        base=base, f1b=f1b:
                                        e.tensor_scalar(
                        SBc.ap, SF.ap, f1b.ap[base:base + 64, 0:1],
                        None, ALU.mult)),
                        after=[SF.wtok, f1b.wtok] + SBc.deps_w())
                    SBc.wrote(tok); SF.read(tok); f1b.read(tok)
                tok = pb.op("pe", (lambda e, pa=pa, base=base,
                                   kt_=kt_, qt=qt: e.matmul(
                    pa.ap, kt_.ap[base:base + 64, :],
                    qt.ap[base:base + 64, :], start=True, stop=True)),
                    after=[kt_.wtok, qt.wtok] + pa.deps_w())
                pa.wrote(tok); kt_.read(tok); qt.read(tok)
                tok = pb.op("dve", (lambda e, pa=pa, ab=ab:
                                    e.scalar_tensor_tensor(
                    ab.ap, pa.ap, 1.0, trilT.ap, ALU.mult, ALU.mult)),
                    after=[pa.wtok, trilT.wtok] + ab.deps_w())
                ab.wrote(tok); pa.read(tok); trilT.read(tok)
                tok = pb.op("pe", (lambda e, pt=pt, base=base, kt_=kt_:
                                   e.transpose(
                    pt.ap, kt_.ap[base:base + 64, :],
                    ident.ap[base:base + 64, base:base + 64])),
                    after=[kt_.wtok, ident.wtok] + pt.deps_w())
                pt.wrote(tok); kt_.read(tok); ident.read(tok)
                tok = pb.op("dve", (lambda e, pt=pt, km=km:
                                    e.tensor_copy(km.ap, pt.ap)),
                    after=[pt.wtok] + km.deps_w())
                km.wrote(tok); pt.read(tok)
                v_sl = V.ap[:, c * 128 + par * 64:c * 128 + par * 64 + 64]
                if not first:
                    tok = pb.op("pe", (lambda e, po=po, SBc=SBc,
                                       base=base, qt=qt: e.matmul(
                        po.ap, SBc.ap, qt.ap[base:base + 64, :],
                        start=True, stop=False)),
                        after=[SBc.wtok, qt.wtok] + po.deps_w())
                    po.wrote(tok); SBc.read(tok); qt.read(tok)
                tok = pb.op("pe", (lambda e, po=po, v_sl=v_sl, ab=ab,
                                   first=first: e.matmul(
                    po.ap, v_sl, ab.ap, start=first, stop=True)),
                    after=[V.wtok, ab.wtok] +
                          (po.deps_w() if first else []))
                po.wrote(tok); V.read(tok); ab.read(tok)
                osl = O.ap[base:base + 64, csl]
                tok = pb.op("act", (lambda e, po=po, osl=osl: e.copy(
                    osl, po.ap)), after=[po.wtok] + O.deps_w())
                O.rtoks.append(tok)
                po.read(tok)
                tok = pb.op("pe", (lambda e, pS_=pS_, km=km, v_sl=v_sl:
                                   e.matmul(
                    pS_.ap, km.ap, v_sl, start=True, stop=True)),
                    after=[km.wtok, V.wtok] + pS_.deps_w())
                pS_.wrote(tok); km.read(tok); V.read(tok)
                if c < NCH - 1:
                    if first:
                        tok = pb.op("dve", (lambda e, pS_=pS_, SF=SF,
                                            base=base, f1b=f1b:
                                            e.tensor_scalar(
                            SF.ap, pS_.ap, f1b.ap[base:base + 64, 0:1],
                            None, ALU.mult)),
                            after=[pS_.wtok, f1b.wtok] + SF.deps_w())
                        SF.wrote(tok); pS_.read(tok); f1b.read(tok)
                    else:
                        tok = pb.op("dve", (lambda e, SF=SF, base=base,
                                            f1b=f1b: e.tensor_scalar(
                            SF.ap, SF.ap, f1b.ap[base:base + 64, 0:1],
                            None, ALU.mult)),
                            after=[SF.wtok, f1b.wtok])
                        SF.wrote(tok); f1b.read(tok)
                        tok = pb.op("dve", (lambda e, SF=SF, pS_=pS_:
                                            e.tensor_add(
                            SF.ap, SF.ap, pS_.ap)),
                            after=[SF.wtok, pS_.wtok])
                        SF.wrote(tok); pS_.read(tok)
                        tok = pb.op("dve", (lambda e, SF=SF, base=base,
                                            f1b=f1b: e.tensor_scalar(
                            SF.ap, SF.ap, f1b.ap[base:base + 64, 0:1],
                            None, ALU.mult)),
                            after=[SF.wtok, f1b.wtok])
                        SF.wrote(tok); f1b.read(tok)

        def store_o(s):
            hp, b = s // NB, s % NB
            ib = s % 4
            O = ob[ib]
            r0, c0 = hp * 128, b * T
            olast = O.rtoks[-1]
            stok = pb.dma("sp", f"s_scos{ib}", o_ap[r0:r0 + 128, c0:c0 + T],
                          O.ap, after=[olast])
            O.read(stok)

        issue_loads(0)
        issue_loads(1)
        for p in range(8):
            if p < 7:
                issue_loads(2 * p + 2)
                issue_loads(2 * p + 3)
            AHEAD = 2
            for a in range(AHEAD):
                for m in (0, 1):
                    emit_front(2 * p + m, m, a)
            for c in range(NCH):
                for m in (0, 1):
                    if c + AHEAD < NCH:
                        emit_front(2 * p + m, m, c + AHEAD)
                    emit_back(2 * p + m, m, c)
            store_o(2 * p)
            store_o(2 * p + 1)
        pb.barrier()



def ph_out1(ctx, o_ap, gate_ap, wo_ap, x_ap, z_ap, out_ap):
    nc, pb, es = ctx["nc"], ctx["pb"], ctx["es"]
    with contextlib.ExitStack() as ph:
        sb = lambda n, s, d: ph.enter_context(nc.sbuf_tensor(unm(n), s, d))
        wo = Buf(sb("o1_wo", [128, DC * 1024], BF16)[:])
        for k in range(DC):
            pb.dma("pool", f"s_o1w{k % 4}", wo.ap[:, k * 1024:(k + 1) * 1024],
                   wo_ap[k * 128:(k + 1) * 128, :], after=wo.deps_w())
        wtoks = [pb.cur(f"s_o1w{k % 4}") for k in range(DC)]
        ot = [Buf(sb("o1_o", [128, DC * TT], BF16)[:]) for _ in range(3)]
        gt = [Buf(sb("o1_g", [128, DC * TT], BF16)[:]) for _ in range(3)]
        og = [Buf(sb("o1_og", [128, DC * TT], BF16)[:]) for _ in range(2)]
        at = [Buf(sb("o1_a", [128, DC * TT], F32)[:]) for _ in range(2)]
        rb = [Buf(sb("o1_r", [128, DC * TT], F32)[:]) for _ in range(2)]
        zt2 = [Buf(sb("o1_z", [128, DC * TT], F32)[:]) for _ in range(2)] \
            if z_ap is not None else None
        psum = ph.enter_context(nc.psum_tensor(unm("o1_ps"), [128, 8 * TT], F32))
        ps = [Buf(psum[:, i * TT:(i + 1) * TT]) for i in range(8)]
        for t in range(NTT):
            ib = t % 2
            OT, GT = ot[t % 3], gt[t % 3]
            OG, AT, RB = og[ib], at[ib], rb[ib]
            for c in range(DC):
                pb.dma("sp", f"s_o1o{t % 3}", OT.ap[:, c * TT:(c + 1) * TT],
                       o_ap[c * 128:(c + 1) * 128, t * TT:(t + 1) * TT],
                       after=OT.deps_w())
            OT.wrote(pb.cur(f"s_o1o{t % 3}"))
            for c in range(DC):
                pb.dma("act", f"s_o1g{t % 3}", GT.ap[:, c * TT:(c + 1) * TT],
                       gate_ap[c * 128:(c + 1) * 128, t * TT:(t + 1) * TT],
                       after=GT.deps_w())
            GT.wrote(pb.cur(f"s_o1g{t % 3}"))
            for c in range(DC):
                pb.dma("act", f"s_o1x{t % 3}", AT.ap[:, c * TT:(c + 1) * TT],
                       x_ap[c * 128:(c + 1) * 128, t * TT:(t + 1) * TT],
                       after=AT.deps_w())
            AT.wrote(pb.cur(f"s_o1x{t % 3}"))
            if zt2 is not None:
                ZT = zt2[ib]
                for c in range(DC):
                    pb.dma("sp", f"s_o1z{t % 3}", ZT.ap[:, c * TT:(c + 1) * TT],
                           z_ap[c * 128:(c + 1) * 128, t * TT:(t + 1) * TT],
                           after=ZT.deps_w())
                ZT.wrote(pb.cur(f"s_o1z{t % 3}"))
            ogt = []
            for c in range(DC):
                eng = "dve" if c % 2 == 0 else "pool"
                sl = slice(c * TT, (c + 1) * TT)
                tok = pb.op(eng, (lambda e, sl=sl, OT=OT, GT=GT, OG=OG:
                                  e.tensor_mul(OG.ap[:, sl], OT.ap[:, sl],
                                               GT.ap[:, sl])),
                            after=[OT.wtok, GT.wtok] + OG.deps_w())
                ogt.append(tok)
                OT.read(tok); GT.read(tok)
            OG.wrote(ogt[-1])
            OG.rtoks = list(ogt[:-1])
            for oc in range(DC):
                p = ps[oc]
                mtok = None
                for k in range(DC):
                    mtok = pb.op("pe", (lambda e, k=k, oc=oc, p=p, OG=OG:
                                        e.matmul(
                        p.ap, wo.ap[:, k * 1024 + oc * 128:
                                    k * 1024 + oc * 128 + 128],
                        OG.ap[:, k * TT:(k + 1) * TT],
                        start=(k == 0), stop=(k == DC - 1))),
                        after=[ogt[k], wtoks[k]] + (p.deps_w() if k == 0 else []))
                p.wrote(mtok)
                wo.read(mtok); OG.read(mtok)
                tok = pb.op("dve", (lambda e, oc=oc, p=p, RB=RB, AT=AT:
                                    e.tensor_add(RB.ap[:, oc * TT:(oc + 1) * TT],
                                                 p.ap,
                                                 AT.ap[:, oc * TT:(oc + 1) * TT])),
                            after=[p.wtok, AT.wtok] + RB.deps_w())
                RB.wrote(tok); p.read(tok); AT.read(tok)
                if zt2 is not None:
                    ZT = zt2[ib]
                    tok = pb.op("pool", (lambda e, oc=oc, RB=RB, ZT=ZT:
                                         e.tensor_add(
                        RB.ap[:, oc * TT:(oc + 1) * TT],
                        RB.ap[:, oc * TT:(oc + 1) * TT],
                        ZT.ap[:, oc * TT:(oc + 1) * TT])),
                        after=[RB.wtok, ZT.wtok])
                    RB.wrote(tok); ZT.read(tok)
                stok = pb.dma("sp", f"s_o1s{t % 3}",
                              out_ap[oc * 128:(oc + 1) * 128, t * TT:(t + 1) * TT],
                              RB.ap[:, oc * TT:(oc + 1) * TT], after=[RB.wtok])
                RB.read(stok)
        pb.barrier()


def ph_wo0(ctx, y_ap, wo_ap, acc_ap, out_ap):
    nc, pb, es = ctx["nc"], ctx["pb"], ctx["es"]
    with contextlib.ExitStack() as ph:
        sb = lambda n, s, d: ph.enter_context(nc.sbuf_tensor(unm(n), s, d))
        wo = Buf(sb("w0_wo", [128, DC * 1024], BF16)[:])
        yt = [Buf(sb("w0_y", [128, DC * TT], BF16)[:]) for _ in range(3)]
        at = [Buf(sb("w0_a", [128, DC * TT], F32)[:]) for _ in range(3)]
        ob = [Buf(sb("w0_o", [128, DC * TT], F32)[:]) for _ in range(3)]
        psum = ph.enter_context(nc.psum_tensor(unm("w0_ps"), [128, 8 * TT], F32))
        ps = [Buf(psum[:, i * TT:(i + 1) * TT]) for i in range(8)]
        for k in range(DC):
            pb.dma("pool", f"s_wo{k % 4}",
                   wo.ap[:, k * 1024:(k + 1) * 1024],
                   wo_ap[k * 128:(k + 1) * 128, :], after=wo.deps_w())
        wtoks = [pb.cur(f"s_wo{k % 4}") for k in range(DC)]
        for t in range(NTT):
            YT, AT, OB = yt[t % 3], at[t % 3], ob[t % 3]
            ytoks = []
            for c in range(DC):
                tok = pb.dma("sp", f"s_woy{t % 3}",
                             YT.ap[:, c * TT:(c + 1) * TT],
                             y_ap[c * 128:(c + 1) * 128, t * TT:(t + 1) * TT],
                             after=YT.deps_w())
                ytoks.append(tok)
            YT.wrote(ytoks[-1])
            atoks = []
            for c in range(DC):
                tok = pb.dma("act", f"s_woa{t % 3}",
                             AT.ap[:, c * TT:(c + 1) * TT],
                             acc_ap[c * 128:(c + 1) * 128, t * TT:(t + 1) * TT],
                             after=AT.deps_w())
                atoks.append(tok)
            AT.wrote(atoks[-1])
            for oc in range(DC):
                p = ps[oc]
                mtok = None
                for k in range(DC):
                    mtok = pb.op("pe", (lambda e, k=k, oc=oc, p=p, YT=YT: e.matmul(
                        p.ap, wo.ap[:, k * 1024 + oc * 128:k * 1024 + oc * 128 + 128],
                        YT.ap[:, k * TT:(k + 1) * TT],
                        start=(k == 0), stop=(k == DC - 1))),
                        after=[YT.wtok, wtoks[k]] + (p.deps_w() if k == 0 else []))
                p.wrote(mtok)
                wo.read(mtok); YT.read(mtok)
                tok = pb.op("dve", (lambda e, oc=oc, p=p, OB=OB, AT=AT:
                                    e.tensor_add(OB.ap[:, oc * TT:(oc + 1) * TT],
                                                 p.ap,
                                                 AT.ap[:, oc * TT:(oc + 1) * TT])),
                            after=[p.wtok, AT.wtok] + OB.deps_w())
                OB.wrote(tok); p.read(tok); AT.read(tok)
                stok = pb.dma("act", f"s_wost{t % 3}",
                              out_ap[oc * 128:(oc + 1) * 128, t * TT:(t + 1) * TT],
                              OB.ap[:, oc * TT:(oc + 1) * TT], after=[OB.wtok])
                OB.read(stok)
        pb.barrier()


def ph_l1f_attn(ctx, z_ap, wqkv_ap, qs_ap, ks_ap, o_ap, lidx):
    nc, pb, es = ctx["nc"], ctx["pb"], ctx["es"]
    ident = ctx["ident_bf"]
    bdiag = ctx["bdiag_bf"]
    Pmat = ctx["rotP_bf"]
    QCS, QSS, KCS, KSS = (ctx["rot_qcs"], ctx["rot_qss"],
                          ctx["rot_kcs"], ctx["rot_kss"])
    MT = ctx["maddt"]
    trilF = ctx["trilT_f32"]
    VW = 65
    with contextlib.ExitStack() as ph:
        sb = lambda n, s, d: ph.enter_context(nc.sbuf_tensor(unm(n), s, d))
        wqkv = Buf(sb("lf_wqkv", [128, DC * 1536], BF16)[:])
        for k in range(DC):
            pb.dma("pool", f"s_lfw{k % 4}", wqkv.ap[:, k * 1536:(k + 1) * 1536],
                   wqkv_ap[k * 128:(k + 1) * 128, :], after=wqkv.deps_w())
        wtoks = [pb.cur(f"s_lfw{k % 4}") for k in range(DC)]
        wqkv.wrote(wtoks[-1])
        qs8 = Buf(sb("lf_qs8", [128, 4], F32)[:])
        pb.dma("sp", "s_lfqs", qs8.ap[:, 0:2], qs_ap, after=qs8.deps_w())
        pb.dma("sp", "s_lfqs", qs8.ap[:, 2:4], ks_ap, after=qs8.deps_w())
        qs8.wrote(pb.cur("s_lfqs"))
        tok = pb.op("dve", (lambda e: e.tensor_scalar_mul(
            qs8.ap[:, 0:2], qs8.ap[:, 0:2], 8.0)), after=[qs8.wtok])
        qs8.wrote(tok)
        qf = Buf(sb("lf_qf", [128, 4 * NT], BF16)[:])
        kf = Buf(sb("lf_kf", [128, 4 * NT], BF16)[:])
        vtm = Buf(sb("lf_vtm", [128, (NT // 128) * VW * 8], BF16)[:])
        vtm3 = vtm.ap.rearrange("p (t h e) -> p t h e", h=8, e=VW)
        otoks = []
        for h8 in range(8):
            otoks.append(pb.op("pool", (lambda e, h8=h8: e.memset(
                vtm3[:, :, h8, 64:65], 1.0)), after=vtm.deps_w()))
        vtm.rtoks.extend(otoks)

        with contextlib.ExitStack() as st1:
            sb1 = lambda n, s, d: st1.enter_context(nc.sbuf_tensor(unm(n), s, d))
            zt = [Buf(sb1("lf_z", [128, DC * TT], F32)[:]) for _ in range(1)]
            hb = [Buf(sb1("lf_h", [128, DC * TT], BF16)[:]) for _ in range(2)]
            NSC = 3
            raws = [Buf(sb1("lf_raw", [128, TT], BF16)[:]) for _ in range(NSC)]
            sqs = [Buf(sb1("lf_sq", [128, TT], BF16)[:]) for _ in range(NSC)]
            facs = [Buf(sb1("lf_fac", [128, TT], F32)[:]) for _ in range(NSC)]
            qns = [Buf(sb1("lf_qn", [128, TT], BF16)[:]) for _ in range(NSC)]
            t1bs = [Buf(sb1("lf_t1", [128, TT], F32)[:]) for _ in range(NSC)]
            t2bs = [Buf(sb1("lf_t2", [128, TT], F32)[:]) for _ in range(NSC)]
            psum = st1.enter_context(
                nc.psum_tensor(unm("lf_ps"), [128, 8 * TT], F32))
            ps = [Buf(psum[:, i * TT:(i + 1) * TT]) for i in range(8)]
            git = [0]
            pend1 = [None]
            pend2 = [None]

            def emit_s1(st):
                raw, sq, p, pn = st["raw"], st["sq"], st["p"], st["pn"]
                fac = st["fac"]
                tok = pb.op("act", (lambda e: e.copy(
                    raw.ap, p.ap)), after=[p.wtok] + raw.deps_w())
                raw.wrote(tok); p.read(tok)
                tok = pb.op("pool", (lambda e: e.tensor_mul(
                    sq.ap, raw.ap, raw.ap)),
                    after=[raw.wtok] + sq.deps_w())
                sq.wrote(tok); raw.read(tok)
                tok = pb.op("pe", (lambda e: e.matmul(
                    pn.ap, bdiag.ap, sq.ap, start=True, stop=True)),
                    after=[sq.wtok, bdiag.wtok] + pn.deps_w())
                pn.wrote(tok); sq.read(tok); bdiag.read(tok)
                tok = pb.op("dve", (lambda e: e.tensor_scalar_max(
                    fac.ap, pn.ap, 1e-24)),
                    after=[pn.wtok] + fac.deps_w())
                fac.wrote(tok); pn.read(tok)

            def emit_s2(st):
                raw, fac, qn = st["raw"], st["fac"], st["qn"]
                t1b, t2b, pr = st["t1b"], st["t2b"], st["pr"]
                which, dstb, t_, cp = st["which"], st["dstb"], st["t"], st["cp"]
                tok = pb.op("act", (lambda e: e.activation(
                    fac.ap, fac.ap, AF.Ln)), after=[fac.wtok])
                fac.wrote(tok)
                tok = pb.op("act", (lambda e: e.activation(
                    fac.ap, fac.ap, AF.Exp, scale=-0.5)),
                    after=[fac.wtok])
                fac.wrote(tok)
                sc_col = lidx if which == "q" else 2 + lidx
                tok = pb.op("dve", (lambda e, sc_col=sc_col:
                                    e.scalar_tensor_tensor(
                    qn.ap, raw.ap, qs8.ap[:, sc_col:sc_col + 1],
                    fac.ap, ALU.mult, ALU.mult)),
                    after=[raw.wtok, fac.wtok, qs8.wtok] + qn.deps_w())
                qn.wrote(tok); raw.read(tok); fac.read(tok)
                qs8.read(tok)
                tok = pb.op("pe", (lambda e: e.matmul(
                    pr.ap, Pmat.ap, qn.ap, start=True, stop=True)),
                    after=[qn.wtok, Pmat.wtok] + pr.deps_w())
                pr.wrote(tok); qn.read(tok); Pmat.read(tok)
                CS = QCS if which == "q" else KCS
                SS = QSS if which == "q" else KSS
                tb_sl = slice((t_ % 4) * TT, (t_ % 4 + 1) * TT)
                tok = pb.op("dve", (lambda e, SS=SS, tb_sl=tb_sl:
                                    e.tensor_mul(
                    t1b.ap, pr.ap, SS.ap[:, tb_sl])),
                    after=[pr.wtok, SS.wtok] + t1b.deps_w())
                t1b.wrote(tok); pr.read(tok); SS.read(tok)
                tok = pb.op("pool", (lambda e, CS=CS, tb_sl=tb_sl:
                                     e.tensor_mul(
                    t2b.ap, qn.ap, CS.ap[:, tb_sl])),
                    after=[qn.wtok, CS.wtok] + t2b.deps_w())
                t2b.wrote(tok); qn.read(tok); CS.read(tok)
                sl = slice(cp * NT + t_ * TT, cp * NT + (t_ + 1) * TT)
                tok = pb.op("dve", (lambda e, sl=sl: e.tensor_add(
                    dstb.ap[:, sl], t1b.ap, t2b.ap)),
                    after=[t1b.wtok, t2b.wtok] + dstb.deps_w())
                dstb.rtoks.append(tok)
                t1b.read(tok); t2b.read(tok)

            for t in range(NTT):
                ib = t % 2
                Z, HB = zt[0], hb[ib]
                for c in range(4):
                    pb.dma("sp", f"s_lfz{t % 3}", Z.ap[:, c * TT:(c + 1) * TT],
                           z_ap[c * 128:(c + 1) * 128, t * TT:(t + 1) * TT],
                           after=Z.deps_w())
                zs = []
                if t % 4 == 0:
                    for c in range(4, DC):
                        zs.append(pb.op("pool", (lambda e, c=c, Z=Z: e.memset(
                            Z.ap[:, c * TT:c * TT + 1], 0.0)),
                            after=Z.deps_w()))
                    for c in range(4, DC):
                        pb.dma("sp", f"s_lfz{t % 3}",
                               Z.ap[:, c * TT + 1:(c + 1) * TT],
                               z_ap[c * 128:(c + 1) * 128,
                                    t * TT:(t + 1) * TT - 1],
                               after=Z.deps_w() + zs)
                else:
                    for c in range(4, DC):
                        pb.dma("sp", f"s_lfz{t % 3}",
                               Z.ap[:, c * TT:(c + 1) * TT],
                               z_ap[c * 128:(c + 1) * 128,
                                    t * TT - 1:(t + 1) * TT - 1],
                               after=Z.deps_w())
                Z.wrote(pb.cur(f"s_lfz{t % 3}"))
                Z.rtoks.extend(zs)
                htoks = []
                for c in range(DC):
                    eng = "dve" if c % 2 == 0 else "pool"
                    tok = pb.op(eng, (lambda e, c=c, Z=Z, HB=HB: e.tensor_copy(
                        HB.ap[:, c * TT:(c + 1) * TT],
                        Z.ap[:, c * TT:(c + 1) * TT])),
                        after=[Z.wtok] + HB.deps_w())
                    htoks.append(tok)
                    Z.read(tok)
                HB.wrote(htoks[-1])
                HB.rtoks = htoks[:-1]
                for ts_ in range(4):
                    p = ps[4 + ts_ % 2]
                    mtok = None
                    for k in range(DC):
                        mtok = pb.op("pe", (lambda e, k=k, ts_=ts_, p=p, HB=HB:
                                            e.matmul(
                            p.ap,
                            HB.ap[:, k * TT + ts_ * 128:k * TT + ts_ * 128 + 128],
                            wqkv.ap[:, k * 1536 + 1024:k * 1536 + 1536],
                            start=(k == 0), stop=(k == DC - 1))),
                            after=[htoks[k], wtoks[k]] +
                                  (p.deps_w() if k == 0 else []))
                    p.wrote(mtok)
                    wqkv.read(mtok); HB.read(mtok)
                    tc = t * 4 + ts_
                    tok = pb.op("act", (lambda e, p=p, tc=tc: e.copy(
                        vtm3[:, tc, :, 0:64],
                        p.ap.rearrange("p (h e) -> p h e", e=64))),
                        after=[p.wtok] + vtm.deps_w())
                    vtm.rtoks.append(tok)
                    p.read(tok)
                for wi, (which, dstb, col0) in enumerate(
                        (("q", qf, 0), ("k", kf, 512))):
                    for cp in range(4):
                        it = git[0]
                        git[0] += 1
                        st = {
                            "raw": raws[it % NSC], "sq": sqs[it % NSC],
                            "fac": facs[it % NSC], "qn": qns[it % NSC],
                            "t1b": t1bs[it % NSC], "t2b": t2bs[it % NSC],
                            "p": ps[cp], "pn": ps[6 + it % 2],
                            "pr": ps[4 + it % 2], "which": which,
                            "dstb": dstb, "t": t, "cp": cp,
                        }
                        p = st["p"]
                        mtok = None
                        for k in range(DC):
                            mtok = pb.op("pe", (lambda e, k=k, cp=cp, p=p,
                                                HB=HB, col0=col0: e.matmul(
                                p.ap,
                                wqkv.ap[:, k * 1536 + col0 + cp * 128:
                                        k * 1536 + col0 + cp * 128 + 128],
                                HB.ap[:, k * TT:(k + 1) * TT],
                                start=(k == 0), stop=(k == DC - 1))),
                                after=[htoks[k], wtoks[k]] +
                                      (p.deps_w() if k == 0 else []))
                        p.wrote(mtok)
                        wqkv.read(mtok); HB.read(mtok)
                        if pend1[0] is not None:
                            emit_s1(pend1[0])
                        if pend2[0] is not None:
                            emit_s2(pend2[0])
                        pend2[0] = pend1[0]
                        pend1[0] = st
            if pend1[0] is not None:
                emit_s1(pend1[0])
            if pend2[0] is not None:
                emit_s2(pend2[0])
            if pend1[0] is not None:
                emit_s2(pend1[0])
            pend1[0] = None
            pend2[0] = None
            for bb in (qf, kf, vtm):
                bb.wrote(bb.rtoks[-1])
                bb.rtoks = []
            pb.barrier()

        with contextlib.ExitStack() as st2:
            sb2 = lambda n, s, d: st2.enter_context(nc.sbuf_tensor(unm(n), s, d))
            EQb = [Buf(sb2("lf_eq", [128, 1024], BF16)[:]) for _ in range(4)]
            NRb = [Buf(sb2("lf_nr", [1, 512], F32)[:]) for _ in range(2)]
            OSb = [Buf(sb2("lf_os", [64, 512], BF16)[:]) for _ in range(2)]
            FINb = [Buf(sb2("lf_fin", [64, 512], BF16)[:]) for _ in range(2)]
            cbits = Buf(sb2("lf_cb", [128, 4], F32)[:])
            negCb = Buf(cbits.ap[:, 0:1])
            ctmp = Buf(cbits.ap[:, 1:2])
            crow = Buf(sb2("lf_cr", [1, 128], F32)[:])
            pSc = [Buf(st2.enter_context(
                nc.psum_tensor(unm("lf_pSc"), [128, 1024], F32))[:])
                for _ in range(2)]
            pOv = [Buf(st2.enter_context(
                nc.psum_tensor(unm("lf_pO"), [128, 512], F32))[0:VW, :])
                for _ in range(2)]
            pB = [Buf(st2.enter_context(
                nc.psum_tensor(unm("lf_pB"), [128, 512], F32))[:])
                for _ in range(2)]
            tok = pb.op("dve", (lambda e: e.tensor_mul(
                ctmp.ap, qs8.ap[:, lidx:lidx + 1],
                qs8.ap[:, 2 + lidx:3 + lidx])),
                after=[qs8.wtok] + cbits.deps_w())
            ctmp.wrote(tok); qs8.read(tok)
            tok = pb.op("act", (lambda e: e.activation(
                ctmp.ap, ctmp.ap, AF.Abs)), after=[ctmp.wtok])
            ctmp.wrote(tok)
            pb0 = pB[0]
            tok = pb.op("pe", (lambda e, pb0=pb0: e.transpose(
                pb0.ap[0:1, 0:128], ctmp.ap, ctx["ident_f32"].ap)),
                after=[ctmp.wtok, ctx["ident_f32"].wtok] + pb0.deps_w())
            pb0.wrote(tok); ctmp.read(tok); ctx["ident_f32"].read(tok)
            tok = pb.op("act", (lambda e, pb0=pb0: e.copy(
                crow.ap, pb0.ap[0:1, 0:128])),
                after=[pb0.wtok] + crow.deps_w())
            crow.wrote(tok); pb0.read(tok)
            tok = pb.op("dve", (lambda e: e.tensor_reduce(
                crow.ap[0:1, 0:1], crow.ap, AX.X, ALU.max, negate=True)),
                after=[crow.wtok])
            crow.wrote(tok)
            tok = pb.op("pe", (lambda e, pb0=pb0: e.matmul(
                pb0.ap[:, 0:1], trilF.ap[0:1, 0:128], crow.ap[0:1, 0:1],
                start=True, stop=True)),
                after=[crow.wtok, trilF.wtok] + pb0.deps_w())
            pb0.wrote(tok); crow.read(tok); trilF.read(tok)
            tok = pb.op("act", (lambda e, pb0=pb0: e.copy(
                negCb.ap, pb0.ap[:, 0:1])), after=[pb0.wtok] + negCb.deps_w())
            negCb.wrote(tok); pb0.read(tok)
            NRbf = [Buf(sb2("lf_nrb", [1, 512], BF16)[:]) for _ in range(2)]
            onesb = ctx["ones_bf"]
            ii_g = 0
            pq_g = 0
            pend_av = None
            pend_epi = None

            def emit_avs(EQ_, pr2, pov, st8):
                for j, ss in enumerate(pr2):
                    tci = (st8["b"] * (T // 128)) + ((st8["w"] - 1) * 4 + ss)
                    v_sl = vtm.ap[:, tci * 8 * VW + st8["h"] * VW:
                                  tci * 8 * VW + st8["h"] * VW + VW]
                    esl = EQ_.ap[:, j * 512:(j + 1) * 512]
                    avi = st8["avi"]
                    nav = st8["nav"]
                    tk = pb.op("pe", (lambda e, v_sl=v_sl, esl=esl,
                                      avi=avi, nav=nav, pov=pov: e.matmul(
                        pov.ap, v_sl, esl,
                        start=(avi == 0), stop=(avi == nav - 1))),
                        after=[EQ_.wtok, vtm.wtok] +
                              (pov.deps_w() if avi == 0 else []))
                    pov.wrote(tk); EQ_.read(tk); vtm.read(tk)
                    st8["avi"] += 1

            def mk_epilogue(pov, NR_, NB_, OS_, FIN_, pbc, b, h, w):
                def epi_a():
                    tk = pb.op("act", (lambda e: e.copy(
                        NR_.ap, pov.ap[64:65, :])),
                        after=[pov.wtok] + NR_.deps_w())
                    NR_.wrote(tk); pov.read(tk)
                    tk = pb.op("act", (lambda e: e.activation(
                        NR_.ap, NR_.ap, AF.Ln)), after=[NR_.wtok])
                    NR_.wrote(tk)
                    tk = pb.op("act", (lambda e: e.activation(
                        NB_.ap, NR_.ap, AF.Exp, scale=-1.0)),
                        after=[NR_.wtok] + NB_.deps_w())
                    NB_.wrote(tk); NR_.read(tk)
                    tk = pb.op("act", (lambda e: e.copy(
                        OS_.ap, pov.ap[0:64, :])),
                        after=[pov.wtok] + OS_.deps_w())
                    OS_.wrote(tk); pov.read(tk)

                def epi_b():
                    tk = pb.op("pe", (lambda e: e.matmul(
                        pbc.ap[0:64, :], onesb.ap[0:1, 0:64], NB_.ap,
                        start=True, stop=True)),
                        after=[NB_.wtok, onesb.wtok] + pbc.deps_w())
                    pbc.wrote(tk); NB_.read(tk); onesb.read(tk)
                    tk = pb.op("dve", (lambda e: e.tensor_mul(
                        FIN_.ap, OS_.ap, pbc.ap[0:64, :])),
                        after=[OS_.wtok, pbc.wtok] + FIN_.deps_w())
                    FIN_.wrote(tk); OS_.read(tk); pbc.read(tk)
                    stok = pb.dma("sp", f"s_lfo{ii_g % 2}",
                                  o_ap[h * 64:(h + 1) * 64,
                                       b * T + w * 512:b * T + (w + 1) * 512],
                                  FIN_.ap, after=[FIN_.wtok])
                    FIN_.read(stok)
                return epi_a, epi_b

            for b in range(NB):
                for h in range(F_HEADS):
                    cp, par = h // 2, h % 2
                    base = par * 64
                    for w in range(F_NW):
                        ip = ii_g % 2
                        first = (w == 0)
                        kcol0 = cp * NT + b * T + (w - 1) * 512
                        qcol0 = cp * NT + b * T + w * 512
                        pov = pOv[ip]
                        q_sl = qf.ap[base:base + 64, qcol0:qcol0 + 512]
                        sslist = list(range(4 if first else 0, 8))
                        pairs = [(sslist[2 * i], sslist[2 * i + 1])
                                 for i in range(len(sslist) // 2)]
                        st8 = {"b": b, "h": h, "w": w, "avi": 0,
                               "nav": len(sslist)}
                        av_jobs = []
                        for pi, pr2 in enumerate(pairs):
                            psc = pSc[pq_g % 2]
                            EQ_ = EQb[pq_g % 4]
                            pq_g += 1
                            for j, ss in enumerate(pr2):
                                k_sl = kf.ap[base:base + 64,
                                             kcol0 + ss * 128:
                                             kcol0 + (ss + 1) * 128]
                                pv = psc.ap[:, j * 512:(j + 1) * 512]
                                tk = pb.op("pe", (lambda e, pv=pv, k_sl=k_sl,
                                                  q_sl=q_sl: e.matmul(
                                    pv, k_sl, q_sl, start=True, stop=False)),
                                    after=[kf.wtok, qf.wtok] +
                                          (psc.deps_w() if j == 0 else []))
                                psc.wrote(tk); kf.read(tk); qf.read(tk)
                                mtsl = MT.ap[:, ss * 512:(ss + 1) * 512]
                                tk = pb.op("pe", (lambda e, pv=pv, mtsl=mtsl:
                                                  e.matmul(
                                    pv, ident.ap, mtsl,
                                    start=False, stop=True)),
                                    after=[ident.wtok, MT.wtok])
                                psc.wrote(tk)
                                ident.read(tk); MT.read(tk)
                            tk = pb.op("act", (lambda e, psc=psc, EQ_=EQ_:
                                               e.activation(
                                EQ_.ap, psc.ap, AF.Exp,
                                bias=negCb.ap)),
                                after=[psc.wtok, negCb.wtok] + EQ_.deps_w())
                            EQ_.wrote(tk); psc.read(tk); negCb.read(tk)
                            av_jobs.append((EQ_, pr2))
                            if pi == 0:
                                if pend_av is not None:
                                    emit_avs(*pend_av)
                                    pend_av = None
                                if pend_epi is not None:
                                    pend_epi[0]()
                            if pi == 1 and pend_epi is not None:
                                pend_epi[1]()
                                pend_epi = None
                            if pi >= 1:
                                EQp, prp = av_jobs[pi - 1]
                                emit_avs(EQp, prp, pov, st8)
                        pend_av = (av_jobs[-1][0], av_jobs[-1][1], pov, st8)
                        pend_epi = mk_epilogue(pov, NRb[ip], NRbf[ip], OSb[ip],
                                               FINb[ip], pB[ip], b, h, w)
                        ii_g += 1
            if pend_av is not None:
                emit_avs(*pend_av)
            if pend_epi is not None:
                pend_epi[0]()
                pend_epi[1]()
            pb.barrier()


def ph_l1f_wo(ctx, o_ap, wo_ap, z_ap):
    nc, pb, es = ctx["nc"], ctx["pb"], ctx["es"]
    KC = 4
    with contextlib.ExitStack() as ph:
        sb = lambda n, s, d: ph.enter_context(nc.sbuf_tensor(unm(n), s, d))
        wo = Buf(sb("lw_wo", [128, KC * 1024], BF16)[:])
        for k in range(KC):
            pb.dma("pool", f"s_lww{k}", wo.ap[:, k * 1024:(k + 1) * 1024],
                   wo_ap[k * 128:(k + 1) * 128, :], after=wo.deps_w())
        wtoks = [pb.cur(f"s_lww{k}") for k in range(KC)]
        yt = [Buf(sb("lw_y", [128, KC * TT], BF16)[:]) for _ in range(3)]
        at = [Buf(sb("lw_a", [128, DC * TT], F32)[:]) for _ in range(3)]
        ob = [Buf(sb("lw_o", [128, DC * TT], F32)[:]) for _ in range(3)]
        psum = ph.enter_context(nc.psum_tensor(unm("lw_ps"), [128, 8 * TT], F32))
        ps = [Buf(psum[:, i * TT:(i + 1) * TT]) for i in range(8)]
        for t in range(NTT):
            ib = t % 3
            YT, AT, OB = yt[ib], at[ib], ob[ib]
            for c in range(KC):
                pb.dma("sp", f"s_lwy{t % 3}", YT.ap[:, c * TT:(c + 1) * TT],
                       o_ap[c * 128:(c + 1) * 128, t * TT:(t + 1) * TT],
                       after=YT.deps_w())
            YT.wrote(pb.cur(f"s_lwy{t % 3}"))
            for c in range(DC):
                pb.dma("act", f"s_lwa{t % 3}", AT.ap[:, c * TT:(c + 1) * TT],
                       z_ap[c * 128:(c + 1) * 128, t * TT:(t + 1) * TT],
                       after=AT.deps_w())
            AT.wrote(pb.cur(f"s_lwa{t % 3}"))
            for oc in range(DC):
                p = ps[oc]
                mtok = None
                for k in range(KC):
                    mtok = pb.op("pe", (lambda e, k=k, oc=oc, p=p, YT=YT:
                                        e.matmul(
                        p.ap, wo.ap[:, k * 1024 + oc * 128:
                                    k * 1024 + oc * 128 + 128],
                        YT.ap[:, k * TT:(k + 1) * TT],
                        start=(k == 0), stop=(k == KC - 1))),
                        after=[YT.wtok, wtoks[k]] + (p.deps_w() if k == 0 else []))
                p.wrote(mtok)
                wo.read(mtok); YT.read(mtok)
                tok = pb.op("dve", (lambda e, oc=oc, p=p, OB=OB, AT=AT:
                                    e.tensor_add(OB.ap[:, oc * TT:(oc + 1) * TT],
                                                 p.ap,
                                                 AT.ap[:, oc * TT:(oc + 1) * TT])),
                            after=[p.wtok, AT.wtok] + OB.deps_w())
                OB.wrote(tok); p.read(tok); AT.read(tok)
                stok = pb.dma("act", f"s_lws{t % 3}",
                              z_ap[oc * 128:(oc + 1) * 128, t * TT:(t + 1) * TT],
                              OB.ap[:, oc * TT:(oc + 1) * TT], after=[OB.wtok])
                OB.read(stok)
        pb.barrier()


def ph_l1f_ffn_half(ctx, z_ap, w1_ap, b1a_ap, b1g_ap, w2_ap, b2_ap,
                    lng_ap, lnb_ap, lidx, half, xnf_ap=None):
    nc, pb, es = ctx["nc"], ctx["pb"], ctx["es"]
    ones_r = ctx["ones_f32r"]
    NP = 11
    p0 = half * 11
    with contextlib.ExitStack() as ph:
        sb = lambda n, s, d: ph.enter_context(nc.sbuf_tensor(unm(n), s, d))
        w1 = Buf(sb("ff_w1", [128, DC * 2816], BF16)[:])
        w2 = Buf(sb("ff_w2", [128, NP * 1024], BF16)[:])
        for k in range(DC):
            pb.dma("pool", f"s_ffw1{k}", w1.ap[:, k * 2816:(k + 1) * 2816],
                   w1_ap[k * 128:(k + 1) * 128, p0 * 256:p0 * 256 + 2816],
                   after=w1.deps_w())
        w1toks = [pb.cur(f"s_ffw1{k}") for k in range(DC)]
        for k in range(NP):
            pb.dma("pool", f"s_ffw2{k % 8}", w2.ap[:, k * 1024:(k + 1) * 1024],
                   w2_ap[(p0 + k) * 128:(p0 + k + 1) * 128, :],
                   after=w2.deps_w())
        w2toks = [pb.cur(f"s_ffw2{k % 8}") for k in range(NP)]
        vecs = Buf(sb("ff_vecs", [128, 136], F32)[:])
        b1a = Buf(vecs.ap[:, 0:44]); b1g = Buf(vecs.ap[:, 44:88])
        b2v = Buf(vecs.ap[:, 88:104]); lng = Buf(vecs.ap[:, 104:120])
        lnb = Buf(vecs.ap[:, 120:136])
        pb.dma("sp", "s_ffv", b1a.ap, b1a_ap, after=vecs.deps_w())
        pb.dma("sp", "s_ffv", b1g.ap, b1g_ap)
        pb.dma("sp", "s_ffv", b2v.ap, b2_ap)
        pb.dma("sp", "s_ffv", lng.ap, lng_ap)
        pb.dma("sp", "s_ffv", lnb.ap, lnb_ap)
        vt = pb.cur("s_ffv")
        for bb in (b1a, b1g, b2v, lng, lnb, vecs):
            bb.wrote(vt)

        zt = [Buf(sb("ff_z", [128, DC * TT], F32R)[:]) for _ in range(2)]
        zr = [Buf(sb("ff_zr", [128, DC * TT], F32)[:]) for _ in range(2)]
        xnf = [Buf(sb("ff_xn", [128, DC * TT], BF16)[:]) for _ in range(2)]
        hbt = [Buf(sb("ff_h", [128, NP * TT], BF16)[:]) for _ in range(1)]
        gl = Buf(sb("ff_gl", [128, TT], BF16)[:])
        M = Buf(sb("ff_m", [128, TT], F32)[:])
        V = Buf(sb("ff_v", [128, TT], F32)[:])
        RS = Buf(sb("ff_rs", [128, TT], F32)[:])
        SC = [Buf(sb("ff_sc", [128, TT], F32)[:]) for _ in range(2)]
        SQs = [Buf(sb("ff_sq", [128, TT], F32R)[:]) for _ in range(2)]
        OB = [Buf(sb("ff_o", [128, TT], F32)[:]) for _ in range(3)]
        psum = ph.enter_context(nc.psum_tensor(unm("ff_ps"), [128, 8 * TT], F32))
        ps = [Buf(psum[:, i * TT:(i + 1) * TT]) for i in range(8)]

        def emit_ln(t):
            ib = t % 2
            Z, ZR, XN = zt[ib], zr[ib], xnf[ib]
            for c in range(DC):
                pb.dma("act", f"s_ffzr{t % 3}", ZR.ap[:, c * TT:(c + 1) * TT],
                       z_ap[c * 128:(c + 1) * 128, t * TT:(t + 1) * TT],
                       after=ZR.deps_w())
            ZR.wrote(pb.cur(f"s_ffzr{t % 3}"))
            if half == 1:
                for c in range(DC):
                    pb.dma("sp", f"s_ffz{t % 3}", XN.ap[:, c * TT:(c + 1) * TT],
                           xnf_ap[c * 128:(c + 1) * 128, t * TT:(t + 1) * TT],
                           after=XN.deps_w())
                XN.wrote(pb.cur(f"s_ffz{t % 3}"))
                xtoks = [XN.wtok] * DC
            else:
                zs = []
                for c in range(4):
                    pb.dma("sp", f"s_ffz{t % 3}", Z.ap[:, c * TT:(c + 1) * TT],
                           z_ap[c * 128:(c + 1) * 128,
                                t * TT:(t + 1) * TT].bitcast(F32R),
                           after=Z.deps_w())
                if t % 4 == 0:
                    for c in range(4, DC):
                        zs.append(pb.op("pool", (lambda e, c=c, Z=Z: e.memset(
                            Z.ap.bitcast(F32)[:, c * TT:c * TT + 1], 0.0)),
                            after=Z.deps_w()))
                    for c in range(4, DC):
                        pb.dma("sp", f"s_ffz{t % 3}",
                               Z.ap[:, c * TT + 1:(c + 1) * TT],
                               z_ap[c * 128:(c + 1) * 128,
                                    t * TT:(t + 1) * TT - 1].bitcast(F32R),
                               after=Z.deps_w() + zs)
                else:
                    for c in range(4, DC):
                        pb.dma("sp", f"s_ffz{t % 3}",
                               Z.ap[:, c * TT:(c + 1) * TT],
                               z_ap[c * 128:(c + 1) * 128,
                                    t * TT - 1:(t + 1) * TT - 1].bitcast(F32R),
                               after=Z.deps_w())
                Z.wrote(pb.cur(f"s_ffz{t % 3}"))
                Z.rtoks.extend(zs)
                p1, p2 = ps[4], ps[5]
                mtok = None
                for c in range(DC):
                    mtok = pb.op("pe", (lambda e, c=c, Z=Z, p1=p1: e.matmul(
                        p1.ap, ones_r.ap, Z.ap[:, c * TT:(c + 1) * TT],
                        start=(c == 0), stop=(c == DC - 1))),
                        after=[Z.wtok, ones_r.wtok] +
                              (p1.deps_w() if c == 0 else []))
                p1.wrote(mtok)
                Z.read(mtok)
                mtok = None
                for c in range(DC):
                    SQ = SQs[c % 2]
                    stok = pb.op("act", (lambda e, c=c, Z=Z, SQ=SQ: e.activation(
                        SQ.ap, Z.ap.bitcast(F32)[:, c * TT:(c + 1) * TT],
                        AF.Square)), after=[Z.wtok] + SQ.deps_w())
                    SQ.wrote(stok)
                    mtok = pb.op("pe", (lambda e, c=c, p2=p2, SQ=SQ: e.matmul(
                        p2.ap, ones_r.ap, SQ.ap,
                        start=(c == 0), stop=(c == DC - 1))),
                        after=[stok, ones_r.wtok] + (p2.deps_w() if c == 0 else []))
                    SQ.read(mtok)
                p2.wrote(mtok)
                tok = pb.op("dve", (lambda e, p1=p1: e.tensor_scalar_mul(
                    M.ap, p1.ap, 1.0 / D)), after=[p1.wtok] + M.deps_w())
                M.wrote(tok); p1.read(tok)
                tok = pb.op("dve", (lambda e: e.tensor_mul(V.ap, M.ap, M.ap)),
                            after=[M.wtok] + V.deps_w())
                V.wrote(tok); M.read(tok)
                tok = pb.op("dve", (lambda e, p2=p2: e.scalar_tensor_tensor(
                    V.ap, p2.ap, 1.0 / D, V.ap, ALU.mult, ALU.subtract)),
                    after=[p2.wtok, V.wtok])
                V.wrote(tok); p2.read(tok)
                cp, ci = ctx["cpool"], ctx["cidx"][EPS]
                tok = pb.op("act", (lambda e, cp=cp, ci=ci: e.activation(
                    RS.ap, V.ap, AF.Ln, bias=cp.ap[:, ci:ci + 1])),
                    after=[V.wtok, cp.wtok] + RS.deps_w())
                RS.wrote(tok); V.read(tok); cp.read(tok)
                tok = pb.op("act", (lambda e: e.activation(
                    RS.ap, RS.ap, AF.Exp, scale=-0.5)), after=[RS.wtok])
                RS.wrote(tok)
                xtoks = []
                for c in range(DC):
                    ei = c % 2
                    Sc = SC[ei]
                    eng = "dve" if ei == 0 else "pool"
                    sl = slice(c * TT, (c + 1) * TT)
                    tok = pb.op(eng, (lambda e, sl=sl, Sc=Sc, Z=Z: e.tensor_sub(
                        Sc.ap, Z.ap.bitcast(F32)[:, sl], M.ap)),
                        after=[Z.wtok, M.wtok] + Sc.deps_w())
                    Sc.wrote(tok); M.read(tok); Z.read(tok)
                    tok = pb.op(eng, (lambda e, Sc=Sc: e.tensor_mul(
                        Sc.ap, Sc.ap, RS.ap)), after=[Sc.wtok, RS.wtok])
                    Sc.wrote(tok); RS.read(tok)
                    tok = pb.op("dve", (lambda e, c=c, sl=sl, Sc=Sc, XN=XN:
                                        e.tensor_scalar(
                        XN.ap[:, sl], Sc.ap,
                        lng.ap[:, c * 2 + lidx:c * 2 + lidx + 1],
                        lnb.ap[:, c * 2 + lidx:c * 2 + lidx + 1],
                        ALU.mult, ALU.add)),
                        after=[Sc.wtok, lng.wtok, lnb.wtok] + XN.deps_w())
                    xtoks.append(tok)
                    Sc.read(tok); lng.read(tok); lnb.read(tok)
                XN.wrote(xtoks[-1])
                XN.rtoks = xtoks[:-1]
                for c in range(DC):
                    pb.dma("sp", f"s_ffxs{t % 3}",
                           xnf_ap[c * 128:(c + 1) * 128, t * TT:(t + 1) * TT],
                           XN.ap[:, c * TT:(c + 1) * TT], after=[xtoks[c]])
                XN.rtoks.append(pb.cur(f"s_ffxs{t % 3}"))
            if half == 1:
                return xtoks
            return xtoks

        def emit_mm(t, xtoks):
            ib = t % 2
            XN, HB = xnf[ib], hbt[0]
            htoks = []
            for j in range(NP):
                pa, pg = ps[(j % 2) * 2], ps[(j % 2) * 2 + 1]
                mtok = None
                for k in range(DC):
                    mtok = pb.op("pe", (lambda e, k=k, j=j, pa=pa, XN=XN:
                                        e.matmul(
                        pa.ap, w1.ap[:, k * 2816 + j * 256:k * 2816 + j * 256 + 128],
                        XN.ap[:, k * TT:(k + 1) * TT],
                        start=(k == 0), stop=(k == DC - 1))),
                        after=[xtoks[k], w1toks[k]] + (pa.deps_w() if k == 0 else []))
                pa.wrote(mtok)
                w1.read(mtok); XN.read(mtok)
                mtok = None
                for k in range(DC):
                    mtok = pb.op("pe", (lambda e, k=k, j=j, pg=pg, XN=XN:
                                        e.matmul(
                        pg.ap, w1.ap[:, k * 2816 + j * 256 + 128:
                                     k * 2816 + j * 256 + 256],
                        XN.ap[:, k * TT:(k + 1) * TT],
                        start=(k == 0), stop=(k == DC - 1))),
                        after=[xtoks[k], w1toks[k]] + (pg.deps_w() if k == 0 else []))
                pg.wrote(mtok)
                w1.read(mtok); XN.read(mtok)
                jj = p0 + j
                tok = pb.op("act", (lambda e, pg=pg, jj=jj: e.activation(
                    gl.ap, pg.ap, AF.Gelu,
                    bias=b1g.ap[:, jj * 2 + lidx:jj * 2 + lidx + 1])),
                    after=[pg.wtok, b1g.wtok] + gl.deps_w())
                gl.wrote(tok); pg.read(tok); b1g.read(tok)
                tok = pb.op("dve", (lambda e, pa=pa, j=j, jj=jj, HB=HB:
                                    e.scalar_tensor_tensor(
                    HB.ap[:, j * TT:(j + 1) * TT], pa.ap,
                    b1a.ap[:, jj * 2 + lidx:jj * 2 + lidx + 1],
                    gl.ap, ALU.add, ALU.mult)),
                    after=[pa.wtok, gl.wtok, b1a.wtok] + HB.deps_w())
                htoks.append(tok)
                pa.read(tok); gl.read(tok); b1a.read(tok)
            HB.wrote(htoks[-1])
            HB.rtoks = htoks[:-1]
            return htoks

        def emit_w2(t, htoks):
            ib = t % 2
            ZR, HB = zr[ib], hbt[0]
            for oc in range(DC):
                OBt = OB[oc % 3]
                p = ps[6 + oc % 2]
                mtok = None
                for k in range(NP):
                    mtok = pb.op("pe", (lambda e, k=k, oc=oc, p=p, HB=HB:
                                        e.matmul(
                        p.ap, w2.ap[:, k * 1024 + oc * 128:
                                    k * 1024 + oc * 128 + 128],
                        HB.ap[:, k * TT:(k + 1) * TT],
                        start=(k == 0), stop=(k == NP - 1))),
                        after=[htoks[k], w2toks[k]] + (p.deps_w() if k == 0 else []))
                p.wrote(mtok)
                w2.read(mtok); HB.read(mtok)
                if half == 0:
                    tok = pb.op("dve", (lambda e, oc=oc, p=p, OBt=OBt, ZR=ZR:
                                        e.scalar_tensor_tensor(
                        OBt.ap, p.ap,
                        b2v.ap[:, oc * 2 + lidx:oc * 2 + lidx + 1],
                        ZR.ap[:, oc * TT:(oc + 1) * TT], ALU.add, ALU.add)),
                        after=[p.wtok, ZR.wtok, b2v.wtok] + OBt.deps_w())
                    b2v.read(tok)
                else:
                    tok = pb.op("dve", (lambda e, oc=oc, p=p, OBt=OBt, ZR=ZR:
                                        e.tensor_add(
                        OBt.ap, p.ap,
                        ZR.ap[:, oc * TT:(oc + 1) * TT])),
                        after=[p.wtok, ZR.wtok] + OBt.deps_w())
                OBt.wrote(tok); p.read(tok); ZR.read(tok)
                stok = pb.dma("act", f"s_ffs{t % 3}",
                              z_ap[oc * 128:(oc + 1) * 128, t * TT:(t + 1) * TT],
                              OBt.ap, after=[OBt.wtok])
                OBt.read(stok)

        xt_pend = emit_ln(0)
        for t in range(NTT):
            ht = emit_mm(t, xt_pend)
            xt_pend = emit_ln(t + 1) if t + 1 < NTT else None
            emit_w2(t, ht)
        pb.barrier()


def ph_copy(ctx, a_ap, out_ap, dtype_bytes=4):
    nc, pb, es = ctx["nc"], ctx["pb"], ctx["es"]
    with contextlib.ExitStack() as ph:
        sb = lambda n, s, d: ph.enter_context(nc.sbuf_tensor(unm(n), s, d))
        at = [Buf(sb("cp_a", [128, DC * TT], F32)[:]) for _ in range(2)]
        for t in range(NTT):
            A_ = at[t % 2]
            for c in range(DC):
                pb.dma("sp", f"s_cpa{t % 3}", A_.ap[:, c * TT:(c + 1) * TT],
                       a_ap[c * 128:(c + 1) * 128, t * TT:(t + 1) * TT],
                       after=A_.deps_w())
            A_.wrote(pb.cur(f"s_cpa{t % 3}"))
            for c in range(DC):
                pb.dma("act", f"s_cps{t % 3}",
                       out_ap[c * 128:(c + 1) * 128, t * TT:(t + 1) * TT],
                       A_.ap[:, c * TT:(c + 1) * TT], after=[A_.wtok])
            A_.rtoks = [pb.cur(f"s_cps{t % 3}")]
        pb.barrier()


def ph_fin(ctx, a_ap, b_ap, out_ap):
    nc, pb, es = ctx["nc"], ctx["pb"], ctx["es"]
    with contextlib.ExitStack() as ph:
        sb = lambda n, s, d: ph.enter_context(nc.sbuf_tensor(unm(n), s, d))
        at = [Buf(sb("fin_a", [128, DC * TT], F32)[:]) for _ in range(2)]
        bt = [Buf(sb("fin_b", [128, DC * TT], F32)[:]) for _ in range(2)]
        ot = [Buf(sb("fin_o", [128, DC * TT], F32)[:]) for _ in range(2)]
        for t in range(NTT):
            ib = t % 2
            A_, B_, O_ = at[ib], bt[ib], ot[ib]
            for c in range(DC):
                pb.dma("sp", f"s_fina{t % 3}", A_.ap[:, c * TT:(c + 1) * TT],
                       a_ap[c * 128:(c + 1) * 128, t * TT:(t + 1) * TT],
                       after=A_.deps_w())
            A_.wrote(pb.cur(f"s_fina{t % 3}"))
            for c in range(DC):
                pb.dma("act", f"s_finb{t % 3}", B_.ap[:, c * TT:(c + 1) * TT],
                       b_ap[c * 128:(c + 1) * 128, t * TT:(t + 1) * TT],
                       after=B_.deps_w())
            B_.wrote(pb.cur(f"s_finb{t % 3}"))
            for c in range(DC):
                eng = "dve" if c % 2 == 0 else "pool"
                sl = slice(c * TT, (c + 1) * TT)
                tok = pb.op(eng, (lambda e, sl=sl, A_=A_, B_=B_, O_=O_:
                                  e.tensor_add(O_.ap[:, sl], A_.ap[:, sl],
                                               B_.ap[:, sl])),
                            after=[A_.wtok, B_.wtok] + O_.deps_w())
                O_.rtoks.append(tok)
                A_.read(tok); B_.read(tok)
                stok = pb.dma("sp", f"s_fins{t % 3}",
                              out_ap[c * 128:(c + 1) * 128, t * TT:(t + 1) * TT],
                              O_.ap[:, sl], after=[tok])
                O_.read(stok)
            O_.wrote(O_.rtoks[-1] if O_.rtoks else None)
        pb.barrier()


def _mk_consts(ctx):
    nc, pb, es = ctx["nc"], ctx["pb"], ctx["es"]
    ones = np.ones((128, 128), dtype=np.float32)
    h = nc.inline_tensor(ones, "ones_f32")
    tile = es.enter_context(nc.sbuf_tensor("ones_r_sb", [128, 128], F32R))
    tok = pb.dma("sp", "s_const", tile[:], h.ap().bitcast(F32R))
    b = Buf(tile[:])
    b.wrote(tok)
    ctx["ones_f32r"] = b
    vals = [EPS]
    cp = np.tile(np.asarray(vals, dtype=np.float32)[None, :], (128, 1))
    hc = nc.inline_tensor(cp, "cpool")
    ct = es.enter_context(nc.sbuf_tensor("cpool_sb", [128, len(vals)], F32))
    ctok = pb.dma("sp", "s_const", ct[:], hc.ap())
    cb = Buf(ct[:])
    cb.wrote(ctok)
    ctx["cpool"] = cb
    ctx["cidx"] = {v: i for i, v in enumerate(vals)}

    def add_const(name, arr, dtype):
        hh = nc.inline_tensor(arr, name)
        tt = es.enter_context(nc.sbuf_tensor(name + "_sb", list(arr.shape), dtype))
        tk = pb.dma("sp", "s_const", tt[:], hh.ap())
        bb = Buf(tt[:])
        bb.wrote(tk)
        ctx[name] = bb
    add_const("ident_bf", np.eye(128, dtype=ml_dtypes.bfloat16), BF16)
    bd = np.zeros((128, 128), dtype=ml_dtypes.bfloat16)
    bd[0:64, 0:64] = 1
    bd[64:128, 64:128] = 1
    add_const("bdiag_bf", bd, BF16)
    add_const("trilT_f32", _tril_maskT(), F32)
    add_const("ident_f32", np.eye(128, dtype=np.float32), F32)
    add_const("ones_bf", np.ones((128, 128), dtype=ml_dtypes.bfloat16), BF16)
    add_const("rotP_bf", _rot_perm(), BF16)
    qcs, qss, kcs, kss = _rotary_tables()
    add_const("rot_qcs", qcs, BF16)
    add_const("rot_qss", qss, BF16)
    add_const("rot_kcs", kcs, BF16)
    add_const("rot_kss", kss, BF16)
    mq, mt = _locattn_masks()
    add_const("maddt", mt, BF16)
    ctx["_consts"] = ["ones_f32r", "cpool", "ident_bf", "bdiag_bf", "trilT_f32",
                      "ident_f32", "ones_bf", "rotP_bf", "rot_qcs", "rot_qss",
                      "rot_kcs", "rot_kss", "maddt"]
    fin = pb.cur("s_const")
    for nm in ctx["_consts"]:
        ctx[nm].wtok = fin


def build_program(phases, io_spec):
    nc = bass.Bass("TRN2", target_bir_lowering=False)
    es = contextlib.ExitStack()
    es.__enter__()
    dram = {}
    for name, (shape, dtype, kind) in io_spec.items():
        dram[name] = nc.dram_tensor(name, list(shape), dtype, kind=kind).ap()
    pb = PB(nc, es)
    ctx = {"nc": nc, "pb": pb, "es": es, "dram": dram}
    _mk_consts(ctx)
    for fn, kw in phases:
        fn(ctx, **{k: dram[v] if isinstance(v, str) else v for k, v in kw.items()})
    pb.emit()
    return nc, es, dram


def prep_core_x(x, core):
    xs = x[NB * core:NB * core + NB]
    return np.ascontiguousarray(xs.transpose(2, 0, 1).reshape(D, NT))


def unprep_out(outs):
    full = np.empty((B, T, D), dtype=np.float32)
    for c, o in enumerate(outs):
        full[NB * c:NB * c + NB] = np.asarray(o).reshape(D, NB, T).transpose(1, 2, 0)
    return full


def build_full(fused=True):
    D32, D16 = F32, BF16
    FM32 = ([1024, NT], F32)
    FM16 = ([1024, NT], BF16)
    inter_specs = {
        "xn0": FM16, "acc0": FM32, "y0": FM16, "x1": FM32,
        "xn1": FM16, "z": FM32, "glogE": FM32,
        "xf_r": FM16, "xf_k": FM16, "xf_v": FM16, "xf_g": FM16,
        "q1": FM16, "k1": FM16, "gate1": FM16, "o1": FM16,
        "vtm1": ([NT, 1024], BF16), "acc1": FM32,
        "o1f": ([512, NT], BF16), "xnf": FM16,
    }
    in_specs = {
        "x": FM32,
        "wqkv0": ([1024, 3072], F32), "wo0": ([1024, 1024], F32),
        "w10": ([1024, 4096], F32), "b1t0": ([128, 32], F32),
        "w20": ([4096, 1024], F32), "b2t0": ([128, 8], F32),
        "maa_x": ([128, 8], F32), "maa5": ([128, 40], F32),
        "w1maa": ([1024, 320], F32), "w2m": ([128, 5120], F32),
        "td": ([128, 8], F32), "w1dec": ([1024, 64], F32),
        "w2dec": ([64, 1024], F32),
        "wq1": ([1024, 1024], F32), "wk1": ([1024, 1024], F32),
        "wv1": ([1024, 1024], F32), "wg1": ([1024, 1024], F32),
        "wo1": ([1024, 1024], F32), "bgt": ([128, 8], F32),
        "fqkv0": ([1024, 1536], F32), "fqkv1": ([1024, 1536], F32),
        "fqs": ([128, 2], F32), "fks": ([128, 2], F32),
        "fwo0": ([512, 1024], F32), "fwo1": ([512, 1024], F32),
        "fw10": ([1024, 2 * F_HIDP], F32), "fw11": ([1024, 2 * F_HIDP], F32),
        "fb1a": ([128, 44], F32), "fb1g": ([128, 44], F32),
        "fw20": ([F_HIDP, 1024], F32), "fw21": ([F_HIDP, 1024], F32),
        "fb2": ([128, 16], F32), "flng": ([128, 16], F32),
        "flnb": ([128, 16], F32),
    }
    out_specs = {"out": FM32}

    def phase_list(ctx, d):
        xf = {"r": d["xf_r"], "k": d["xf_k"], "v": d["xf_v"], "g": d["xf_g"]}
        return [
            lambda: ph_layernorm(ctx, d["x"], d["xn0"]),
            lambda: ph_ffn0(ctx, d["xn0"], d["x"], d["w10"], d["b1t0"],
                            d["w20"], d["b2t0"], d["acc0"]),
            lambda: ph_attn0(ctx, d["xn0"], d["wqkv0"], d["y0"]),
            lambda: ph_wo0(ctx, d["y0"], d["wo0"], d["acc0"], d["x1"]),
            lambda: ph_layernorm(ctx, d["x1"], d["xn1"], z_ap=d["z"]),
            lambda: ph_mix1(ctx, d["xn1"], d["maa_x"], d["maa5"], d["w1maa"],
                            d["w2m"], d["td"], d["w1dec"], d["w2dec"], xf,
                            d["glogE"]),
            lambda: ph_qkvg1(ctx, xf, d["glogE"], d["wq1"], d["wk1"],
                             d["wv1"], d["wg1"], d["bgt"], d["q1"], d["k1"],
                             d["vtm1"], d["gate1"]),
            lambda: ph_scan1(ctx, d["q1"], d["k1"], d["vtm1"], d["glogE"],
                             d["o1"]),
            lambda: ph_l1f_attn(ctx, d["z"], d["fqkv0"], d["fqs"], d["fks"],
                                d["o1f"], 0),
            lambda: ph_l1f_wo(ctx, d["o1f"], d["fwo0"], d["z"]),
            lambda: ph_l1f_ffn_half(ctx, d["z"], d["fw10"], d["fb1a"],
                                    d["fb1g"], d["fw20"], d["fb2"], d["flng"],
                                    d["flnb"], 0, 0, d["xnf"]),
            lambda: ph_l1f_ffn_half(ctx, d["z"], d["fw10"], d["fb1a"],
                                    d["fb1g"], d["fw20"], d["fb2"], d["flng"],
                                    d["flnb"], 0, 1, d["xnf"]),
            lambda: ph_l1f_attn(ctx, d["z"], d["fqkv1"], d["fqs"], d["fks"],
                                d["o1f"], 1),
            lambda: ph_l1f_wo(ctx, d["o1f"], d["fwo1"], d["z"]),
            lambda: ph_l1f_ffn_half(ctx, d["z"], d["fw11"], d["fb1a"],
                                    d["fb1g"], d["fw21"], d["fb2"], d["flng"],
                                    d["flnb"], 1, 0, d["xnf"]),
            lambda: ph_l1f_ffn_half(ctx, d["z"], d["fw11"], d["fb1a"],
                                    d["fb1g"], d["fw21"], d["fb2"], d["flng"],
                                    d["flnb"], 1, 1, d["xnf"]),
            lambda: ph_out1(ctx, d["o1"], d["gate1"], d["wo1"], d["x1"],
                            d["z"], d["out"]),
        ]

    if fused:
        nc = bass.Bass("TRN2", target_bir_lowering=False)
        es = contextlib.ExitStack()
        es.__enter__()
        d = {}
        for nm, (shape, dt) in in_specs.items():
            d[nm] = nc.dram_tensor(nm, list(shape), dt, kind="ExternalInput").ap()
        for nm, (shape, dt) in out_specs.items():
            d[nm] = nc.dram_tensor(nm, list(shape), dt, kind="ExternalOutput").ap()
        for nm, (shape, dt) in inter_specs.items():
            d[nm] = nc.dram_tensor(nm, list(shape), dt, kind="Internal").ap()
        pb = PB(nc, es)
        ctx = {"nc": nc, "pb": pb, "es": es}
        _mk_consts(ctx)
        names = ["ln0", "ffn0", "attn0", "wo0", "ln1", "mix1", "qkvg1",
                 "scan1", "lfa0", "lfw0", "lff0a", "lff0b",
                 "lfa1", "lfw1", "lff1a", "lff1b", "out1"]
        for nm, fn in zip(names, phase_list(ctx, d)):
            pb.phase = nm
            fn()
        pb.emit()
        return [(nc, list(in_specs), list(out_specs))]
    else:
        launches = []
        all_specs = {}
        all_specs.update(in_specs)
        all_specs.update(out_specs)
        all_specs.update(inter_specs)
        pio = [
            (["x"], ["xn0"]),
            (["xn0", "x", "w10", "b1t0", "w20", "b2t0"], ["acc0"]),
            (["xn0", "wqkv0"], ["y0"]),
            (["y0", "wo0", "acc0"], ["x1"]),
            (["x1"], ["xn1", "z"]),
            (["xn1", "maa_x", "maa5", "w1maa", "w2m", "td", "w1dec", "w2dec"],
             ["xf_r", "xf_k", "xf_v", "xf_g", "glogE"]),
            (["xf_r", "xf_k", "xf_v", "xf_g", "glogE", "wq1", "wk1", "wv1",
              "wg1", "bgt"], ["q1", "k1", "vtm1", "gate1"]),
            (["q1", "k1", "vtm1", "glogE"], ["o1"]),
            (["o1", "gate1", "wo1", "x1"], ["acc1"]),
            (["z", "fqkv0", "fqs", "fks"], ["o1f"]),
            (["o1f", "fwo0", "z"], ["z"]),
            (["z", "fw10", "fb1a", "fb1g", "fw20", "fb2", "flng", "flnb"],
             ["z", "xnf"]),
            (["z", "xnf", "fw10", "fb1a", "fb1g", "fw20", "fb2", "flng",
              "flnb"], ["z"]),
            (["z", "fqkv1", "fqs", "fks"], ["o1f"]),
            (["o1f", "fwo1", "z"], ["z"]),
            (["z", "fw11", "fb1a", "fb1g", "fw21", "fb2", "flng", "flnb"],
             ["z", "xnf"]),
            (["z", "xnf", "fw11", "fb1a", "fb1g", "fw21", "fb2", "flng",
              "flnb"], ["z"]),
            (["acc1", "z"], ["out"]),
        ]
        for pi, (ins, outs) in enumerate(pio):
            nc = bass.Bass("TRN2", target_bir_lowering=False)
            es = contextlib.ExitStack()
            es.__enter__()
            d = {}
            for nm in ins:
                shape, dt = all_specs[nm]
                d[nm] = nc.dram_tensor(nm, list(shape), dt,
                                       kind="ExternalInput").ap()
            for nm in outs:
                shape, dt = all_specs[nm]
                if nm in d:
                    d[nm + "_out"] = nc.dram_tensor(
                        nm + "_out", list(shape), dt, kind="ExternalOutput").ap()
                else:
                    d[nm] = nc.dram_tensor(nm, list(shape), dt,
                                           kind="ExternalOutput").ap()
            pb = PB(nc, es)
            ctx = {"nc": nc, "pb": pb, "es": es}
            _mk_consts(ctx)
            full = phase_list(ctx, _MultiDict(d))
            full[pi]()
            pb.emit()
            launches.append((nc, ins, outs))
        return launches


class _MultiDict(dict):
    def __missing__(self, key):
        raise KeyError(key)


def prep_inputs(inputs):
    g = lambda k: np.asarray(inputs[k], dtype=np.float32)
    w = {}
    w["wqkv0"] = g("L0_attn_W_qkv")
    w["wo0"] = g("L0_attn_W_o")
    w["w10"] = g("L0_ffn_w1")
    w["b1t0"] = _perpart(g("L0_ffn_b1"))
    w["w20"] = g("L0_ffn_w2")
    w["b2t0"] = _perpart(g("L0_ffn_b2"))
    w["maa_x"] = _perpart(g("L1_attn_time_maa_x"))
    maa5 = np.zeros((128, 40), np.float32)
    for f, k in enumerate(("r", "k", "v", "w", "g")):
        maa5[:, f * 8:(f + 1) * 8] = _perpart(g(f"L1_attn_time_maa_{k}"))
    w["maa5"] = maa5
    w1maa = g("L1_attn_time_maa_w1")
    w1p = np.zeros((1024, 320), np.float32)
    for f in range(5):
        w1p[:, f * 64:f * 64 + 32] = w1maa[:, f * 32:(f + 1) * 32]
    w["w1maa"] = w1p
    w2maa = g("L1_attn_time_maa_w2")
    w2m = np.zeros((128, 5 * 1024), np.float32)
    for f in range(5):
        w2m[0:32, f * 1024:(f + 1) * 1024] = w2maa[f]
        w2m[64:96, f * 1024:(f + 1) * 1024] = w2maa[f]
    w["w2m"] = w2m
    w["td"] = _perpart(g("L1_attn_time_decay"))
    w["w1dec"] = g("L1_attn_time_decay_w1")
    w["w2dec"] = g("L1_attn_time_decay_w2")
    w["wq1"] = g("L1_attn_Wq")
    w["wk1"] = g("L1_attn_Wk")
    w["wv1"] = g("L1_attn_Wv")
    w["wg1"] = g("L1_attn_Wg")
    w["wo1"] = g("L1_attn_Wo")
    w["bgt"] = _perpart(g("L1_attn_bg"))
    fqkv = g("L1_ffn_wqkv")
    w["fqkv0"], w["fqkv1"] = (np.ascontiguousarray(fqkv[0]),
                              np.ascontiguousarray(fqkv[1]))
    qsv = np.zeros((128, 2), np.float32)
    ksv = np.zeros((128, 2), np.float32)
    for l in range(2):
        qsv[:, l] = np.tile(g("L1_ffn_q_scale")[l], 2)
        ksv[:, l] = np.tile(g("L1_ffn_k_scale")[l], 2)
    w["fqs"], w["fks"] = qsv, ksv
    fwo = g("L1_ffn_wo")
    w["fwo0"], w["fwo1"] = (np.ascontiguousarray(fwo[0]),
                            np.ascontiguousarray(fwo[1]))
    HID = F_HID
    fw1 = g("L1_ffn_w1")
    fb1 = g("L1_ffn_b1")
    b1a = np.zeros((128, 44), np.float32)
    b1g = np.zeros((128, 44), np.float32)
    for l in range(2):
        w1p_ = np.zeros((1024, 2 * F_HIDP), np.float32)
        for j in range(22):
            n = min(128, HID - j * 128)
            w1p_[:, j * 256:j * 256 + n] = fw1[l][:, j * 128:j * 128 + n]
            w1p_[:, j * 256 + 128:j * 256 + 128 + n] = \
                fw1[l][:, HID + j * 128:HID + j * 128 + n]
            b1a[0:n, j * 2 + l] = fb1[l][j * 128:j * 128 + n]
            b1g[0:n, j * 2 + l] = fb1[l][HID + j * 128:HID + j * 128 + n]
        w[f"fw1{l}"] = w1p_
    w["fb1a"], w["fb1g"] = b1a, b1g
    fw2 = g("L1_ffn_w2")
    for l in range(2):
        w2p_ = np.zeros((F_HIDP, 1024), np.float32)
        w2p_[0:HID] = fw2[l]
        w[f"fw2{l}"] = w2p_
    fb2 = g("L1_ffn_b2")
    flng = np.zeros((128, 16), np.float32)
    flnb = np.zeros((128, 16), np.float32)
    fb2t = np.zeros((128, 16), np.float32)
    for l in range(2):
        flng[:, l::2] = _perpart(g("L1_ffn_ln_g")[l])
        flnb[:, l::2] = _perpart(g("L1_ffn_ln_b")[l])
        fb2t[:, l::2] = _perpart(g("L1_ffn_b2")[l])
    w["flng"], w["flnb"], w["fb2"] = flng, flnb, fb2t
    return w


_CACHE = {}


def kernel(**inputs):
    w = prep_inputs(inputs)
    x = np.asarray(inputs["L0_attn_x"], dtype=np.float32)
    launches = build_full(fused=True)
    state = [dict(w) for _ in range(NCORE)]
    for c in range(NCORE):
        state[c]["x"] = prep_core_x(x, c)
    outs = [None] * NCORE
    for (nc, ins, outs_names) in launches:
        in_maps = []
        for c in range(NCORE):
            in_maps.append({nm: state[c][nm] for nm in ins})
        res = run_bass_kernel_spmd(nc, in_maps, core_ids=list(range(NCORE)))
        for c in range(NCORE):
            rr = res.results[c]
            for nm in outs_names:
                key = nm + "_out" if (nm + "_out") in rr else nm
                state[c][nm] = np.asarray(rr[key])
    return unprep_out([state[c]["out"] for c in range(NCORE)])


# version 16 (final)
# speedup vs baseline: 1.0331x; 1.0331x over previous
import contextlib
import numpy as np
import ml_dtypes
import concourse.bass as bass
import concourse.mybir as mybir
from concourse.bass_utils import run_bass_kernel_spmd

F32 = mybir.dt.float32
F32R = mybir.dt.float32r
BF16 = mybir.dt.bfloat16
AF = mybir.ActivationFunctionType
ALU = mybir.AluOpType
AX = mybir.AxisListType

NCORE = 8
B = 16
T = 2048
D = 1024
NB = B // NCORE
NT = NB * T
DC = D // 128
TT = 512
NTT = NT // TT
EPS = 1e-5

H0, DK0 = 16, 64
H1, HD1 = 16, 64
NM, DMIX, DDEC = 5, 32, 64
F_HEADS, F_DH, F_WIN = 8, 64, 512
F_INNER = F_HEADS * F_DH
F_HID = 2730
F_HIDP = 2816
F_NW = T // F_WIN

ENGINES = ("pe", "dve", "act", "pool", "sp")
_UNM = [0]


def unm(base):
    _UNM[0] += 1
    return f"{base}_{_UNM[0]}"

SECT = {"pe": "tensor", "dve": "vector", "act": "scalar", "pool": "gpsimd",
        "sp": "sync"}


class PB:
    POOL = 18

    def __init__(self, nc, es):
        self.nc = nc
        self.es = es
        self.ops = {e: [] for e in ENGINES}
        self.cnt = {}
        self.semh = {}
        self.hw = {e: {} for e in ENGINES}
        self.dma_sems = set()
        self.alias = {}
        self.alias_n = {}
        self.phase = "init"
        for e in ENGINES:
            self._sem("s_" + e)

    def _canon(self, qeng, name):
        key = (qeng, name)
        if key not in self.alias:
            i = self.alias_n.get(qeng, 0)
            assert i < self.POOL, f"stream pool exhausted on {qeng}"
            self.alias_n[qeng] = i + 1
            self.alias[key] = f"sq_{qeng}_{i}"
        return self.alias[key]

    def _sem(self, name):
        if name not in self.semh:
            self.semh[name] = self.es.enter_context(self.nc.semaphore(name))
            self.cnt[name] = 0
        return name

    def _waits(self, eng, after):
        ws = []
        hw = self.hw[eng]
        for tok in after:
            if tok is None:
                continue
            sn, v = tok
            if v <= 0 or hw.get(sn, 0) >= v:
                continue
            if sn in self.dma_sems and v < self.cnt[sn]:
                v = self.cnt[sn]
                if hw.get(sn, 0) >= v:
                    continue
            ws.append((sn, v))
            hw[sn] = v
        return ws

    def op(self, eng, fn, after=()):
        sn = "s_" + eng
        if eng == "pe":
            after = [t for t in after if t is not None and t[0] != sn]
        ws = self._waits(eng, after)
        self.cnt[sn] += 1
        tok = (sn, self.cnt[sn])
        self.ops[eng].append((ws, fn, (sn, 1), self.phase))
        return tok

    def dma(self, qeng, stream, out, in_, after=()):
        sn = self._sem(self._canon(qeng, stream))
        self.dma_sems.add(sn)
        ws = self._waits(qeng, after)
        self.cnt[sn] += 16
        tok = (sn, self.cnt[sn])

        def fn(e, out=out, in_=in_):
            return e.dma_start(out=out, in_=in_)
        self.ops[qeng].append((ws, fn, (sn, 16), self.phase))
        return tok

    def cur(self, stream, qeng=None):
        if qeng is None:
            hits = [v for (q, n), v in self.alias.items() if n == stream]
            assert len(hits) == 1, f"ambiguous stream {stream}: {hits}"
            sn = hits[0]
        else:
            sn = self._canon(qeng, stream)
        return (sn, self.cnt[sn])

    def barrier(self):
        toks = [(sn, c) for sn, c in self.cnt.items() if c > 0]
        for e in ENGINES:
            ws = self._waits(e, toks)
            if ws:
                self.ops[e].append((ws, None, None, self.phase))
        self.alias = {}
        self.alias_n = {}

    def emit(self):
        block = self.es.enter_context(self.nc.Block())
        for eng in ENGINES:
            lst = self.ops[eng]
            semh = self.semh

            nc = self.nc

            def body(e, lst=lst):
                import contextlib as _cl
                cur_ph = None
                stk = _cl.ExitStack()
                for (ws, fn, inc, phname) in lst:
                    if phname != cur_ph:
                        stk.close()
                        stk = _cl.ExitStack()
                        stk.enter_context(nc.named_scope(phname))
                        cur_ph = phname
                    for (sn, v) in ws:
                        e.wait_ge(semh[sn], v)
                    if fn is not None:
                        inst = fn(e)
                        if inc is not None:
                            inst.then_inc(semh[inc[0]], inc[1])
                stk.close()
            getattr(block, SECT[eng])(body)


class Buf:
    __slots__ = ("ap", "wtok", "rtoks")

    def __init__(self, ap):
        self.ap = ap
        self.wtok = None
        self.rtoks = []

    def deps_w(self):
        d = list(self.rtoks)
        if self.wtok is not None:
            d.append(self.wtok)
        return d

    def wrote(self, tok):
        self.wtok = tok
        self.rtoks = []

    def read(self, tok):
        self.rtoks.append(tok)


def _tril_maskT():
    return np.triu(np.ones((128, 128), dtype=np.float32))


def _rot_perm():
    P = np.zeros((128, 128), dtype=np.float32)
    for hb in (0, 64):
        for j in range(32):
            P[hb + j + 32, hb + j] = -1.0
        for j in range(32, 64):
            P[hb + j - 32, hb + j] = 1.0
    return P.astype(ml_dtypes.bfloat16)


def _rotary_tables():
    n = T
    half = np.arange(0, F_DH, 2, dtype=np.float64)
    inv_freq = 1.0 / 10000.0 ** (half / F_DH)
    t = np.arange(n, dtype=np.float64)
    freqs = t[:, None] * inv_freq[None, :]
    freqs = np.concatenate([freqs, freqs], axis=1)
    base = (half + 0.4 * F_DH) / (1.4 * F_DH)
    power = (t - n // 2) / float(F_WIN // 2)
    scale = base[None, :] ** power[:, None]
    scale = np.concatenate([scale, scale], axis=1)
    cos, sin = np.cos(freqs), np.sin(freqs)
    qcs = (cos * scale).T
    qss = (sin * scale).T
    kcs = (cos / scale).T
    kss = (sin / scale).T

    def two(x):
        return np.concatenate([x, x], axis=0).astype(ml_dtypes.bfloat16)
    return two(qcs), two(qss), two(kcs), two(kss)


def _locattn_masks():
    s_ = np.arange(1024)
    mq = np.zeros((128, 4 * 1024), dtype=np.float32)
    for r in range(4):
        for tl in range(128):
            i_rel = 512 + r * 128 + tl
            bad = ~((0 <= (i_rel - s_)) & ((i_rel - s_) <= 512))
            mq[tl, r * 1024:(r + 1) * 1024] = np.where(bad, -1e30, 0.0)
    mt = np.zeros((128, 8 * 512), dtype=np.float32)
    tloc = np.arange(512)
    for ss in range(8):
        for sl in range(128):
            j_rel = ss * 128 + sl
            i_rel = 512 + tloc
            bad = ~((0 <= (i_rel - j_rel)) & ((i_rel - j_rel) <= 512))
            mt[sl, ss * 512:(ss + 1) * 512] = np.where(bad, -1e30, 0.0)
    return mq.astype(ml_dtypes.bfloat16), mt.astype(ml_dtypes.bfloat16)


def _perpart(v):
    v = np.asarray(v, dtype=np.float32).reshape(-1)
    n = v.shape[0] // 128
    return np.ascontiguousarray(v.reshape(n, 128).T)


def const_to_sbuf(ctx, name, np_arr, dtype):
    nc, pb, es = ctx["nc"], ctx["pb"], ctx["es"]
    h = nc.inline_tensor(np_arr, name)
    tile = es.enter_context(nc.sbuf_tensor(name + "_sb", list(np_arr.shape), dtype))
    tok = pb.dma("sp", "s_const", tile[:], h.ap())
    b = Buf(tile[:])
    b.wrote(tok)
    return b


def ph_layernorm(ctx, x_ap, xn_ap, z_ap=None, g_ap=None, b_ap=None):
    nc, pb, es = ctx["nc"], ctx["pb"], ctx["es"]
    ones_r = ctx["ones_f32r"]
    with contextlib.ExitStack() as ph:
        sb = lambda n, s, d: ph.enter_context(nc.sbuf_tensor(unm(n), s, d))
        xt = [Buf(sb(f"ln_x{i}", [128, DC * TT], F32R)[:]) for i in range(2)]
        sq = [Buf(sb(f"ln_sq{i}", [128, DC * TT], F32R)[:]) for i in range(2)]
        M = [Buf(sb(f"ln_m{i}", [128, TT], F32)[:]) for i in range(2)]
        RS = [Buf(sb(f"ln_rs{i}", [128, TT], F32)[:]) for i in range(2)]
        VAR = [Buf(sb(f"ln_v{i}", [128, TT], F32)[:]) for i in range(2)]
        SCR = [[Buf(sb(f"ln_s{i}{e}", [128, TT], F32)[:]) for e in range(2)]
               for i in range(2)]
        xnb = [Buf(sb(f"ln_xn{i}", [128, DC * TT], BF16)[:]) for i in range(2)]
        znb = [Buf(sb(f"ln_zn{i}", [128, DC * TT], F32)[:]) for i in range(2)] \
            if z_ap is not None else None
        psum = ph.enter_context(nc.psum_tensor(unm("ln_ps"), [128, 4 * TT], F32))
        ps = [Buf(psum[:, i * TT:(i + 1) * TT]) for i in range(4)]

        for t in range(NTT):
            ib = t % 2
            X, SQ, Mb, RSb, Vb, XN = xt[ib], sq[ib], M[ib], RS[ib], VAR[ib], xnb[ib]
            ZN = znb[ib] if znb is not None else None
            p1, p2 = ps[2 * ib], ps[2 * ib + 1]
            ltoks = []
            for c in range(DC):
                tok = pb.dma("sp", f"s_lnx{t % 3}",
                             X.ap[:, c * TT:(c + 1) * TT],
                             x_ap[c * 128:(c + 1) * 128, t * TT:(t + 1) * TT]
                             .bitcast(F32R),
                             after=X.deps_w())
                ltoks.append(tok)
            X.wrote(ltoks[-1])
            stoks = []
            for c in range(DC):
                tok = pb.op("act", (lambda e, c=c, X=X, SQ=SQ: e.activation(
                    SQ.ap[:, c * TT:(c + 1) * TT],
                    X.ap.bitcast(F32)[:, c * TT:(c + 1) * TT], AF.Square)),
                    after=[X.wtok] + SQ.deps_w())
                stoks.append(tok)
            SQ.wrote(stoks[-1])
            mtok = None
            for c in range(DC):
                mtok = pb.op("pe", (lambda e, c=c, X=X, p1=p1: e.matmul(
                    p1.ap, ones_r.ap, X.ap[:, c * TT:(c + 1) * TT],
                    start=(c == 0), stop=(c == DC - 1))),
                    after=[X.wtok, ones_r.wtok] + (p1.deps_w() if c == 0 else []))
            p1.wrote(mtok)
            for c in range(DC):
                mtok = pb.op("pe", (lambda e, c=c, SQ=SQ, p2=p2: e.matmul(
                    p2.ap, ones_r.ap, SQ.ap[:, c * TT:(c + 1) * TT],
                    start=(c == 0), stop=(c == DC - 1))),
                    after=[stoks[-1], ones_r.wtok] + (p2.deps_w() if c == 0 else []))
            p2.wrote(mtok)
            tok = pb.op("dve", (lambda e, Mb=Mb, p1=p1: e.tensor_scalar_mul(
                Mb.ap, p1.ap, 1.0 / D)), after=[p1.wtok] + Mb.deps_w())
            Mb.wrote(tok); p1.read(tok)
            tok = pb.op("dve", (lambda e, Vb=Vb, Mb=Mb: e.tensor_mul(
                Vb.ap, Mb.ap, Mb.ap)), after=[Mb.wtok] + Vb.deps_w())
            Vb.wrote(tok); Mb.read(tok)
            tok = pb.op("dve", (lambda e, Vb=Vb, p2=p2: e.scalar_tensor_tensor(
                Vb.ap, p2.ap, 1.0 / D, Vb.ap, ALU.mult, ALU.subtract)),
                after=[p2.wtok, Vb.wtok])
            Vb.wrote(tok); p2.read(tok)
            cp = ctx["cpool"]
            ci = ctx["cidx"][EPS]
            tok = pb.op("act", (lambda e, RSb=RSb, Vb=Vb, cp=cp, ci=ci:
                                e.activation(RSb.ap, Vb.ap, AF.Ln,
                                             bias=cp.ap[:, ci:ci + 1])),
                        after=[Vb.wtok, cp.wtok] + RSb.deps_w())
            RSb.wrote(tok); Vb.read(tok); cp.read(tok)
            tok = pb.op("act", (lambda e, RSb=RSb: e.activation(
                RSb.ap, RSb.ap, AF.Exp, scale=-0.5)), after=[RSb.wtok])
            RSb.wrote(tok)
            ztoks = [None] * DC
            xtoks = [None] * DC
            for c in range(DC):
                ei = c % 2
                eng = "dve" if ei == 0 else "pool"
                Sc = SCR[ib][ei]
                sl = slice(c * TT, (c + 1) * TT)
                tok = pb.op(eng, (lambda e, sl=sl, Sc=Sc, Mb=Mb, X=X:
                                  e.tensor_sub(Sc.ap,
                                               X.ap.bitcast(F32)[:, sl], Mb.ap)),
                            after=[X.wtok, Mb.wtok] + Sc.deps_w())
                Sc.wrote(tok); Mb.read(tok); X.read(tok)
                if g_ap is not None:
                    tok = pb.op(eng, (lambda e, Sc=Sc, RSb=RSb:
                                      e.tensor_mul(Sc.ap, Sc.ap, RSb.ap)),
                                after=[Sc.wtok, RSb.wtok])
                    Sc.wrote(tok); RSb.read(tok)
                    dst = ZN if ZN is not None else XN
                    tok2 = pb.op("dve", (lambda e, c=c, sl=sl, Sc=Sc, dst=dst:
                                       e.tensor_scalar(
                                           dst.ap[:, sl], Sc.ap,
                                           g_ap.ap[:, c:c + 1], b_ap.ap[:, c:c + 1],
                                           ALU.mult, ALU.add)),
                                 after=[Sc.wtok, g_ap.wtok, b_ap.wtok] + dst.deps_w())
                    dst.wrote(tok2); Sc.read(tok2)
                    g_ap.read(tok2); b_ap.read(tok2)
                else:
                    dst = ZN if ZN is not None else XN
                    tok2 = pb.op(eng, (lambda e, sl=sl, Sc=Sc, RSb=RSb, dst=dst:
                                       e.tensor_mul(dst.ap[:, sl], Sc.ap, RSb.ap)),
                                 after=[Sc.wtok, RSb.wtok] + dst.deps_w())
                    dst.wrote(tok2); Sc.read(tok2); RSb.read(tok2)
                if ZN is not None:
                    ztoks[c] = tok2
                    tok3 = pb.op(eng, (lambda e, sl=sl, XN=XN, ZN=ZN:
                                       e.tensor_copy(XN.ap[:, sl], ZN.ap[:, sl])),
                                 after=[tok2] + XN.deps_w())
                    XN.wrote(tok3); ZN.read(tok3)
                    xtoks[c] = tok3
                else:
                    xtoks[c] = tok2
            for c in range(DC):
                sl = slice(c * TT, (c + 1) * TT)
                pb.dma("act", f"s_lnst{t % 3}",
                       xn_ap[c * 128:(c + 1) * 128, t * TT:(t + 1) * TT],
                       XN.ap[:, sl], after=[xtoks[c]])
                if ZN is not None:
                    pb.dma("sp", f"s_lnzst{t % 3}",
                           z_ap[c * 128:(c + 1) * 128, t * TT:(t + 1) * TT],
                           ZN.ap[:, sl], after=[ztoks[c]])
            XN.rtoks.append(pb.cur(f"s_lnst{t % 3}"))
            if ZN is not None:
                ZN.rtoks.append(pb.cur(f"s_lnzst{t % 3}"))
        pb.barrier()


def ph_ffn0_half(ctx, xn_ap, x_ap, w1_ap, b1t_ap, w2_ap, b2t_ap, acc_ap, half):
    nc, pb, es = ctx["nc"], ctx["pb"], ctx["es"]
    NH = 2048 // 128
    h0 = half * 2048
    with contextlib.ExitStack() as ph:
        sb = lambda n, s, d: ph.enter_context(nc.sbuf_tensor(unm(n), s, d))
        w1 = Buf(sb("f0_w1", [128, DC * 2048], BF16)[:])
        w2 = Buf(sb("f0_w2", [128, NH * 1024], BF16)[:])
        b1 = Buf(sb("f0_b1", [128, NH], F32)[:])
        b2 = Buf(sb("f0_b2", [128, DC], F32)[:])
        xn = [Buf(sb(f"f0_xn{i}", [128, DC * TT], BF16)[:]) for i in range(2)]
        xr = [Buf(sb(f"f0_x{i}", [128, DC * TT], F32)[:]) for i in range(2)]
        hb = [Buf(sb(f"f0_h{i}", [128, NH * TT], BF16)[:]) for i in range(2)]
        ob = [Buf(sb(f"f0_o{i}", [128, DC * TT], F32)[:]) for i in range(1)]
        psum = ph.enter_context(nc.psum_tensor(unm("f0_ps"), [128, 8 * TT], F32))
        ps = [Buf(psum[:, i * TT:(i + 1) * TT]) for i in range(8)]

        w1toks = [None] * DC
        w2toks = [None] * NH

        for k in range(DC):
            pb.dma("pool", f"s_w1_{k}", w1.ap[:, k * 2048:(k + 1) * 2048],
                   w1_ap[k * 128:(k + 1) * 128, h0:h0 + 2048],
                   after=w1.deps_w())
        for k in range(NH):
            pb.dma("pool", f"s_w2_{k % 8}", w2.ap[:, k * 1024:(k + 1) * 1024],
                   w2_ap[h0 + k * 128:h0 + (k + 1) * 128, :],
                   after=w2.deps_w())
        w1toks = [pb.cur(f"s_w1_{k}") for k in range(DC)]
        w2toks = [pb.cur(f"s_w2_{k % 8}") for k in range(NH)]
        pb.dma("sp", "s_b", b1.ap, b1t_ap[:, half * NH:(half + 1) * NH],
               after=b1.deps_w())
        if half == 0:
            pb.dma("sp", "s_b", b2.ap, b2t_ap, after=b2.deps_w())
        b1.wrote(pb.cur("s_b"))
        b2.wrote(pb.cur("s_b"))

        for t in range(NTT):
            ib = t % 2
            XN, XR, HB, OB = xn[ib], xr[ib], hb[ib], ob[0]
            xtoks = []
            for c in range(DC):
                tok = pb.dma("sp", f"s_fxn{t % 3}",
                             XN.ap[:, c * TT:(c + 1) * TT],
                             xn_ap[c * 128:(c + 1) * 128, t * TT:(t + 1) * TT],
                             after=XN.deps_w())
                xtoks.append(tok)
            XN.wrote(xtoks[-1])
            rtoks = []
            rsrc = x_ap if half == 0 else acc_ap
            for c in range(DC):
                tok = pb.dma("act", f"s_fx{t % 3}",
                             XR.ap[:, c * TT:(c + 1) * TT],
                             rsrc[c * 128:(c + 1) * 128, t * TT:(t + 1) * TT],
                             after=XR.deps_w())
                rtoks.append(tok)
            XR.wrote(rtoks[-1])
            htoks = []
            for hc in range(NH):
                p = ps[hc % 4]
                mtok = None
                for k in range(DC):
                    mtok = pb.op("pe", (lambda e, k=k, hc=hc, p=p, XN=XN: e.matmul(
                        p.ap, w1.ap[:, k * 2048 + hc * 128: k * 2048 + hc * 128 + 128],
                        XN.ap[:, k * TT:(k + 1) * TT],
                        start=(k == 0), stop=(k == DC - 1))),
                        after=[XN.wtok, w1toks[k]] + (p.deps_w() if k == 0 else []))
                p.wrote(mtok)
                w1.read(mtok); XN.read(mtok)
                tok = pb.op("act", (lambda e, hc=hc, p=p, HB=HB: e.activation(
                    HB.ap[:, hc * TT:(hc + 1) * TT], p.ap, AF.Gelu,
                    bias=b1.ap[:, hc:hc + 1])),
                    after=[p.wtok, b1.wtok] + HB.deps_w())
                htoks.append(tok)
                p.read(tok); b1.read(tok)
            HB.wrote(htoks[-1])
            hread = [[] for _ in range(NH)]
            for oc in range(DC):
                p = ps[4 + oc % 4]
                mtok = None
                for k in range(NH):
                    mtok = pb.op("pe", (lambda e, k=k, oc=oc, p=p, HB=HB: e.matmul(
                        p.ap, w2.ap[:, k * 1024 + oc * 128: k * 1024 + oc * 128 + 128],
                        HB.ap[:, k * TT:(k + 1) * TT],
                        start=(k == 0), stop=(k == NH - 1))),
                        after=[htoks[k], w2toks[k]] + (p.deps_w() if k == 0 else []))
                    hread[k].append(mtok)
                p.wrote(mtok)
                w2.read(mtok)
                if half == 0:
                    tok = pb.op("dve", (lambda e, oc=oc, p=p, OB=OB, XR=XR:
                                        e.scalar_tensor_tensor(
                                            OB.ap[:, oc * TT:(oc + 1) * TT], p.ap,
                                            b2.ap[:, oc:oc + 1],
                                            XR.ap[:, oc * TT:(oc + 1) * TT],
                                            ALU.add, ALU.add)),
                                after=[p.wtok, XR.wtok, b2.wtok] + OB.deps_w())
                    b2.read(tok)
                else:
                    tok = pb.op("dve", (lambda e, oc=oc, p=p, OB=OB, XR=XR:
                                        e.tensor_add(
                                            OB.ap[:, oc * TT:(oc + 1) * TT], p.ap,
                                            XR.ap[:, oc * TT:(oc + 1) * TT])),
                                after=[p.wtok, XR.wtok] + OB.deps_w())
                OB.wrote(tok); p.read(tok); XR.read(tok)
                stok = pb.dma("act", f"s_fst{t % 3}",
                              acc_ap[oc * 128:(oc + 1) * 128, t * TT:(t + 1) * TT],
                              OB.ap[:, oc * TT:(oc + 1) * TT], after=[OB.wtok])
                OB.read(stok)
            for k in range(NH):
                HB.rtoks.extend(hread[k][-1:])
        pb.barrier()


def ph_ffn0(ctx, xn_ap, x_ap, w1_ap, b1t_ap, w2_ap, b2t_ap, acc_ap):
    ph_ffn0_half(ctx, xn_ap, x_ap, w1_ap, b1t_ap, w2_ap, b2t_ap, acc_ap, 0)
    ph_ffn0_half(ctx, xn_ap, x_ap, w1_ap, b1t_ap, w2_ap, b2t_ap, acc_ap, 1)


def ph_attn0_half(ctx, xn_ap, wqkv_ap, y_ap, half):
    nc, pb, es = ctx["nc"], ctx["pb"], ctx["es"]
    ident = ctx["ident_bf"]
    bdiag = ctx["bdiag_bf"]
    trilT = ctx["trilT_f32"]
    CH = 128
    NCH = T // CH
    with contextlib.ExitStack() as ph:
        sb = lambda n, s, d: ph.enter_context(nc.sbuf_tensor(unm(n), s, d))
        wq = Buf(sb("a0_wq", [128, DC * 512], BF16)[:])
        wk = Buf(sb("a0_wk", [128, DC * 512], BF16)[:])
        wv = Buf(sb("a0_wv", [128, DC * 512], BF16)[:])
        qf = Buf(sb("a0_qf", [128, 4 * NT], BF16)[:])
        kf = Buf(sb("a0_kf", [128, 4 * NT], BF16)[:])
        vtm = Buf(sb("a0_vtm", [128, (NT // 128) * 512], BF16)[:])
        yb = Buf(sb("a0_yb", [128, 4 * NT], BF16)[:])

        for k in range(DC):
            for nm, wb, c0 in (("q", wq, half * 512),
                               ("k", wk, 1024 + half * 512),
                               ("v", wv, 2048 + half * 512)):
                pb.dma("pool", f"s_a0w{k % 4}",
                       wb.ap[:, k * 512:(k + 1) * 512],
                       wqkv_ap[k * 128:(k + 1) * 128, c0:c0 + 512],
                       after=wb.deps_w())
        wtoks = {nm: [pb.cur(f"s_a0w{k % 4}") for k in range(DC)]
                 for nm in ("q", "k", "v")}

        with contextlib.ExitStack() as st1:
            sb1 = lambda n, s, d: st1.enter_context(nc.sbuf_tensor(unm(n), s, d))
            xn = [Buf(sb1("a0_xn", [128, DC * TT], BF16)[:]) for _ in range(1)]
            SQ = Buf(sb1("a0_sq", [128, TT], BF16)[:])
            KRB = Buf(sb1("a0_krb", [128, TT], BF16)[:])
            MEAN = Buf(sb1("a0_mean", [128, TT], F32)[:])
            VAR = Buf(sb1("a0_var", [128, TT], F32)[:])
            RSTD = Buf(sb1("a0_rstd", [128, TT], F32)[:])
            SCR = Buf(sb1("a0_scr", [128, TT], F32)[:])
            psums = st1.enter_context(
                nc.psum_tensor(unm("a0_ps"), [128, 6 * TT], F32))
            pjq = [Buf(psums[:, i * TT:(i + 1) * TT]) for i in range(4)]
            pst = [Buf(psums[:, (4 + i) * TT:(5 + i) * TT]) for i in range(2)]

            for t in range(NTT):
                XN = xn[0]
                xtoks = []
                for c in range(DC):
                    tok = pb.dma("sp", f"s_a0x{t % 3}",
                                 XN.ap[:, c * TT:(c + 1) * TT],
                                 xn_ap[c * 128:(c + 1) * 128,
                                       t * TT:(t + 1) * TT],
                                 after=XN.deps_w())
                    xtoks.append(tok)
                XN.wrote(xtoks[-1])
                for nm, wb, dstb in (("q", wq, qf), ("k", wk, kf)):
                    for hp in range(4):
                        p = pjq[hp]
                        mtok = None
                        for k in range(DC):
                            mtok = pb.op("pe", (lambda e, k=k, hp=hp, p=p, XN=XN,
                                                wb=wb: e.matmul(
                                p.ap,
                                wb.ap[:, k * 512 + hp * 128:k * 512 + hp * 128 + 128],
                                XN.ap[:, k * TT:(k + 1) * TT],
                                start=(k == 0), stop=(k == DC - 1))),
                                after=[XN.wtok, wtoks[nm][k]] +
                                      (p.deps_w() if k == 0 else []))
                        p.wrote(mtok)
                        wb.read(mtok); XN.read(mtok)
                        sl = slice(hp * NT + t * TT, hp * NT + (t + 1) * TT)
                        if nm == "q":
                            tok = pb.op("act", (lambda e, p=p, dstb=dstb, sl=sl:
                                                e.copy(dstb.ap[:, sl], p.ap)),
                                        after=[p.wtok] + dstb.deps_w())
                            dstb.rtoks.append(tok)
                            p.read(tok)
                        else:
                            tok = pb.op("act", (lambda e, p=p: e.copy(
                                KRB.ap, p.ap)), after=[p.wtok] + KRB.deps_w())
                            KRB.wrote(tok); p.read(tok)
                            tok = pb.op("pool", (lambda e: e.tensor_mul(
                                SQ.ap, KRB.ap, KRB.ap)),
                                after=[KRB.wtok] + SQ.deps_w())
                            SQ.wrote(tok); KRB.read(tok)
                            p2, p3 = pst[0], pst[1]
                            tok = pb.op("pe", (lambda e, p2=p2: e.matmul(
                                p2.ap, bdiag.ap, KRB.ap, start=True, stop=True)),
                                after=[KRB.wtok, bdiag.wtok] + p2.deps_w())
                            p2.wrote(tok); KRB.read(tok); bdiag.read(tok)
                            tok = pb.op("pe", (lambda e, p3=p3: e.matmul(
                                p3.ap, bdiag.ap, SQ.ap, start=True, stop=True)),
                                after=[SQ.wtok, bdiag.wtok] + p3.deps_w())
                            p3.wrote(tok); SQ.read(tok); bdiag.read(tok)
                            tok = pb.op("dve", (lambda e, p2=p2: e.tensor_scalar_mul(
                                MEAN.ap, p2.ap, 1.0 / 64)),
                                after=[p2.wtok] + MEAN.deps_w())
                            MEAN.wrote(tok); p2.read(tok)
                            tok = pb.op("dve", (lambda e: e.tensor_mul(
                                VAR.ap, MEAN.ap, MEAN.ap)),
                                after=[MEAN.wtok] + VAR.deps_w())
                            VAR.wrote(tok); MEAN.read(tok)
                            tok = pb.op("dve", (lambda e, p3=p3:
                                                e.scalar_tensor_tensor(
                                VAR.ap, p3.ap, 1.0 / 64, VAR.ap,
                                ALU.mult, ALU.subtract)),
                                after=[p3.wtok, VAR.wtok])
                            VAR.wrote(tok); p3.read(tok)
                            cp, ci = ctx["cpool"], ctx["cidx"][EPS]
                            tok = pb.op("act", (lambda e, cp=cp, ci=ci:
                                                e.activation(
                                RSTD.ap, VAR.ap, AF.Ln,
                                bias=cp.ap[:, ci:ci + 1])),
                                after=[VAR.wtok, cp.wtok] + RSTD.deps_w())
                            RSTD.wrote(tok); VAR.read(tok); cp.read(tok)
                            tok = pb.op("act", (lambda e: e.activation(
                                RSTD.ap, RSTD.ap, AF.Exp, scale=-0.5)),
                                after=[RSTD.wtok])
                            RSTD.wrote(tok)
                            tok = pb.op("dve", (lambda e: e.tensor_sub(
                                SCR.ap, KRB.ap, MEAN.ap)),
                                after=[KRB.wtok, MEAN.wtok] + SCR.deps_w())
                            SCR.wrote(tok); KRB.read(tok); MEAN.read(tok)
                            tok = pb.op("dve", (lambda e, sl=sl, dstb=dstb:
                                                e.tensor_mul(
                                dstb.ap[:, sl], SCR.ap, RSTD.ap)),
                                after=[SCR.wtok, RSTD.wtok] + dstb.deps_w())
                            dstb.rtoks.append(tok)
                            SCR.read(tok); RSTD.read(tok)
                for ts in range(4):
                    p = pjq[ts]
                    mtok = None
                    for k in range(DC):
                        mtok = pb.op("pe", (lambda e, k=k, ts=ts, p=p, XN=XN:
                                            e.matmul(
                            p.ap,
                            XN.ap[:, k * TT + ts * 128:k * TT + ts * 128 + 128],
                            wv.ap[:, k * 512:(k + 1) * 512],
                            start=(k == 0), stop=(k == DC - 1))),
                            after=[XN.wtok, wtoks["v"][k]] +
                                  (p.deps_w() if k == 0 else []))
                    p.wrote(mtok)
                    wv.read(mtok); XN.read(mtok)
                    tc = t * 4 + ts
                    tok = pb.op("act", (lambda e, p=p, tc=tc: e.copy(
                        vtm.ap[:, tc * 512:(tc + 1) * 512], p.ap)),
                        after=[p.wtok] + vtm.deps_w())
                    vtm.rtoks.append(tok)
                    p.read(tok)
            for bb in (qf, kf, vtm):
                bb.wrote(bb.rtoks[-1])
                bb.rtoks = []
            pb.barrier()

        with contextlib.ExitStack() as st2:
            sb2 = lambda n, s, d: st2.enter_context(nc.sbuf_tensor(unm(n), s, d))
            stbf = Buf(sb2("a0_stbf", [128, 8 * 64], BF16)[:])
            stf32 = Buf(sb2("a0_stf32", [128, 8 * 64], F32)[:])
            abf = [Buf(sb2("a0_abf", [128, 128], BF16)[:]) for _ in range(4)]
            ktmc = [Buf(sb2("a0_ktmc", [128, 64], BF16)[:]) for _ in range(4)]
            pS = [Buf(st2.enter_context(
                nc.psum_tensor(unm("a0_pS"), [128, 512], F32))[:])
                for _ in range(2)]
            pA = [Buf(st2.enter_context(
                nc.psum_tensor(unm("a0_pA"), [128, 512], F32))[:, 0:128])
                for _ in range(2)]
            pO = [Buf(st2.enter_context(
                nc.psum_tensor(unm("a0_pO"), [128, 512], F32))[0:64, 0:128])
                for _ in range(2)]
            pT = [Buf(st2.enter_context(
                nc.psum_tensor(unm("a0_pT"), [128, 1024], BF16))[:, 0:64])
                for _ in range(2)]
            stS = [Buf(pS[par].ap[0:64, 0:64]) for par in (0, 1)]
            stF = [[Buf(stf32.ap[par * 64:(par + 1) * 64, i * 64:(i + 1) * 64])
                    for i in range(8)] for par in (0, 1)]
            stB = [[Buf(stbf.ap[par * 64:(par + 1) * 64, i * 64:(i + 1) * 64])
                    for i in range(8)] for par in (0, 1)]

            for hp in range(4):
                for b in range(NB):
                    for c in range(NCH):
                        for par in (0, 1):
                            base = par * 64
                            col = hp * NT + b * T + c * CH
                            q_sl = qf.ap[base:base + 64, col:col + CH]
                            k_sl = kf.ap[base:base + 64, col:col + CH]
                            pa = pA[par]
                            pt = pT[par]
                            kt = ktmc[2 * par + (c % 2)]
                            ab = abf[2 * par + (c % 2)]
                            tok = pb.op("pe", (lambda e, pa=pa, k_sl=k_sl,
                                               q_sl=q_sl: e.matmul(
                                pa.ap, k_sl, q_sl, start=True, stop=True)),
                                after=[kf.wtok, qf.wtok] + pa.deps_w())
                            pa.wrote(tok)
                            kf.read(tok); qf.read(tok)
                            tok = pb.op("dve", (lambda e, pa=pa, ab=ab:
                                                e.scalar_tensor_tensor(
                                ab.ap, pa.ap, 1.0, trilT.ap,
                                ALU.mult, ALU.mult)),
                                after=[pa.wtok, trilT.wtok] + ab.deps_w())
                            ab.wrote(tok); pa.read(tok); trilT.read(tok)
                            idsl = ident.ap[base:base + 64, base:base + 64]
                            tok = pb.op("pe", (lambda e, pt=pt, k_sl=k_sl,
                                               idsl=idsl: e.transpose(
                                pt.ap, k_sl, idsl)),
                                after=[kf.wtok, ident.wtok] + pt.deps_w())
                            pt.wrote(tok)
                            kf.read(tok); ident.read(tok)
                            tok = pb.op("dve", (lambda e, pt=pt, kt=kt:
                                                e.tensor_copy(kt.ap, pt.ap)),
                                after=[pt.wtok] + kt.deps_w())
                            kt.wrote(tok); pt.read(tok)
                        for par in (0, 1):
                            h = hp * 2 + par
                            sidx = b * 4 + hp
                            S = stS[par]
                            SF = stF[par][sidx]
                            SB = stB[par][sidx]
                            base = par * 64
                            col = hp * NT + b * T + c * CH
                            q_sl = qf.ap[base:base + 64, col:col + CH]
                            po = pO[par]
                            kt = ktmc[2 * par + (c % 2)]
                            ab = abf[2 * par + (c % 2)]
                            tci = b * (T // CH) + c
                            v_sl = vtm.ap[:, tci * 512 + h * 64:
                                          tci * 512 + h * 64 + 64]
                            first = (c == 0)
                            if not first:
                                tok = pb.op("pe", (lambda e, po=po, SB=SB,
                                                   q_sl=q_sl: e.matmul(
                                    po.ap, SB.ap, q_sl,
                                    start=True, stop=False)),
                                    after=[SB.wtok, qf.wtok] + po.deps_w())
                                po.wrote(tok)
                                SB.read(tok); qf.read(tok)
                            tok = pb.op("pe", (lambda e, po=po, v_sl=v_sl,
                                               ab=ab, first=first: e.matmul(
                                po.ap, v_sl, ab.ap,
                                start=first, stop=True)),
                                after=[vtm.wtok, ab.wtok] +
                                      (po.deps_w() if first else []))
                            po.wrote(tok)
                            vtm.read(tok); ab.read(tok)
                            ysl = yb.ap[base:base + 64, hp * NT + b * T + c * CH:
                                        hp * NT + b * T + c * CH + CH]
                            tok = pb.op("act", (lambda e, po=po, ysl=ysl:
                                                e.copy(ysl, po.ap)),
                                after=[po.wtok] + yb.deps_w())
                            yb.rtoks.append(tok)
                            po.read(tok)
                            tok = pb.op("pe", (lambda e, S=S, kt=kt, v_sl=v_sl:
                                               e.matmul(
                                S.ap, kt.ap, v_sl, start=True, stop=True)),
                                after=[kt.wtok, vtm.wtok] + S.deps_w())
                            S.wrote(tok)
                            kt.read(tok); vtm.read(tok)
                            if c < NCH - 1:
                                if first:
                                    tok = pb.op("dve", (lambda e, S=S, SF=SF:
                                                        e.tensor_copy(SF.ap, S.ap)),
                                        after=[S.wtok] + SF.deps_w())
                                else:
                                    tok = pb.op("dve", (lambda e, S=S, SF=SF:
                                                        e.tensor_add(SF.ap, SF.ap,
                                                                     S.ap)),
                                        after=[S.wtok, SF.wtok])
                                SF.wrote(tok); S.read(tok)
                                tok = pb.op("dve", (lambda e, SF=SF, SB=SB:
                                                    e.tensor_copy(SB.ap, SF.ap)),
                                    after=[SF.wtok] + SB.deps_w())
                                SB.wrote(tok); SF.read(tok)
            ylast = yb.rtoks[-1]
            for hp in range(4):
                tok = pb.dma("sp", "s_a0yst",
                             y_ap[half * 512 + hp * 128:half * 512 + (hp + 1) * 128, :],
                             yb.ap[:, hp * NT:(hp + 1) * NT],
                             after=[ylast])
                yb.read(tok)
            pb.barrier()


def ph_attn0(ctx, xn_ap, wqkv_ap, y_ap):
    ph_attn0_half(ctx, xn_ap, wqkv_ap, y_ap, 0)
    ph_attn0_half(ctx, xn_ap, wqkv_ap, y_ap, 1)


def ph_mix1(ctx, xn_ap, maa_x_ap, maa5_ap, w1maa_ap, w2m_ap, td_ap,
            w1dec_ap, w2dec_ap, xf_aps, glog_ap):
    nc, pb, es = ctx["nc"], ctx["pb"], ctx["es"]
    FNAMES = ["r", "k", "v", "w", "g"]
    with contextlib.ExitStack() as ph:
        sb = lambda n, s, d: ph.enter_context(nc.sbuf_tensor(unm(n), s, d))
        w1p = Buf(sb("m1_w1p", [128, DC * 320], BF16)[:])
        w2m = Buf(sb("m1_w2m", [128, 5 * 1024], BF16)[:])
        w1d = Buf(sb("m1_w1d", [128, DC * 64], BF16)[:])
        w2d = Buf(sb("m1_w2d", [64, 1024], BF16)[:])
        maax = Buf(sb("m1_maax", [128, 8], F32)[:])
        maa5 = Buf(sb("m1_maa5", [128, 40], F32)[:])
        tdt = Buf(sb("m1_td", [128, 8], F32)[:])
        xn = [Buf(sb("m1_xn", [128, DC * TT], BF16)[:]) for _ in range(2)]
        xp = [Buf(sb("m1_xp", [128, DC * TT], BF16)[:]) for _ in range(2)]
        dx = [Buf(sb("m1_dx", [128, DC * TT], BF16)[:]) for _ in range(2)]
        xxx = [Buf(sb("m1_xxx", [128, DC * TT], BF16)[:]) for _ in range(2)]
        m1b = Buf(sb("m1_m1", [128, TT], BF16)[:])
        m2b = Buf(sb("m1_m2", [128, TT], BF16)[:])
        m3b = Buf(sb("m1_m3", [64, TT], BF16)[:])
        xw = Buf(sb("m1_xw", [128, DC * TT], BF16)[:])
        tdcb = Buf(sb("m1_tdc", [64, TT], BF16)[:])
        tmp = [Buf(sb("m1_tmp", [128, TT], BF16)[:]) for _ in range(4)]
        xob = [Buf(sb("m1_xo", [128, TT], BF16)[:]) for _ in range(4)]
        glt = [Buf(sb("m1_gl", [128, TT], F32)[:]) for _ in range(4)]
        psum = ph.enter_context(nc.psum_tensor(unm("m1_ps"), [128, 8 * TT], F32))
        pmx = [Buf(psum[:, i * TT:(i + 1) * TT]) for i in range(3)]
        pf = [Buf(psum[:, (3 + i) * TT:(4 + i) * TT]) for i in range(2)]
        pd = Buf(psum[0:64, 5 * TT:6 * TT])
        pd2 = [Buf(psum[:, (6 + i) * TT:(7 + i) * TT]) for i in range(2)]

        for k in range(DC):
            pb.dma("pool", f"s_m1a{k}", w1p.ap[:, k * 320:(k + 1) * 320],
                   w1maa_ap[k * 128:(k + 1) * 128, :], after=w1p.deps_w())
            pb.dma("pool", f"s_m1a{k}", w1d.ap[:, k * 64:(k + 1) * 64],
                   w1dec_ap[k * 128:(k + 1) * 128, :], after=w1d.deps_w())
        pb.dma("pool", "s_m1b", w2m.ap, w2m_ap, after=w2m.deps_w())
        pb.dma("pool", "s_m1b", w2d.ap, w2dec_ap, after=w2d.deps_w())
        w1ptok = [pb.cur(f"s_m1a{k}") for k in range(DC)]
        wtok = pb.cur("s_m1b")
        w1p.wrote(w1ptok[-1]); w1d.wrote(w1ptok[-1])
        w2m.wrote(wtok); w2d.wrote(wtok)
        vt = pb.dma("sp", "s_m1v", maax.ap, maa_x_ap)
        vt = pb.dma("sp", "s_m1v", maa5.ap, maa5_ap)
        vt = pb.dma("sp", "s_m1v", tdt.ap, td_ap)
        maax.wrote(vt); maa5.wrote(vt); tdt.wrote(vt)

        for t in range(NTT):
            ib = t % 2
            XN, XP, DX, XXX = xn[ib], xp[ib], dx[ib], xxx[ib]
            for c in range(DC):
                pb.dma("sp", f"s_m1x{t % 3}", XN.ap[:, c * TT:(c + 1) * TT],
                       xn_ap[c * 128:(c + 1) * 128, t * TT:(t + 1) * TT],
                       after=XN.deps_w())
            xt = pb.cur(f"s_m1x{t % 3}")
            XN.wrote(xt)
            zs = []
            if t % 4 == 0:
                for c in range(DC):
                    zs.append(pb.op("pool", (lambda e, c=c, XP=XP: e.memset(
                        XP.ap[:, c * TT:c * TT + 1], 0.0)), after=XP.deps_w()))
                for c in range(DC):
                    pb.dma("act", f"s_m1p{t % 3}",
                           XP.ap[:, c * TT + 1:(c + 1) * TT],
                           xn_ap[c * 128:(c + 1) * 128,
                                 t * TT:(t + 1) * TT - 1],
                           after=XP.deps_w() + zs)
            else:
                for c in range(DC):
                    pb.dma("act", f"s_m1p{t % 3}",
                           XP.ap[:, c * TT:(c + 1) * TT],
                           xn_ap[c * 128:(c + 1) * 128,
                                 t * TT - 1:(t + 1) * TT - 1],
                           after=XP.deps_w())
            pt = pb.cur(f"s_m1p{t % 3}")
            XP.wrote(pt)
            XP.rtoks.extend(zs)
            dtoks = []
            for c in range(DC):
                eng = "dve" if c % 2 == 0 else "pool"
                sl = slice(c * TT, (c + 1) * TT)
                tok = pb.op(eng, (lambda e, sl=sl, DX=DX, XP=XP, XN=XN:
                                  e.tensor_sub(DX.ap[:, sl], XP.ap[:, sl],
                                               XN.ap[:, sl])),
                            after=[XP.wtok, XN.wtok] + XP.rtoks[-DC:] + DX.deps_w())
                DX.wrote(tok) if c == DC - 1 else None
                dtoks.append(tok)
                XP.read(tok); XN.read(tok)
                tok2 = pb.op("dve", (lambda e, c=c, sl=sl, DX=DX, XN=XN, XXX=XXX:
                                   e.scalar_tensor_tensor(
                                       XXX.ap[:, sl], DX.ap[:, sl],
                                       maax.ap[:, c:c + 1], XN.ap[:, sl],
                                       ALU.mult, ALU.add)),
                             after=[dtoks[c], maax.wtok, XN.wtok] + XXX.deps_w())
                XXX.wrote(tok2) if c == DC - 1 else None
                XXX.rtoks.append(tok2)
                maax.read(tok2); XN.read(tok2)
            dxtok = dtoks[-1]
            xxtok = XXX.rtoks[-1]
            mts = []
            for sbi, (c0, c1, dst) in enumerate(
                    ((0, 128, m1b), (128, 256, m2b), (256, 320, m3b))):
                p = pmx[sbi]
                mtok = None
                for k in range(DC):
                    mtok = pb.op("pe", (lambda e, k=k, c0=c0, c1=c1, p=p,
                                        XXX=XXX: e.matmul(
                        p.ap[0:c1 - c0, :],
                        w1p.ap[:, k * 320 + c0:k * 320 + c1],
                        XXX.ap[:, k * TT:(k + 1) * TT],
                        start=(k == 0), stop=(k == DC - 1))),
                        after=[xxtok, w1ptok[k]] + (p.deps_w() if k == 0 else []))
                p.wrote(mtok)
                w1p.read(mtok)
                tok = pb.op("act", (lambda e, p=p, dst=dst, n=c1 - c0:
                                    e.activation(dst.ap, p.ap[0:n, :], AF.Tanh)),
                            after=[p.wtok] + dst.deps_w())
                dst.wrote(tok); p.read(tok)
                mts.append(tok)
            rhs_f = [(m1b, 0), (m1b, 64), (m2b, 0), (m2b, 64), (m3b, 0)]
            for f in range(5):
                src_b, base = rhs_f[f]
                for c in range(DC):
                    p = pf[c % 2]
                    mtok = pb.op("pe", (lambda e, f=f, c=c, p=p, src_b=src_b,
                                        base=base: e.matmul(
                        p.ap,
                        w2m.ap[base:base + 32,
                               f * 1024 + c * 128:f * 1024 + (c + 1) * 128],
                        src_b.ap[base:base + 32, :],
                        start=True, stop=True)),
                        after=[src_b.wtok, w2m.wtok] + p.deps_w())
                    p.wrote(mtok)
                    src_b.read(mtok); w2m.read(mtok)
                    tb = tmp[c % 4]
                    tok = pb.op("dve", (lambda e, f=f, c=c, p=p, tb=tb, DX=DX:
                                        e.scalar_tensor_tensor(
                        tb.ap, p.ap, maa5.ap[:, f * 8 + c:f * 8 + c + 1],
                        DX.ap[:, c * TT:(c + 1) * TT], ALU.add, ALU.mult)),
                        after=[p.wtok, dxtok, maa5.wtok] + tb.deps_w())
                    tb.wrote(tok); p.read(tok); DX.read(tok); maa5.read(tok)
                    if f == 3:
                        dst_ap = xw.ap[:, c * TT:(c + 1) * TT]
                        dstb = xw
                    else:
                        dstb = xob[c % 4]
                        dst_ap = dstb.ap
                    tok2 = pb.op("pool", (lambda e, c=c, tb=tb, XN=XN,
                                          dst_ap=dst_ap: e.tensor_add(
                        dst_ap, tb.ap, XN.ap[:, c * TT:(c + 1) * TT])),
                        after=[tb.wtok, XN.wtok] + dstb.deps_w())
                    dstb.wrote(tok2)
                    tb.read(tok2); XN.read(tok2)
                    if f != 3:
                        fn = FNAMES[f]
                        stok = pb.dma("sp", f"s_m1s{c % 4}",
                                      xf_aps[fn][c * 128:(c + 1) * 128,
                                                 t * TT:(t + 1) * TT],
                                      dst_ap, after=[tok2])
                        dstb.read(stok)
            mtok = None
            for k in range(DC):
                mtok = pb.op("pe", (lambda e, k=k: e.matmul(
                    pd.ap, w1d.ap[:, k * 64:(k + 1) * 64],
                    xw.ap[:, k * TT:(k + 1) * TT],
                    start=(k == 0), stop=(k == DC - 1))),
                    after=[xw.wtok, w1ptok[k]] + (pd.deps_w() if k == 0 else []))
            pd.wrote(mtok)
            w1d.read(mtok); xw.read(mtok)
            tok = pb.op("act", (lambda e: e.activation(
                tdcb.ap, pd.ap, AF.Tanh)), after=[pd.wtok] + tdcb.deps_w())
            tdcb.wrote(tok); pd.read(tok)
            for c in range(DC):
                p = pd2[c % 2]
                mtok = pb.op("pe", (lambda e, c=c, p=p: e.matmul(
                    p.ap, w2d.ap[:, c * 128:(c + 1) * 128], tdcb.ap,
                    start=True, stop=True)),
                    after=[tdcb.wtok, w2d.wtok] + p.deps_w())
                p.wrote(mtok)
                tdcb.read(mtok); w2d.read(mtok)
                gl = glt[c % 4]
                tok = pb.op("act", (lambda e, c=c, p=p, gl=gl: e.activation(
                    gl.ap, p.ap, AF.Exp, bias=tdt.ap[:, c:c + 1])),
                    after=[p.wtok, tdt.wtok] + gl.deps_w())
                gl.wrote(tok); p.read(tok); tdt.read(tok)
                stok = pb.dma("act", f"s_m1g{c % 4}",
                              glog_ap[c * 128:(c + 1) * 128, t * TT:(t + 1) * TT],
                              gl.ap, after=[tok])
                gl.read(stok)
        pb.barrier()


def ph_qkvg1(ctx, xf_aps, glog_ap, wq_ap, wk_ap, wv_ap, wg_ap, bg_ap,
             q_ap, k_ap, vtm_ap, gate_ap):
    nc, pb, es = ctx["nc"], ctx["pb"], ctx["es"]
    with contextlib.ExitStack() as ph:
        sb = lambda n, s, d: ph.enter_context(nc.sbuf_tensor(unm(n), s, d))
        ws = {}
        for nm, ap in (("q", wq_ap), ("k", wk_ap), ("v", wv_ap), ("g", wg_ap)):
            ws[nm] = Buf(sb(f"qg_w{nm}", [128, DC * 1024], BF16)[:])
            for k in range(DC):
                pb.dma("pool", f"s_qgw{nm}{k % 2}",
                       ws[nm].ap[:, k * 1024:(k + 1) * 1024],
                       ap[k * 128:(k + 1) * 128, :], after=ws[nm].deps_w())
            ws[nm].wrote(pb.cur(f"s_qgw{nm}1"))
        wtoks = {nm: [pb.cur(f"s_qgw{nm}{k % 2}") for k in range(DC)]
                 for nm in ("q", "k", "v", "g")}
        bg = Buf(sb("qg_bg", [128, 8], F32)[:])
        bg.wrote(pb.dma("sp", "s_qgb", bg.ap, bg_ap))
        xt = {nm: [Buf(sb(f"qg_x{nm}", [128, DC * TT], BF16)[:])
                   for _ in range(2)] for nm in ("r", "k", "v", "g")}
        et = [Buf(sb("qg_e", [128, DC * TT], F32)[:]) for _ in range(2)]
        wf = [Buf(sb("qg_wf", [128, TT], F32)[:]) for _ in range(2)]
        vout = [Buf(sb("qg_vo", [128, 1024], BF16)[:]) for _ in range(2)]
        outt = [Buf(sb("qg_ot", [128, TT], BF16)[:]) for _ in range(4)]
        psum = ph.enter_context(nc.psum_tensor(unm("qg_ps"), [128, 8 * TT], F32))
        ps = [Buf(psum[:, i * TT:(i + 1) * TT]) for i in range(8)]

        for t in range(NTT):
            ib = t % 2
            for nm in ("r", "k", "v", "g"):
                X = xt[nm][ib]
                for c in range(DC):
                    pb.dma("sp" if nm in ("r", "k") else "act",
                           f"s_qgx{nm}{t % 3}", X.ap[:, c * TT:(c + 1) * TT],
                           xf_aps[nm][c * 128:(c + 1) * 128,
                                      t * TT:(t + 1) * TT], after=X.deps_w())
                X.wrote(pb.cur(f"s_qgx{nm}{t % 3}"))
            E = et[ib]
            for c in range(DC):
                pb.dma("act", f"s_qge{t % 3}", E.ap[:, c * TT:(c + 1) * TT],
                       glog_ap[c * 128:(c + 1) * 128, t * TT:(t + 1) * TT],
                       after=E.deps_w())
            E.wrote(pb.cur(f"s_qge{t % 3}"))
            ko = [Buf(None) for _ in range(4)]
            for nm, src_nm, dst_ap in (("q", "r", q_ap), ("k", "k", k_ap),
                                       ("g", "g", gate_ap)):
                X = xt[src_nm][ib]
                for oc in range(DC):
                    p = ps[oc % 4]
                    mtok = None
                    for k in range(DC):
                        mtok = pb.op("pe", (lambda e, k=k, oc=oc, p=p, X=X,
                                            nm=nm: e.matmul(
                            p.ap,
                            ws[nm].ap[:, k * 1024 + oc * 128:
                                      k * 1024 + oc * 128 + 128],
                            X.ap[:, k * TT:(k + 1) * TT],
                            start=(k == 0), stop=(k == DC - 1))),
                            after=[X.wtok, wtoks[nm][k]] +
                                  (p.deps_w() if k == 0 else []))
                    p.wrote(mtok)
                    ws[nm].read(mtok); X.read(mtok)
                    ob = outt[oc % 4]
                    if nm == "q":
                        tok = pb.op("act", (lambda e, p=p, ob=ob: e.activation(
                            ob.ap, p.ap, AF.Copy, scale=0.125)),
                            after=[p.wtok] + ob.deps_w())
                        ob.wrote(tok); p.read(tok)
                    elif nm == "g":
                        tok = pb.op("act", (lambda e, p=p, ob=ob, oc=oc:
                                            e.activation(
                            ob.ap, p.ap, AF.Silu, bias=bg.ap[:, oc:oc + 1])),
                            after=[p.wtok, bg.wtok] + ob.deps_w())
                        ob.wrote(tok); p.read(tok); bg.read(tok)
                    else:
                        W = wf[oc % 2]
                        tok = pb.op("act", (lambda e, oc=oc, W=W, E=E:
                                            e.activation(
                            W.ap, E.ap[:, oc * TT:(oc + 1) * TT], AF.Exp,
                            scale=-1.0)), after=[E.wtok] + W.deps_w())
                        W.wrote(tok); E.read(tok)
                        tok = pb.op("dve", (lambda e, W=W: e.tensor_scalar(
                            W.ap, W.ap, -1.0, 1.0, ALU.mult, ALU.add)),
                            after=[W.wtok])
                        W.wrote(tok)
                        tok = pb.op("dve", (lambda e, p=p, W=W, ob=ob:
                                            e.tensor_mul(ob.ap, p.ap, W.ap)),
                            after=[p.wtok, W.wtok] + ob.deps_w())
                        ob.wrote(tok); p.read(tok); W.read(tok)
                    stok = pb.dma("sp", f"s_qgs{oc % 4}",
                                  dst_ap[oc * 128:(oc + 1) * 128,
                                         t * TT:(t + 1) * TT],
                                  ob.ap, after=[ob.wtok])
                    ob.read(stok)
            X = xt["v"][ib]
            for ts_ in range(4):
                VO = vout[ts_ % 2]
                vtoks = []
                for c2 in range(2):
                    p = ps[4 + 2 * (ts_ % 2) + c2]
                    mtok = None
                    for k in range(DC):
                        mtok = pb.op("pe", (lambda e, k=k, ts_=ts_, c2=c2, p=p,
                                            X=X: e.matmul(
                            p.ap,
                            X.ap[:, k * TT + ts_ * 128:k * TT + ts_ * 128 + 128],
                            ws["v"].ap[:, k * 1024 + c2 * 512:
                                       k * 1024 + c2 * 512 + 512],
                            start=(k == 0), stop=(k == DC - 1))),
                            after=[X.wtok, wtoks["v"][k]] +
                                  (p.deps_w() if k == 0 else []))
                    p.wrote(mtok)
                    ws["v"].read(mtok); X.read(mtok)
                    tok = pb.op("act", (lambda e, p=p, c2=c2, VO=VO: e.copy(
                        VO.ap[:, c2 * 512:(c2 + 1) * 512], p.ap)),
                        after=[p.wtok] + VO.deps_w())
                    vtoks.append(tok)
                    p.read(tok)
                VO.wrote(vtoks[-1])
                VO.rtoks = [vtoks[0]]
                r0 = t * TT + ts_ * 128
                stok = pb.dma("act", f"s_qgvs{ts_ % 2}",
                              vtm_ap[r0:r0 + 128, :], VO.ap,
                              after=vtoks)
                VO.read(stok)
        pb.barrier()


def ph_scan1(ctx, q_ap, k_ap, vtm_ap, glog_ap, o_ap):
    nc, pb, es = ctx["nc"], ctx["pb"], ctx["es"]
    trilT = ctx["trilT_f32"]
    ident = ctx["ident_bf"]
    CH = 128
    NCH = T // CH
    with contextlib.ExitStack() as ph:
        sb = lambda n, s, d: ph.enter_context(nc.sbuf_tensor(unm(n), s, d))
        qs = [Buf(sb("sc_q", [128, T], BF16)[:]) for _ in range(4)]
        ks = [Buf(sb("sc_k", [128, T], BF16)[:]) for _ in range(4)]
        es_ = [Buf(sb("sc_e", [128, T], F32)[:]) for _ in range(4)]
        vs = [Buf(sb("sc_v", [128, T], BF16)[:]) for _ in range(4)]
        ob = [Buf(sb("sc_o", [128, T], BF16)[:]) for _ in range(4)]
        zer = Buf(sb("sc_zero", [128, CH], F32)[:])
        NSD = 6
        cums = [[Buf(sb("sc_cum", [128, CH], F32)[:]) for _ in range(NSD)]
                for _ in range(2)]
        f1bs = [[Buf(sb("sc_f1", [128, 1], F32)[:]) for _ in range(NSD)]
                for _ in range(2)]
        aabs = [[Buf(sb("sc_a", [128, CH], F32)[:]) for _ in range(NSD)]
                for _ in range(2)]
        ambs = [[Buf(sb("sc_am", [128, CH], F32)[:]) for _ in range(NSD)]
                for _ in range(2)]
        qts = [[Buf(sb("sc_qt", [128, CH], BF16)[:]) for _ in range(NSD)]
               for _ in range(2)]
        kts = [[Buf(sb("sc_kt", [128, CH], BF16)[:]) for _ in range(NSD)]
               for _ in range(2)]
        ktm = [[Buf(sb("sc_ktm", [128, 64], BF16)[:]) for _ in range(4)]
               for _ in range(2)]
        abf = [[Buf(sb("sc_abf", [128, CH], BF16)[:]) for _ in range(4)]
               for _ in range(2)]
        sfb = Buf(sb("sc_sf", [128, 16 * 64], F32)[:])
        sbb = Buf(sb("sc_sb", [128, 16 * 64], BF16)[:])
        stf = [[Buf(sfb.ap[par * 64:(par + 1) * 64, i * 64:(i + 1) * 64])
                for i in range(16)] for par in (0, 1)]
        stb = [[Buf(sbb.ap[par * 64:(par + 1) * 64, i * 64:(i + 1) * 64])
                for i in range(16)] for par in (0, 1)]
        psA_t = ph.enter_context(nc.psum_tensor(unm("sc_pA"), [128, 512], F32))
        psO_t = ph.enter_context(nc.psum_tensor(unm("sc_pO"), [128, 512], F32))
        psS_t = ph.enter_context(nc.psum_tensor(unm("sc_pS"), [128, 512], F32))
        psT_t = ph.enter_context(
            nc.psum_tensor(unm("sc_pT"), [128, 1024], BF16))
        pA = [[Buf(psA_t[:, (m * 2 + par) * CH:(m * 2 + par + 1) * CH])
               for par in (0, 1)] for m in range(2)]
        pO = [[Buf(psO_t[0:64, (m * 2 + par) * CH:(m * 2 + par + 1) * CH])
               for par in (0, 1)] for m in range(2)]
        pS = [[Buf(psS_t[0:64, (m * 2 + par) * 64:(m * 2 + par + 1) * 64])
               for par in (0, 1)] for m in range(2)]
        pT = [[Buf(psT_t[:, (m * 2 + par) * 64:(m * 2 + par + 1) * 64])
               for par in (0, 1)] for m in range(2)]
        ztok = pb.op("pool", (lambda e: e.memset(zer.ap, 0.0)))
        zer.wrote(ztok)

        def issue_loads(s):
            hp, b = s // NB, s % NB
            ib = s % 4
            Q, K, E, V = qs[ib], ks[ib], es_[ib], vs[ib]
            r0, c0 = hp * 128, b * T
            for (Bf, ap_, strm, qn) in ((Q, q_ap, "q", "sp"),
                                        (K, k_ap, "k", "act"),
                                        (E, glog_ap, "e", "act")):
                pb.dma(qn, f"s_sc{strm}{ib}", Bf.ap,
                       ap_[r0:r0 + 128, c0:c0 + T], after=Bf.deps_w())
                Bf.wrote(pb.cur(f"s_sc{strm}{ib}"))
            PSZ = min(512, T)
            for part in range(T // PSZ):
                pb.dma("sp", f"s_scv{ib}",
                       V.ap[:, part * PSZ:(part + 1) * PSZ].rearrange(
                           "p (a n) -> p a n", n=128),
                       vtm_ap[c0 + part * PSZ:c0 + (part + 1) * PSZ,
                              r0:r0 + 128].rearrange(
                           "(a p) n -> p a n", p=128),
                       after=V.deps_w())
            V.wrote(pb.cur(f"s_scv{ib}"))

        def emit_front(s, m, c):
            ib = s % 4
            Q, K, E = qs[ib], ks[ib], es_[ib]
            cum, f1b = cums[m][c % NSD], f1bs[m][c % NSD]
            aab, amb = aabs[m][c % NSD], ambs[m][c % NSD]
            qt, kt_ = qts[m][c % NSD], kts[m][c % NSD]
            csl = slice(c * CH, (c + 1) * CH)
            tok = pb.op("dve", (lambda e: e.tensor_tensor_scan(
                cum.ap, E.ap[:, csl], zer.ap,
                0.0, ALU.add, ALU.add)),
                after=[E.wtok, zer.wtok] + cum.deps_w())
            cum.wrote(tok); E.read(tok); zer.read(tok)
            tok = pb.op("dve", (lambda e: e.tensor_scalar_mul(
                f1b.ap, cum.ap[:, CH - 1:CH], 0.5)),
                after=[cum.wtok] + f1b.deps_w())
            f1b.wrote(tok); cum.read(tok)
            tok = pb.op("dve", (lambda e: e.tensor_scalar(
                cum.ap, cum.ap, f1b.ap[:, 0:1], None, ALU.subtract)),
                after=[cum.wtok, f1b.wtok])
            cum.wrote(tok); f1b.read(tok)
            tok = pb.op("dve", (lambda e: e.tensor_scalar(
                cum.ap, cum.ap, 40.0, -40.0, ALU.min, ALU.max)),
                after=[cum.wtok])
            cum.wrote(tok)
            tok = pb.op("act", (lambda e: e.activation(
                aab.ap, cum.ap, AF.Exp, scale=-1.0)),
                after=[cum.wtok] + aab.deps_w())
            aab.wrote(tok); cum.read(tok)
            tok = pb.op("act", (lambda e: e.activation(
                amb.ap, cum.ap, AF.Exp)),
                after=[cum.wtok] + amb.deps_w())
            amb.wrote(tok); cum.read(tok)
            tok = pb.op("act", (lambda e: e.activation(
                f1b.ap, f1b.ap, AF.Exp, scale=-1.0)),
                after=[f1b.wtok])
            f1b.wrote(tok)
            tok = pb.op("pool", (lambda e: e.tensor_mul(
                qt.ap, Q.ap[:, csl], aab.ap)),
                after=[Q.wtok, aab.wtok] + qt.deps_w())
            qt.wrote(tok); Q.read(tok); aab.read(tok)
            tok = pb.op("pool", (lambda e: e.tensor_mul(
                kt_.ap, K.ap[:, csl], amb.ap)),
                after=[K.wtok, amb.wtok] + kt_.deps_w())
            kt_.wrote(tok); K.read(tok); amb.read(tok)

        def emit_back(s, m, c):
            hp, b = s // NB, s % NB
            ib = s % 4
            V, O = vs[ib], ob[ib]
            f1b = f1bs[m][c % NSD]
            qt, kt_ = qts[m][c % NSD], kts[m][c % NSD]
            csl = slice(c * CH, (c + 1) * CH)
            for par in (0, 1):
                base = par * 64
                sidx = b * 8 + hp
                SF = stf[par][sidx]
                SBc = stb[par][sidx]
                pa, po = pA[m][par], pO[m][par]
                pS_, pt = pS[m][par], pT[m][par]
                km = ktm[m][2 * par + (c % 2)]
                ab = abf[m][2 * par + (c % 2)]
                first = (c == 0)
                if not first:
                    tok = pb.op("dve", (lambda e, SF=SF, SBc=SBc,
                                        base=base, f1b=f1b:
                                        e.tensor_scalar(
                        SBc.ap, SF.ap, f1b.ap[base:base + 64, 0:1],
                        None, ALU.mult)),
                        after=[SF.wtok, f1b.wtok] + SBc.deps_w())
                    SBc.wrote(tok); SF.read(tok); f1b.read(tok)
                tok = pb.op("pe", (lambda e, pa=pa, base=base,
                                   kt_=kt_, qt=qt: e.matmul(
                    pa.ap, kt_.ap[base:base + 64, :],
                    qt.ap[base:base + 64, :], start=True, stop=True)),
                    after=[kt_.wtok, qt.wtok] + pa.deps_w())
                pa.wrote(tok); kt_.read(tok); qt.read(tok)
                tok = pb.op("dve", (lambda e, pa=pa, ab=ab:
                                    e.scalar_tensor_tensor(
                    ab.ap, pa.ap, 1.0, trilT.ap, ALU.mult, ALU.mult)),
                    after=[pa.wtok, trilT.wtok] + ab.deps_w())
                ab.wrote(tok); pa.read(tok); trilT.read(tok)
                tok = pb.op("pe", (lambda e, pt=pt, base=base, kt_=kt_:
                                   e.transpose(
                    pt.ap, kt_.ap[base:base + 64, :],
                    ident.ap[base:base + 64, base:base + 64])),
                    after=[kt_.wtok, ident.wtok] + pt.deps_w())
                pt.wrote(tok); kt_.read(tok); ident.read(tok)
                tok = pb.op("dve", (lambda e, pt=pt, km=km:
                                    e.tensor_copy(km.ap, pt.ap)),
                    after=[pt.wtok] + km.deps_w())
                km.wrote(tok); pt.read(tok)
                v_sl = V.ap[:, c * 128 + par * 64:c * 128 + par * 64 + 64]
                if not first:
                    tok = pb.op("pe", (lambda e, po=po, SBc=SBc,
                                       base=base, qt=qt: e.matmul(
                        po.ap, SBc.ap, qt.ap[base:base + 64, :],
                        start=True, stop=False)),
                        after=[SBc.wtok, qt.wtok] + po.deps_w())
                    po.wrote(tok); SBc.read(tok); qt.read(tok)
                tok = pb.op("pe", (lambda e, po=po, v_sl=v_sl, ab=ab,
                                   first=first: e.matmul(
                    po.ap, v_sl, ab.ap, start=first, stop=True)),
                    after=[V.wtok, ab.wtok] +
                          (po.deps_w() if first else []))
                po.wrote(tok); V.read(tok); ab.read(tok)
                osl = O.ap[base:base + 64, csl]
                tok = pb.op("act", (lambda e, po=po, osl=osl: e.copy(
                    osl, po.ap)), after=[po.wtok] + O.deps_w())
                O.rtoks.append(tok)
                po.read(tok)
                tok = pb.op("pe", (lambda e, pS_=pS_, km=km, v_sl=v_sl:
                                   e.matmul(
                    pS_.ap, km.ap, v_sl, start=True, stop=True)),
                    after=[km.wtok, V.wtok] + pS_.deps_w())
                pS_.wrote(tok); km.read(tok); V.read(tok)
                if c < NCH - 1:
                    if first:
                        tok = pb.op("dve", (lambda e, pS_=pS_, SF=SF,
                                            base=base, f1b=f1b:
                                            e.tensor_scalar(
                            SF.ap, pS_.ap, f1b.ap[base:base + 64, 0:1],
                            None, ALU.mult)),
                            after=[pS_.wtok, f1b.wtok] + SF.deps_w())
                        SF.wrote(tok); pS_.read(tok); f1b.read(tok)
                    else:
                        tok = pb.op("dve", (lambda e, SF=SF, base=base,
                                            f1b=f1b: e.tensor_scalar(
                            SF.ap, SF.ap, f1b.ap[base:base + 64, 0:1],
                            None, ALU.mult)),
                            after=[SF.wtok, f1b.wtok])
                        SF.wrote(tok); f1b.read(tok)
                        tok = pb.op("dve", (lambda e, SF=SF, pS_=pS_:
                                            e.tensor_add(
                            SF.ap, SF.ap, pS_.ap)),
                            after=[SF.wtok, pS_.wtok])
                        SF.wrote(tok); pS_.read(tok)
                        tok = pb.op("dve", (lambda e, SF=SF, base=base,
                                            f1b=f1b: e.tensor_scalar(
                            SF.ap, SF.ap, f1b.ap[base:base + 64, 0:1],
                            None, ALU.mult)),
                            after=[SF.wtok, f1b.wtok])
                        SF.wrote(tok); f1b.read(tok)

        def store_o(s):
            hp, b = s // NB, s % NB
            ib = s % 4
            O = ob[ib]
            r0, c0 = hp * 128, b * T
            olast = O.rtoks[-1]
            stok = pb.dma("sp", f"s_scos{ib}", o_ap[r0:r0 + 128, c0:c0 + T],
                          O.ap, after=[olast])
            O.read(stok)

        issue_loads(0)
        issue_loads(1)
        for p in range(8):
            if p < 7:
                issue_loads(2 * p + 2)
                issue_loads(2 * p + 3)
            AHEAD = 2
            for a in range(AHEAD):
                for m in (0, 1):
                    emit_front(2 * p + m, m, a)
            for c in range(NCH):
                for m in (0, 1):
                    if c + AHEAD < NCH:
                        emit_front(2 * p + m, m, c + AHEAD)
                    emit_back(2 * p + m, m, c)
            store_o(2 * p)
            store_o(2 * p + 1)
        pb.barrier()



def ph_out1(ctx, o_ap, gate_ap, wo_ap, x_ap, z_ap, out_ap):
    nc, pb, es = ctx["nc"], ctx["pb"], ctx["es"]
    with contextlib.ExitStack() as ph:
        sb = lambda n, s, d: ph.enter_context(nc.sbuf_tensor(unm(n), s, d))
        wo = Buf(sb("o1_wo", [128, DC * 1024], BF16)[:])
        for k in range(DC):
            pb.dma("pool", f"s_o1w{k % 4}", wo.ap[:, k * 1024:(k + 1) * 1024],
                   wo_ap[k * 128:(k + 1) * 128, :], after=wo.deps_w())
        wtoks = [pb.cur(f"s_o1w{k % 4}") for k in range(DC)]
        ot = [Buf(sb("o1_o", [128, DC * TT], BF16)[:]) for _ in range(3)]
        gt = [Buf(sb("o1_g", [128, DC * TT], BF16)[:]) for _ in range(3)]
        og = [Buf(sb("o1_og", [128, DC * TT], BF16)[:]) for _ in range(2)]
        at = [Buf(sb("o1_a", [128, DC * TT], F32)[:]) for _ in range(2)]
        rb = [Buf(sb("o1_r", [128, DC * TT], F32)[:]) for _ in range(2)]
        zt2 = [Buf(sb("o1_z", [128, DC * TT], F32)[:]) for _ in range(2)] \
            if z_ap is not None else None
        psum = ph.enter_context(nc.psum_tensor(unm("o1_ps"), [128, 8 * TT], F32))
        ps = [Buf(psum[:, i * TT:(i + 1) * TT]) for i in range(8)]
        for t in range(NTT):
            ib = t % 2
            OT, GT = ot[t % 3], gt[t % 3]
            OG, AT, RB = og[ib], at[ib], rb[ib]
            for c in range(DC):
                pb.dma("sp", f"s_o1o{t % 3}", OT.ap[:, c * TT:(c + 1) * TT],
                       o_ap[c * 128:(c + 1) * 128, t * TT:(t + 1) * TT],
                       after=OT.deps_w())
            OT.wrote(pb.cur(f"s_o1o{t % 3}"))
            for c in range(DC):
                pb.dma("act", f"s_o1g{t % 3}", GT.ap[:, c * TT:(c + 1) * TT],
                       gate_ap[c * 128:(c + 1) * 128, t * TT:(t + 1) * TT],
                       after=GT.deps_w())
            GT.wrote(pb.cur(f"s_o1g{t % 3}"))
            for c in range(DC):
                pb.dma("act", f"s_o1x{t % 3}", AT.ap[:, c * TT:(c + 1) * TT],
                       x_ap[c * 128:(c + 1) * 128, t * TT:(t + 1) * TT],
                       after=AT.deps_w())
            AT.wrote(pb.cur(f"s_o1x{t % 3}"))
            if zt2 is not None:
                ZT = zt2[ib]
                for c in range(DC):
                    pb.dma("sp", f"s_o1z{t % 3}", ZT.ap[:, c * TT:(c + 1) * TT],
                           z_ap[c * 128:(c + 1) * 128, t * TT:(t + 1) * TT],
                           after=ZT.deps_w())
                ZT.wrote(pb.cur(f"s_o1z{t % 3}"))
            ogt = []
            for c in range(DC):
                eng = "dve" if c % 2 == 0 else "pool"
                sl = slice(c * TT, (c + 1) * TT)
                tok = pb.op(eng, (lambda e, sl=sl, OT=OT, GT=GT, OG=OG:
                                  e.tensor_mul(OG.ap[:, sl], OT.ap[:, sl],
                                               GT.ap[:, sl])),
                            after=[OT.wtok, GT.wtok] + OG.deps_w())
                ogt.append(tok)
                OT.read(tok); GT.read(tok)
            OG.wrote(ogt[-1])
            OG.rtoks = list(ogt[:-1])
            for oc in range(DC):
                p = ps[oc]
                mtok = None
                for k in range(DC):
                    mtok = pb.op("pe", (lambda e, k=k, oc=oc, p=p, OG=OG:
                                        e.matmul(
                        p.ap, wo.ap[:, k * 1024 + oc * 128:
                                    k * 1024 + oc * 128 + 128],
                        OG.ap[:, k * TT:(k + 1) * TT],
                        start=(k == 0), stop=(k == DC - 1))),
                        after=[ogt[k], wtoks[k]] + (p.deps_w() if k == 0 else []))
                p.wrote(mtok)
                wo.read(mtok); OG.read(mtok)
                tok = pb.op("dve", (lambda e, oc=oc, p=p, RB=RB, AT=AT:
                                    e.tensor_add(RB.ap[:, oc * TT:(oc + 1) * TT],
                                                 p.ap,
                                                 AT.ap[:, oc * TT:(oc + 1) * TT])),
                            after=[p.wtok, AT.wtok] + RB.deps_w())
                RB.wrote(tok); p.read(tok); AT.read(tok)
                if zt2 is not None:
                    ZT = zt2[ib]
                    tok = pb.op("pool", (lambda e, oc=oc, RB=RB, ZT=ZT:
                                         e.tensor_add(
                        RB.ap[:, oc * TT:(oc + 1) * TT],
                        RB.ap[:, oc * TT:(oc + 1) * TT],
                        ZT.ap[:, oc * TT:(oc + 1) * TT])),
                        after=[RB.wtok, ZT.wtok])
                    RB.wrote(tok); ZT.read(tok)
                stok = pb.dma("sp", f"s_o1s{t % 3}",
                              out_ap[oc * 128:(oc + 1) * 128, t * TT:(t + 1) * TT],
                              RB.ap[:, oc * TT:(oc + 1) * TT], after=[RB.wtok])
                RB.read(stok)
        pb.barrier()


def ph_wo0(ctx, y_ap, wo_ap, acc_ap, out_ap):
    nc, pb, es = ctx["nc"], ctx["pb"], ctx["es"]
    with contextlib.ExitStack() as ph:
        sb = lambda n, s, d: ph.enter_context(nc.sbuf_tensor(unm(n), s, d))
        wo = Buf(sb("w0_wo", [128, DC * 1024], BF16)[:])
        yt = [Buf(sb("w0_y", [128, DC * TT], BF16)[:]) for _ in range(3)]
        at = [Buf(sb("w0_a", [128, DC * TT], F32)[:]) for _ in range(3)]
        ob = [Buf(sb("w0_o", [128, DC * TT], F32)[:]) for _ in range(3)]
        psum = ph.enter_context(nc.psum_tensor(unm("w0_ps"), [128, 8 * TT], F32))
        ps = [Buf(psum[:, i * TT:(i + 1) * TT]) for i in range(8)]
        for k in range(DC):
            pb.dma("pool", f"s_wo{k % 4}",
                   wo.ap[:, k * 1024:(k + 1) * 1024],
                   wo_ap[k * 128:(k + 1) * 128, :], after=wo.deps_w())
        wtoks = [pb.cur(f"s_wo{k % 4}") for k in range(DC)]
        for t in range(NTT):
            YT, AT, OB = yt[t % 3], at[t % 3], ob[t % 3]
            ytoks = []
            for c in range(DC):
                tok = pb.dma("sp", f"s_woy{t % 3}",
                             YT.ap[:, c * TT:(c + 1) * TT],
                             y_ap[c * 128:(c + 1) * 128, t * TT:(t + 1) * TT],
                             after=YT.deps_w())
                ytoks.append(tok)
            YT.wrote(ytoks[-1])
            atoks = []
            for c in range(DC):
                tok = pb.dma("act", f"s_woa{t % 3}",
                             AT.ap[:, c * TT:(c + 1) * TT],
                             acc_ap[c * 128:(c + 1) * 128, t * TT:(t + 1) * TT],
                             after=AT.deps_w())
                atoks.append(tok)
            AT.wrote(atoks[-1])
            for oc in range(DC):
                p = ps[oc]
                mtok = None
                for k in range(DC):
                    mtok = pb.op("pe", (lambda e, k=k, oc=oc, p=p, YT=YT: e.matmul(
                        p.ap, wo.ap[:, k * 1024 + oc * 128:k * 1024 + oc * 128 + 128],
                        YT.ap[:, k * TT:(k + 1) * TT],
                        start=(k == 0), stop=(k == DC - 1))),
                        after=[YT.wtok, wtoks[k]] + (p.deps_w() if k == 0 else []))
                p.wrote(mtok)
                wo.read(mtok); YT.read(mtok)
                tok = pb.op("dve", (lambda e, oc=oc, p=p, OB=OB, AT=AT:
                                    e.tensor_add(OB.ap[:, oc * TT:(oc + 1) * TT],
                                                 p.ap,
                                                 AT.ap[:, oc * TT:(oc + 1) * TT])),
                            after=[p.wtok, AT.wtok] + OB.deps_w())
                OB.wrote(tok); p.read(tok); AT.read(tok)
                stok = pb.dma("act", f"s_wost{t % 3}",
                              out_ap[oc * 128:(oc + 1) * 128, t * TT:(t + 1) * TT],
                              OB.ap[:, oc * TT:(oc + 1) * TT], after=[OB.wtok])
                OB.read(stok)
        pb.barrier()


def ph_l1f_attn(ctx, z_ap, wqkv_ap, qs_ap, ks_ap, o_ap, lidx):
    nc, pb, es = ctx["nc"], ctx["pb"], ctx["es"]
    ident = ctx["ident_bf"]
    bdiag = ctx["bdiag_bf"]
    Pmat = ctx["rotP_bf"]
    QCS, QSS, KCS, KSS = (ctx["rot_qcs"], ctx["rot_qss"],
                          ctx["rot_kcs"], ctx["rot_kss"])
    MT = ctx["maddt"]
    trilF = ctx["trilT_f32"]
    VW = 65
    with contextlib.ExitStack() as ph:
        sb = lambda n, s, d: ph.enter_context(nc.sbuf_tensor(unm(n), s, d))
        wqkv = Buf(sb("lf_wqkv", [128, DC * 1536], BF16)[:])
        for k in range(DC):
            pb.dma("pool", f"s_lfw{k % 4}", wqkv.ap[:, k * 1536:(k + 1) * 1536],
                   wqkv_ap[k * 128:(k + 1) * 128, :], after=wqkv.deps_w())
        wtoks = [pb.cur(f"s_lfw{k % 4}") for k in range(DC)]
        wqkv.wrote(wtoks[-1])
        qs8 = Buf(sb("lf_qs8", [128, 4], F32)[:])
        pb.dma("sp", "s_lfqs", qs8.ap[:, 0:2], qs_ap, after=qs8.deps_w())
        pb.dma("sp", "s_lfqs", qs8.ap[:, 2:4], ks_ap, after=qs8.deps_w())
        qs8.wrote(pb.cur("s_lfqs"))
        tok = pb.op("dve", (lambda e: e.tensor_scalar_mul(
            qs8.ap[:, 0:2], qs8.ap[:, 0:2], 8.0)), after=[qs8.wtok])
        qs8.wrote(tok)
        qf = Buf(sb("lf_qf", [128, 4 * NT], BF16)[:])
        kf = Buf(sb("lf_kf", [128, 4 * NT], BF16)[:])
        vtm = Buf(sb("lf_vtm", [128, (NT // 128) * VW * 8], BF16)[:])
        vtm3 = vtm.ap.rearrange("p (t h e) -> p t h e", h=8, e=VW)
        otoks = []
        for h8 in range(8):
            otoks.append(pb.op("pool", (lambda e, h8=h8: e.memset(
                vtm3[:, :, h8, 64:65], 1.0)), after=vtm.deps_w()))
        vtm.rtoks.extend(otoks)

        with contextlib.ExitStack() as st1:
            sb1 = lambda n, s, d: st1.enter_context(nc.sbuf_tensor(unm(n), s, d))
            zt = [Buf(sb1("lf_z", [128, DC * TT], F32)[:]) for _ in range(1)]
            hb = [Buf(sb1("lf_h", [128, DC * TT], BF16)[:]) for _ in range(2)]
            NSC = 3
            raws = [Buf(sb1("lf_raw", [128, TT], BF16)[:]) for _ in range(NSC)]
            sqs = [Buf(sb1("lf_sq", [128, TT], BF16)[:]) for _ in range(NSC)]
            facs = [Buf(sb1("lf_fac", [128, TT], F32)[:]) for _ in range(NSC)]
            qns = [Buf(sb1("lf_qn", [128, TT], BF16)[:]) for _ in range(NSC)]
            t1bs = [Buf(sb1("lf_t1", [128, TT], F32)[:]) for _ in range(NSC)]
            t2bs = [Buf(sb1("lf_t2", [128, TT], F32)[:]) for _ in range(NSC)]
            psum = st1.enter_context(
                nc.psum_tensor(unm("lf_ps"), [128, 8 * TT], F32))
            ps = [Buf(psum[:, i * TT:(i + 1) * TT]) for i in range(8)]
            git = [0]
            pend1 = [None]
            pend2 = [None]

            def emit_s1(st):
                raw, sq, p, pn = st["raw"], st["sq"], st["p"], st["pn"]
                fac = st["fac"]
                tok = pb.op("act", (lambda e: e.copy(
                    raw.ap, p.ap)), after=[p.wtok] + raw.deps_w())
                raw.wrote(tok); p.read(tok)
                tok = pb.op("pool", (lambda e: e.tensor_mul(
                    sq.ap, raw.ap, raw.ap)),
                    after=[raw.wtok] + sq.deps_w())
                sq.wrote(tok); raw.read(tok)
                tok = pb.op("pe", (lambda e: e.matmul(
                    pn.ap, bdiag.ap, sq.ap, start=True, stop=True)),
                    after=[sq.wtok, bdiag.wtok] + pn.deps_w())
                pn.wrote(tok); sq.read(tok); bdiag.read(tok)
                tok = pb.op("dve", (lambda e: e.tensor_scalar_max(
                    fac.ap, pn.ap, 1e-24)),
                    after=[pn.wtok] + fac.deps_w())
                fac.wrote(tok); pn.read(tok)

            def emit_s2(st):
                raw, fac, qn = st["raw"], st["fac"], st["qn"]
                t1b, t2b, pr = st["t1b"], st["t2b"], st["pr"]
                which, dstb, t_, cp = st["which"], st["dstb"], st["t"], st["cp"]
                tok = pb.op("act", (lambda e: e.activation(
                    fac.ap, fac.ap, AF.Ln)), after=[fac.wtok])
                fac.wrote(tok)
                tok = pb.op("act", (lambda e: e.activation(
                    fac.ap, fac.ap, AF.Exp, scale=-0.5)),
                    after=[fac.wtok])
                fac.wrote(tok)
                sc_col = lidx if which == "q" else 2 + lidx
                tok = pb.op("dve", (lambda e, sc_col=sc_col:
                                    e.scalar_tensor_tensor(
                    qn.ap, raw.ap, qs8.ap[:, sc_col:sc_col + 1],
                    fac.ap, ALU.mult, ALU.mult)),
                    after=[raw.wtok, fac.wtok, qs8.wtok] + qn.deps_w())
                qn.wrote(tok); raw.read(tok); fac.read(tok)
                qs8.read(tok)
                tok = pb.op("pe", (lambda e: e.matmul(
                    pr.ap, Pmat.ap, qn.ap, start=True, stop=True)),
                    after=[qn.wtok, Pmat.wtok] + pr.deps_w())
                pr.wrote(tok); qn.read(tok); Pmat.read(tok)
                CS = QCS if which == "q" else KCS
                SS = QSS if which == "q" else KSS
                tb_sl = slice((t_ % 4) * TT, (t_ % 4 + 1) * TT)
                tok = pb.op("dve", (lambda e, SS=SS, tb_sl=tb_sl:
                                    e.tensor_mul(
                    t1b.ap, pr.ap, SS.ap[:, tb_sl])),
                    after=[pr.wtok, SS.wtok] + t1b.deps_w())
                t1b.wrote(tok); pr.read(tok); SS.read(tok)
                tok = pb.op("pool", (lambda e, CS=CS, tb_sl=tb_sl:
                                     e.tensor_mul(
                    t2b.ap, qn.ap, CS.ap[:, tb_sl])),
                    after=[qn.wtok, CS.wtok] + t2b.deps_w())
                t2b.wrote(tok); qn.read(tok); CS.read(tok)
                sl = slice(cp * NT + t_ * TT, cp * NT + (t_ + 1) * TT)
                tok = pb.op("dve", (lambda e, sl=sl: e.tensor_add(
                    dstb.ap[:, sl], t1b.ap, t2b.ap)),
                    after=[t1b.wtok, t2b.wtok] + dstb.deps_w())
                dstb.rtoks.append(tok)
                t1b.read(tok); t2b.read(tok)

            for t in range(NTT):
                ib = t % 2
                Z, HB = zt[0], hb[ib]
                for c in range(4):
                    pb.dma("sp", f"s_lfz{t % 3}", Z.ap[:, c * TT:(c + 1) * TT],
                           z_ap[c * 128:(c + 1) * 128, t * TT:(t + 1) * TT],
                           after=Z.deps_w())
                zs = []
                if t % 4 == 0:
                    for c in range(4, DC):
                        zs.append(pb.op("pool", (lambda e, c=c, Z=Z: e.memset(
                            Z.ap[:, c * TT:c * TT + 1], 0.0)),
                            after=Z.deps_w()))
                    for c in range(4, DC):
                        pb.dma("sp", f"s_lfz{t % 3}",
                               Z.ap[:, c * TT + 1:(c + 1) * TT],
                               z_ap[c * 128:(c + 1) * 128,
                                    t * TT:(t + 1) * TT - 1],
                               after=Z.deps_w() + zs)
                else:
                    for c in range(4, DC):
                        pb.dma("sp", f"s_lfz{t % 3}",
                               Z.ap[:, c * TT:(c + 1) * TT],
                               z_ap[c * 128:(c + 1) * 128,
                                    t * TT - 1:(t + 1) * TT - 1],
                               after=Z.deps_w())
                Z.wrote(pb.cur(f"s_lfz{t % 3}"))
                Z.rtoks.extend(zs)
                htoks = []
                for c in range(DC):
                    eng = "dve" if c % 2 == 0 else "pool"
                    tok = pb.op(eng, (lambda e, c=c, Z=Z, HB=HB: e.tensor_copy(
                        HB.ap[:, c * TT:(c + 1) * TT],
                        Z.ap[:, c * TT:(c + 1) * TT])),
                        after=[Z.wtok] + HB.deps_w())
                    htoks.append(tok)
                    Z.read(tok)
                HB.wrote(htoks[-1])
                HB.rtoks = htoks[:-1]
                for ts_ in range(4):
                    p = ps[4 + ts_ % 2]
                    mtok = None
                    for k in range(DC):
                        mtok = pb.op("pe", (lambda e, k=k, ts_=ts_, p=p, HB=HB:
                                            e.matmul(
                            p.ap,
                            HB.ap[:, k * TT + ts_ * 128:k * TT + ts_ * 128 + 128],
                            wqkv.ap[:, k * 1536 + 1024:k * 1536 + 1536],
                            start=(k == 0), stop=(k == DC - 1))),
                            after=[htoks[k], wtoks[k]] +
                                  (p.deps_w() if k == 0 else []))
                    p.wrote(mtok)
                    wqkv.read(mtok); HB.read(mtok)
                    tc = t * 4 + ts_
                    tok = pb.op("act", (lambda e, p=p, tc=tc: e.copy(
                        vtm3[:, tc, :, 0:64],
                        p.ap.rearrange("p (h e) -> p h e", e=64))),
                        after=[p.wtok] + vtm.deps_w())
                    vtm.rtoks.append(tok)
                    p.read(tok)
                for wi, (which, dstb, col0) in enumerate(
                        (("q", qf, 0), ("k", kf, 512))):
                    for cp in range(4):
                        it = git[0]
                        git[0] += 1
                        st = {
                            "raw": raws[it % NSC], "sq": sqs[it % NSC],
                            "fac": facs[it % NSC], "qn": qns[it % NSC],
                            "t1b": t1bs[it % NSC], "t2b": t2bs[it % NSC],
                            "p": ps[cp], "pn": ps[6 + it % 2],
                            "pr": ps[4 + it % 2], "which": which,
                            "dstb": dstb, "t": t, "cp": cp,
                        }
                        p = st["p"]
                        mtok = None
                        for k in range(DC):
                            mtok = pb.op("pe", (lambda e, k=k, cp=cp, p=p,
                                                HB=HB, col0=col0: e.matmul(
                                p.ap,
                                wqkv.ap[:, k * 1536 + col0 + cp * 128:
                                        k * 1536 + col0 + cp * 128 + 128],
                                HB.ap[:, k * TT:(k + 1) * TT],
                                start=(k == 0), stop=(k == DC - 1))),
                                after=[htoks[k], wtoks[k]] +
                                      (p.deps_w() if k == 0 else []))
                        p.wrote(mtok)
                        wqkv.read(mtok); HB.read(mtok)
                        if pend1[0] is not None:
                            emit_s1(pend1[0])
                        if pend2[0] is not None:
                            emit_s2(pend2[0])
                        pend2[0] = pend1[0]
                        pend1[0] = st
            if pend1[0] is not None:
                emit_s1(pend1[0])
            if pend2[0] is not None:
                emit_s2(pend2[0])
            if pend1[0] is not None:
                emit_s2(pend1[0])
            pend1[0] = None
            pend2[0] = None
            for bb in (qf, kf, vtm):
                bb.wrote(bb.rtoks[-1])
                bb.rtoks = []
            pb.barrier()

        with contextlib.ExitStack() as st2:
            sb2 = lambda n, s, d: st2.enter_context(nc.sbuf_tensor(unm(n), s, d))
            EQb = [Buf(sb2("lf_eq", [128, 1024], BF16)[:]) for _ in range(4)]
            NRb = [Buf(sb2("lf_nr", [1, 512], F32)[:]) for _ in range(2)]
            OSb = [Buf(sb2("lf_os", [64, 512], BF16)[:]) for _ in range(2)]
            FINb = [Buf(sb2("lf_fin", [64, 512], BF16)[:]) for _ in range(2)]
            cbits = Buf(sb2("lf_cb", [128, 4], F32)[:])
            negCb = Buf(cbits.ap[:, 0:1])
            ctmp = Buf(cbits.ap[:, 1:2])
            crow = Buf(sb2("lf_cr", [1, 128], F32)[:])
            pSc = [Buf(st2.enter_context(
                nc.psum_tensor(unm("lf_pSc"), [128, 1024], F32))[:])
                for _ in range(2)]
            pOv = [Buf(st2.enter_context(
                nc.psum_tensor(unm("lf_pO"), [128, 512], F32))[0:VW, :])
                for _ in range(2)]
            pB = [Buf(st2.enter_context(
                nc.psum_tensor(unm("lf_pB"), [128, 512], F32))[:])
                for _ in range(2)]
            tok = pb.op("dve", (lambda e: e.tensor_mul(
                ctmp.ap, qs8.ap[:, lidx:lidx + 1],
                qs8.ap[:, 2 + lidx:3 + lidx])),
                after=[qs8.wtok] + cbits.deps_w())
            ctmp.wrote(tok); qs8.read(tok)
            tok = pb.op("act", (lambda e: e.activation(
                ctmp.ap, ctmp.ap, AF.Abs)), after=[ctmp.wtok])
            ctmp.wrote(tok)
            pb0 = pB[0]
            tok = pb.op("pe", (lambda e, pb0=pb0: e.transpose(
                pb0.ap[0:1, 0:128], ctmp.ap, ctx["ident_f32"].ap)),
                after=[ctmp.wtok, ctx["ident_f32"].wtok] + pb0.deps_w())
            pb0.wrote(tok); ctmp.read(tok); ctx["ident_f32"].read(tok)
            tok = pb.op("act", (lambda e, pb0=pb0: e.copy(
                crow.ap, pb0.ap[0:1, 0:128])),
                after=[pb0.wtok] + crow.deps_w())
            crow.wrote(tok); pb0.read(tok)
            tok = pb.op("dve", (lambda e: e.tensor_reduce(
                crow.ap[0:1, 0:1], crow.ap, AX.X, ALU.max, negate=True)),
                after=[crow.wtok])
            crow.wrote(tok)
            tok = pb.op("pe", (lambda e, pb0=pb0: e.matmul(
                pb0.ap[:, 0:1], trilF.ap[0:1, 0:128], crow.ap[0:1, 0:1],
                start=True, stop=True)),
                after=[crow.wtok, trilF.wtok] + pb0.deps_w())
            pb0.wrote(tok); crow.read(tok); trilF.read(tok)
            tok = pb.op("act", (lambda e, pb0=pb0: e.copy(
                negCb.ap, pb0.ap[:, 0:1])), after=[pb0.wtok] + negCb.deps_w())
            negCb.wrote(tok); pb0.read(tok)
            NRbf = [Buf(sb2("lf_nrb", [1, 512], BF16)[:]) for _ in range(2)]
            onesb = ctx["ones_bf"]
            ii_g = 0
            pq_g = 0
            pend_av = None
            pend_epi = None

            def emit_avs(EQ_, pr2, pov, st8):
                for j, ss in enumerate(pr2):
                    lo = (ss - 4) * 128 if ss >= 4 else 0
                    hi = 512 if ss >= 4 else (ss + 1) * 128
                    tci = (st8["b"] * (T // 128)) + ((st8["w"] - 1) * 4 + ss)
                    v_sl = vtm.ap[:, tci * 8 * VW + st8["h"] * VW:
                                  tci * 8 * VW + st8["h"] * VW + VW]
                    esl = EQ_.ap[:, j * 512 + lo:j * 512 + hi]
                    avi = st8["avi"]
                    nav = st8["nav"]
                    tk = pb.op("pe", (lambda e, v_sl=v_sl, esl=esl, lo=lo,
                                      hi=hi, avi=avi, nav=nav, pov=pov:
                                      e.matmul(
                        pov.ap[:, lo:hi], v_sl, esl,
                        start=(avi == 0), stop=(avi == nav - 1))),
                        after=[EQ_.wtok, vtm.wtok] +
                              (pov.deps_w() if avi == 0 else []))
                    pov.wrote(tk); EQ_.read(tk); vtm.read(tk)
                    st8["avi"] += 1

            def mk_epilogue(pov, NR_, NB_, OS_, FIN_, pbc, b, h, w):
                def epi_a():
                    tk = pb.op("act", (lambda e: e.copy(
                        NR_.ap, pov.ap[64:65, :])),
                        after=[pov.wtok] + NR_.deps_w())
                    NR_.wrote(tk); pov.read(tk)
                    tk = pb.op("act", (lambda e: e.activation(
                        NR_.ap, NR_.ap, AF.Ln)), after=[NR_.wtok])
                    NR_.wrote(tk)
                    tk = pb.op("act", (lambda e: e.activation(
                        NB_.ap, NR_.ap, AF.Exp, scale=-1.0)),
                        after=[NR_.wtok] + NB_.deps_w())
                    NB_.wrote(tk); NR_.read(tk)
                    tk = pb.op("act", (lambda e: e.copy(
                        OS_.ap, pov.ap[0:64, :])),
                        after=[pov.wtok] + OS_.deps_w())
                    OS_.wrote(tk); pov.read(tk)

                def epi_b():
                    tk = pb.op("pe", (lambda e: e.matmul(
                        pbc.ap[0:64, :], onesb.ap[0:1, 0:64], NB_.ap,
                        start=True, stop=True)),
                        after=[NB_.wtok, onesb.wtok] + pbc.deps_w())
                    pbc.wrote(tk); NB_.read(tk); onesb.read(tk)
                    tk = pb.op("dve", (lambda e: e.tensor_mul(
                        FIN_.ap, OS_.ap, pbc.ap[0:64, :])),
                        after=[OS_.wtok, pbc.wtok] + FIN_.deps_w())
                    FIN_.wrote(tk); OS_.read(tk); pbc.read(tk)
                    stok = pb.dma("sp", f"s_lfo{ii_g % 2}",
                                  o_ap[h * 64:(h + 1) * 64,
                                       b * T + w * 512:b * T + (w + 1) * 512],
                                  FIN_.ap, after=[FIN_.wtok])
                    FIN_.read(stok)
                return epi_a, epi_b

            for b in range(NB):
                for h in range(F_HEADS):
                    cp, par = h // 2, h % 2
                    base = par * 64
                    for w in range(F_NW):
                        ip = ii_g % 2
                        first = (w == 0)
                        kcol0 = cp * NT + b * T + (w - 1) * 512
                        qcol0 = cp * NT + b * T + w * 512
                        pov = pOv[ip]
                        q_sl = qf.ap[base:base + 64, qcol0:qcol0 + 512]
                        sslist = list(range(4, 8)) + \
                            ([] if first else list(range(0, 4)))
                        pairs = [(sslist[2 * i], sslist[2 * i + 1])
                                 for i in range(len(sslist) // 2)]
                        st8 = {"b": b, "h": h, "w": w, "avi": 0,
                               "nav": len(sslist)}
                        av_jobs = []
                        for pi, pr2 in enumerate(pairs):
                            psc = pSc[pq_g % 2]
                            EQ_ = EQb[pq_g % 4]
                            pq_g += 1
                            for j, ss in enumerate(pr2):
                                lo = 0 if ss >= 4 else 0
                                lo = (ss - 4) * 128 if ss >= 4 else 0
                                hi = 512 if ss >= 4 else (ss + 1) * 128
                                k_sl = kf.ap[base:base + 64,
                                             kcol0 + ss * 128:
                                             kcol0 + (ss + 1) * 128]
                                qv = qf.ap[base:base + 64,
                                           qcol0 + lo:qcol0 + hi]
                                pv = psc.ap[:, j * 512 + lo:j * 512 + hi]
                                tk = pb.op("pe", (lambda e, pv=pv, k_sl=k_sl,
                                                  qv=qv: e.matmul(
                                    pv, k_sl, qv, start=True, stop=False)),
                                    after=[kf.wtok, qf.wtok] +
                                          (psc.deps_w() if j == 0 else []))
                                psc.wrote(tk); kf.read(tk); qf.read(tk)
                                mtsl = MT.ap[:, ss * 512 + lo:ss * 512 + hi]
                                tk = pb.op("pe", (lambda e, pv=pv, mtsl=mtsl:
                                                  e.matmul(
                                    pv, ident.ap, mtsl,
                                    start=False, stop=True)),
                                    after=[ident.wtok, MT.wtok])
                                psc.wrote(tk)
                                ident.read(tk); MT.read(tk)
                                esl = EQ_.ap[:, j * 512 + lo:j * 512 + hi]
                                psl = psc.ap[:, j * 512 + lo:j * 512 + hi]
                                tk = pb.op("act", (lambda e, psl=psl, esl=esl:
                                                   e.activation(
                                    esl, psl, AF.Exp,
                                    bias=negCb.ap)),
                                    after=[psc.wtok, negCb.wtok] +
                                          (EQ_.deps_w() if j == 0 else []))
                                EQ_.wrote(tk); psc.read(tk); negCb.read(tk)
                            av_jobs.append((EQ_, pr2))
                            if pi == 0:
                                if pend_av is not None:
                                    emit_avs(*pend_av)
                                    pend_av = None
                                if pend_epi is not None:
                                    pend_epi[0]()
                            if pi == 1 and pend_epi is not None:
                                pend_epi[1]()
                                pend_epi = None
                            if pi >= 1:
                                EQp, prp = av_jobs[pi - 1]
                                emit_avs(EQp, prp, pov, st8)
                        pend_av = (av_jobs[-1][0], av_jobs[-1][1], pov, st8)
                        pend_epi = mk_epilogue(pov, NRb[ip], NRbf[ip], OSb[ip],
                                               FINb[ip], pB[ip], b, h, w)
                        ii_g += 1
            if pend_av is not None:
                emit_avs(*pend_av)
            if pend_epi is not None:
                pend_epi[0]()
                pend_epi[1]()
            pb.barrier()


def ph_l1f_wo(ctx, o_ap, wo_ap, z_ap):
    nc, pb, es = ctx["nc"], ctx["pb"], ctx["es"]
    KC = 4
    with contextlib.ExitStack() as ph:
        sb = lambda n, s, d: ph.enter_context(nc.sbuf_tensor(unm(n), s, d))
        wo = Buf(sb("lw_wo", [128, KC * 1024], BF16)[:])
        for k in range(KC):
            pb.dma("pool", f"s_lww{k}", wo.ap[:, k * 1024:(k + 1) * 1024],
                   wo_ap[k * 128:(k + 1) * 128, :], after=wo.deps_w())
        wtoks = [pb.cur(f"s_lww{k}") for k in range(KC)]
        yt = [Buf(sb("lw_y", [128, KC * TT], BF16)[:]) for _ in range(3)]
        at = [Buf(sb("lw_a", [128, DC * TT], F32)[:]) for _ in range(3)]
        ob = [Buf(sb("lw_o", [128, DC * TT], F32)[:]) for _ in range(3)]
        psum = ph.enter_context(nc.psum_tensor(unm("lw_ps"), [128, 8 * TT], F32))
        ps = [Buf(psum[:, i * TT:(i + 1) * TT]) for i in range(8)]
        for t in range(NTT):
            ib = t % 3
            YT, AT, OB = yt[ib], at[ib], ob[ib]
            for c in range(KC):
                pb.dma("sp", f"s_lwy{t % 3}", YT.ap[:, c * TT:(c + 1) * TT],
                       o_ap[c * 128:(c + 1) * 128, t * TT:(t + 1) * TT],
                       after=YT.deps_w())
            YT.wrote(pb.cur(f"s_lwy{t % 3}"))
            for c in range(DC):
                pb.dma("act", f"s_lwa{t % 3}", AT.ap[:, c * TT:(c + 1) * TT],
                       z_ap[c * 128:(c + 1) * 128, t * TT:(t + 1) * TT],
                       after=AT.deps_w())
            AT.wrote(pb.cur(f"s_lwa{t % 3}"))
            for oc in range(DC):
                p = ps[oc]
                mtok = None
                for k in range(KC):
                    mtok = pb.op("pe", (lambda e, k=k, oc=oc, p=p, YT=YT:
                                        e.matmul(
                        p.ap, wo.ap[:, k * 1024 + oc * 128:
                                    k * 1024 + oc * 128 + 128],
                        YT.ap[:, k * TT:(k + 1) * TT],
                        start=(k == 0), stop=(k == KC - 1))),
                        after=[YT.wtok, wtoks[k]] + (p.deps_w() if k == 0 else []))
                p.wrote(mtok)
                wo.read(mtok); YT.read(mtok)
                tok = pb.op("dve", (lambda e, oc=oc, p=p, OB=OB, AT=AT:
                                    e.tensor_add(OB.ap[:, oc * TT:(oc + 1) * TT],
                                                 p.ap,
                                                 AT.ap[:, oc * TT:(oc + 1) * TT])),
                            after=[p.wtok, AT.wtok] + OB.deps_w())
                OB.wrote(tok); p.read(tok); AT.read(tok)
                stok = pb.dma("act", f"s_lws{t % 3}",
                              z_ap[oc * 128:(oc + 1) * 128, t * TT:(t + 1) * TT],
                              OB.ap[:, oc * TT:(oc + 1) * TT], after=[OB.wtok])
                OB.read(stok)
        pb.barrier()


def ph_l1f_ffn_half(ctx, z_ap, w1_ap, b1a_ap, b1g_ap, w2_ap, b2_ap,
                    lng_ap, lnb_ap, lidx, half, xnf_ap=None):
    nc, pb, es = ctx["nc"], ctx["pb"], ctx["es"]
    ones_r = ctx["ones_f32r"]
    NP = 11
    p0 = half * 11
    with contextlib.ExitStack() as ph:
        sb = lambda n, s, d: ph.enter_context(nc.sbuf_tensor(unm(n), s, d))
        w1 = Buf(sb("ff_w1", [128, DC * 2816], BF16)[:])
        w2 = Buf(sb("ff_w2", [128, NP * 1024], BF16)[:])
        for k in range(DC):
            pb.dma("pool", f"s_ffw1{k}", w1.ap[:, k * 2816:(k + 1) * 2816],
                   w1_ap[k * 128:(k + 1) * 128, p0 * 256:p0 * 256 + 2816],
                   after=w1.deps_w())
        w1toks = [pb.cur(f"s_ffw1{k}") for k in range(DC)]
        for k in range(NP):
            pb.dma("pool", f"s_ffw2{k % 8}", w2.ap[:, k * 1024:(k + 1) * 1024],
                   w2_ap[(p0 + k) * 128:(p0 + k + 1) * 128, :],
                   after=w2.deps_w())
        w2toks = [pb.cur(f"s_ffw2{k % 8}") for k in range(NP)]
        vecs = Buf(sb("ff_vecs", [128, 136], F32)[:])
        b1a = Buf(vecs.ap[:, 0:44]); b1g = Buf(vecs.ap[:, 44:88])
        b2v = Buf(vecs.ap[:, 88:104]); lng = Buf(vecs.ap[:, 104:120])
        lnb = Buf(vecs.ap[:, 120:136])
        pb.dma("sp", "s_ffv", b1a.ap, b1a_ap, after=vecs.deps_w())
        pb.dma("sp", "s_ffv", b1g.ap, b1g_ap)
        pb.dma("sp", "s_ffv", b2v.ap, b2_ap)
        pb.dma("sp", "s_ffv", lng.ap, lng_ap)
        pb.dma("sp", "s_ffv", lnb.ap, lnb_ap)
        vt = pb.cur("s_ffv")
        for bb in (b1a, b1g, b2v, lng, lnb, vecs):
            bb.wrote(vt)

        zt = [Buf(sb("ff_z", [128, DC * TT], F32R)[:]) for _ in range(2)]
        zr = [Buf(sb("ff_zr", [128, DC * TT], F32)[:]) for _ in range(2)]
        xnf = [Buf(sb("ff_xn", [128, DC * TT], BF16)[:]) for _ in range(2)]
        hbt = [Buf(sb("ff_h", [128, NP * TT], BF16)[:]) for _ in range(1)]
        gl = Buf(sb("ff_gl", [128, TT], BF16)[:])
        M = Buf(sb("ff_m", [128, TT], F32)[:])
        V = Buf(sb("ff_v", [128, TT], F32)[:])
        RS = Buf(sb("ff_rs", [128, TT], F32)[:])
        SC = [Buf(sb("ff_sc", [128, TT], F32)[:]) for _ in range(2)]
        SQs = [Buf(sb("ff_sq", [128, TT], F32R)[:]) for _ in range(2)]
        OB = [Buf(sb("ff_o", [128, TT], F32)[:]) for _ in range(3)]
        psum = ph.enter_context(nc.psum_tensor(unm("ff_ps"), [128, 8 * TT], F32))
        ps = [Buf(psum[:, i * TT:(i + 1) * TT]) for i in range(8)]

        def emit_ln(t):
            ib = t % 2
            Z, ZR, XN = zt[ib], zr[ib], xnf[ib]
            for c in range(DC):
                pb.dma("act", f"s_ffzr{t % 3}", ZR.ap[:, c * TT:(c + 1) * TT],
                       z_ap[c * 128:(c + 1) * 128, t * TT:(t + 1) * TT],
                       after=ZR.deps_w())
            ZR.wrote(pb.cur(f"s_ffzr{t % 3}"))
            if half == 1:
                for c in range(DC):
                    pb.dma("sp", f"s_ffz{t % 3}", XN.ap[:, c * TT:(c + 1) * TT],
                           xnf_ap[c * 128:(c + 1) * 128, t * TT:(t + 1) * TT],
                           after=XN.deps_w())
                XN.wrote(pb.cur(f"s_ffz{t % 3}"))
                xtoks = [XN.wtok] * DC
            else:
                zs = []
                for c in range(4):
                    pb.dma("sp", f"s_ffz{t % 3}", Z.ap[:, c * TT:(c + 1) * TT],
                           z_ap[c * 128:(c + 1) * 128,
                                t * TT:(t + 1) * TT].bitcast(F32R),
                           after=Z.deps_w())
                if t % 4 == 0:
                    for c in range(4, DC):
                        zs.append(pb.op("pool", (lambda e, c=c, Z=Z: e.memset(
                            Z.ap.bitcast(F32)[:, c * TT:c * TT + 1], 0.0)),
                            after=Z.deps_w()))
                    for c in range(4, DC):
                        pb.dma("sp", f"s_ffz{t % 3}",
                               Z.ap[:, c * TT + 1:(c + 1) * TT],
                               z_ap[c * 128:(c + 1) * 128,
                                    t * TT:(t + 1) * TT - 1].bitcast(F32R),
                               after=Z.deps_w() + zs)
                else:
                    for c in range(4, DC):
                        pb.dma("sp", f"s_ffz{t % 3}",
                               Z.ap[:, c * TT:(c + 1) * TT],
                               z_ap[c * 128:(c + 1) * 128,
                                    t * TT - 1:(t + 1) * TT - 1].bitcast(F32R),
                               after=Z.deps_w())
                Z.wrote(pb.cur(f"s_ffz{t % 3}"))
                Z.rtoks.extend(zs)
                p1, p2 = ps[4], ps[5]
                mtok = None
                for c in range(DC):
                    mtok = pb.op("pe", (lambda e, c=c, Z=Z, p1=p1: e.matmul(
                        p1.ap, ones_r.ap, Z.ap[:, c * TT:(c + 1) * TT],
                        start=(c == 0), stop=(c == DC - 1))),
                        after=[Z.wtok, ones_r.wtok] +
                              (p1.deps_w() if c == 0 else []))
                p1.wrote(mtok)
                Z.read(mtok)
                mtok = None
                for c in range(DC):
                    SQ = SQs[c % 2]
                    stok = pb.op("act", (lambda e, c=c, Z=Z, SQ=SQ: e.activation(
                        SQ.ap, Z.ap.bitcast(F32)[:, c * TT:(c + 1) * TT],
                        AF.Square)), after=[Z.wtok] + SQ.deps_w())
                    SQ.wrote(stok)
                    mtok = pb.op("pe", (lambda e, c=c, p2=p2, SQ=SQ: e.matmul(
                        p2.ap, ones_r.ap, SQ.ap,
                        start=(c == 0), stop=(c == DC - 1))),
                        after=[stok, ones_r.wtok] + (p2.deps_w() if c == 0 else []))
                    SQ.read(mtok)
                p2.wrote(mtok)
                tok = pb.op("dve", (lambda e, p1=p1: e.tensor_scalar_mul(
                    M.ap, p1.ap, 1.0 / D)), after=[p1.wtok] + M.deps_w())
                M.wrote(tok); p1.read(tok)
                tok = pb.op("dve", (lambda e: e.tensor_mul(V.ap, M.ap, M.ap)),
                            after=[M.wtok] + V.deps_w())
                V.wrote(tok); M.read(tok)
                tok = pb.op("dve", (lambda e, p2=p2: e.scalar_tensor_tensor(
                    V.ap, p2.ap, 1.0 / D, V.ap, ALU.mult, ALU.subtract)),
                    after=[p2.wtok, V.wtok])
                V.wrote(tok); p2.read(tok)
                cp, ci = ctx["cpool"], ctx["cidx"][EPS]
                tok = pb.op("act", (lambda e, cp=cp, ci=ci: e.activation(
                    RS.ap, V.ap, AF.Ln, bias=cp.ap[:, ci:ci + 1])),
                    after=[V.wtok, cp.wtok] + RS.deps_w())
                RS.wrote(tok); V.read(tok); cp.read(tok)
                tok = pb.op("act", (lambda e: e.activation(
                    RS.ap, RS.ap, AF.Exp, scale=-0.5)), after=[RS.wtok])
                RS.wrote(tok)
                xtoks = []
                for c in range(DC):
                    ei = c % 2
                    Sc = SC[ei]
                    eng = "dve" if ei == 0 else "pool"
                    sl = slice(c * TT, (c + 1) * TT)
                    tok = pb.op(eng, (lambda e, sl=sl, Sc=Sc, Z=Z: e.tensor_sub(
                        Sc.ap, Z.ap.bitcast(F32)[:, sl], M.ap)),
                        after=[Z.wtok, M.wtok] + Sc.deps_w())
                    Sc.wrote(tok); M.read(tok); Z.read(tok)
                    tok = pb.op(eng, (lambda e, Sc=Sc: e.tensor_mul(
                        Sc.ap, Sc.ap, RS.ap)), after=[Sc.wtok, RS.wtok])
                    Sc.wrote(tok); RS.read(tok)
                    tok = pb.op("dve", (lambda e, c=c, sl=sl, Sc=Sc, XN=XN:
                                        e.tensor_scalar(
                        XN.ap[:, sl], Sc.ap,
                        lng.ap[:, c * 2 + lidx:c * 2 + lidx + 1],
                        lnb.ap[:, c * 2 + lidx:c * 2 + lidx + 1],
                        ALU.mult, ALU.add)),
                        after=[Sc.wtok, lng.wtok, lnb.wtok] + XN.deps_w())
                    xtoks.append(tok)
                    Sc.read(tok); lng.read(tok); lnb.read(tok)
                XN.wrote(xtoks[-1])
                XN.rtoks = xtoks[:-1]
                for c in range(DC):
                    pb.dma("sp", f"s_ffxs{t % 3}",
                           xnf_ap[c * 128:(c + 1) * 128, t * TT:(t + 1) * TT],
                           XN.ap[:, c * TT:(c + 1) * TT], after=[xtoks[c]])
                XN.rtoks.append(pb.cur(f"s_ffxs{t % 3}"))
            if half == 1:
                return xtoks
            return xtoks

        def emit_mm(t, xtoks):
            ib = t % 2
            XN, HB = xnf[ib], hbt[0]
            htoks = []
            for j in range(NP):
                pa, pg = ps[(j % 2) * 2], ps[(j % 2) * 2 + 1]
                mtok = None
                for k in range(DC):
                    mtok = pb.op("pe", (lambda e, k=k, j=j, pa=pa, XN=XN:
                                        e.matmul(
                        pa.ap, w1.ap[:, k * 2816 + j * 256:k * 2816 + j * 256 + 128],
                        XN.ap[:, k * TT:(k + 1) * TT],
                        start=(k == 0), stop=(k == DC - 1))),
                        after=[xtoks[k], w1toks[k]] + (pa.deps_w() if k == 0 else []))
                pa.wrote(mtok)
                w1.read(mtok); XN.read(mtok)
                mtok = None
                for k in range(DC):
                    mtok = pb.op("pe", (lambda e, k=k, j=j, pg=pg, XN=XN:
                                        e.matmul(
                        pg.ap, w1.ap[:, k * 2816 + j * 256 + 128:
                                     k * 2816 + j * 256 + 256],
                        XN.ap[:, k * TT:(k + 1) * TT],
                        start=(k == 0), stop=(k == DC - 1))),
                        after=[xtoks[k], w1toks[k]] + (pg.deps_w() if k == 0 else []))
                pg.wrote(mtok)
                w1.read(mtok); XN.read(mtok)
                jj = p0 + j
                tok = pb.op("act", (lambda e, pg=pg, jj=jj: e.activation(
                    gl.ap, pg.ap, AF.Gelu,
                    bias=b1g.ap[:, jj * 2 + lidx:jj * 2 + lidx + 1])),
                    after=[pg.wtok, b1g.wtok] + gl.deps_w())
                gl.wrote(tok); pg.read(tok); b1g.read(tok)
                tok = pb.op("dve", (lambda e, pa=pa, j=j, jj=jj, HB=HB:
                                    e.scalar_tensor_tensor(
                    HB.ap[:, j * TT:(j + 1) * TT], pa.ap,
                    b1a.ap[:, jj * 2 + lidx:jj * 2 + lidx + 1],
                    gl.ap, ALU.add, ALU.mult)),
                    after=[pa.wtok, gl.wtok, b1a.wtok] + HB.deps_w())
                htoks.append(tok)
                pa.read(tok); gl.read(tok); b1a.read(tok)
            HB.wrote(htoks[-1])
            HB.rtoks = htoks[:-1]
            return htoks

        def emit_w2(t, htoks):
            ib = t % 2
            ZR, HB = zr[ib], hbt[0]
            for oc in range(DC):
                OBt = OB[oc % 3]
                p = ps[6 + oc % 2]
                mtok = None
                for k in range(NP):
                    mtok = pb.op("pe", (lambda e, k=k, oc=oc, p=p, HB=HB:
                                        e.matmul(
                        p.ap, w2.ap[:, k * 1024 + oc * 128:
                                    k * 1024 + oc * 128 + 128],
                        HB.ap[:, k * TT:(k + 1) * TT],
                        start=(k == 0), stop=(k == NP - 1))),
                        after=[htoks[k], w2toks[k]] + (p.deps_w() if k == 0 else []))
                p.wrote(mtok)
                w2.read(mtok); HB.read(mtok)
                if half == 0:
                    tok = pb.op("dve", (lambda e, oc=oc, p=p, OBt=OBt, ZR=ZR:
                                        e.scalar_tensor_tensor(
                        OBt.ap, p.ap,
                        b2v.ap[:, oc * 2 + lidx:oc * 2 + lidx + 1],
                        ZR.ap[:, oc * TT:(oc + 1) * TT], ALU.add, ALU.add)),
                        after=[p.wtok, ZR.wtok, b2v.wtok] + OBt.deps_w())
                    b2v.read(tok)
                else:
                    tok = pb.op("dve", (lambda e, oc=oc, p=p, OBt=OBt, ZR=ZR:
                                        e.tensor_add(
                        OBt.ap, p.ap,
                        ZR.ap[:, oc * TT:(oc + 1) * TT])),
                        after=[p.wtok, ZR.wtok] + OBt.deps_w())
                OBt.wrote(tok); p.read(tok); ZR.read(tok)
                stok = pb.dma("act", f"s_ffs{t % 3}",
                              z_ap[oc * 128:(oc + 1) * 128, t * TT:(t + 1) * TT],
                              OBt.ap, after=[OBt.wtok])
                OBt.read(stok)

        xt_pend = emit_ln(0)
        for t in range(NTT):
            ht = emit_mm(t, xt_pend)
            xt_pend = emit_ln(t + 1) if t + 1 < NTT else None
            emit_w2(t, ht)
        pb.barrier()


def ph_copy(ctx, a_ap, out_ap, dtype_bytes=4):
    nc, pb, es = ctx["nc"], ctx["pb"], ctx["es"]
    with contextlib.ExitStack() as ph:
        sb = lambda n, s, d: ph.enter_context(nc.sbuf_tensor(unm(n), s, d))
        at = [Buf(sb("cp_a", [128, DC * TT], F32)[:]) for _ in range(2)]
        for t in range(NTT):
            A_ = at[t % 2]
            for c in range(DC):
                pb.dma("sp", f"s_cpa{t % 3}", A_.ap[:, c * TT:(c + 1) * TT],
                       a_ap[c * 128:(c + 1) * 128, t * TT:(t + 1) * TT],
                       after=A_.deps_w())
            A_.wrote(pb.cur(f"s_cpa{t % 3}"))
            for c in range(DC):
                pb.dma("act", f"s_cps{t % 3}",
                       out_ap[c * 128:(c + 1) * 128, t * TT:(t + 1) * TT],
                       A_.ap[:, c * TT:(c + 1) * TT], after=[A_.wtok])
            A_.rtoks = [pb.cur(f"s_cps{t % 3}")]
        pb.barrier()


def ph_fin(ctx, a_ap, b_ap, out_ap):
    nc, pb, es = ctx["nc"], ctx["pb"], ctx["es"]
    with contextlib.ExitStack() as ph:
        sb = lambda n, s, d: ph.enter_context(nc.sbuf_tensor(unm(n), s, d))
        at = [Buf(sb("fin_a", [128, DC * TT], F32)[:]) for _ in range(2)]
        bt = [Buf(sb("fin_b", [128, DC * TT], F32)[:]) for _ in range(2)]
        ot = [Buf(sb("fin_o", [128, DC * TT], F32)[:]) for _ in range(2)]
        for t in range(NTT):
            ib = t % 2
            A_, B_, O_ = at[ib], bt[ib], ot[ib]
            for c in range(DC):
                pb.dma("sp", f"s_fina{t % 3}", A_.ap[:, c * TT:(c + 1) * TT],
                       a_ap[c * 128:(c + 1) * 128, t * TT:(t + 1) * TT],
                       after=A_.deps_w())
            A_.wrote(pb.cur(f"s_fina{t % 3}"))
            for c in range(DC):
                pb.dma("act", f"s_finb{t % 3}", B_.ap[:, c * TT:(c + 1) * TT],
                       b_ap[c * 128:(c + 1) * 128, t * TT:(t + 1) * TT],
                       after=B_.deps_w())
            B_.wrote(pb.cur(f"s_finb{t % 3}"))
            for c in range(DC):
                eng = "dve" if c % 2 == 0 else "pool"
                sl = slice(c * TT, (c + 1) * TT)
                tok = pb.op(eng, (lambda e, sl=sl, A_=A_, B_=B_, O_=O_:
                                  e.tensor_add(O_.ap[:, sl], A_.ap[:, sl],
                                               B_.ap[:, sl])),
                            after=[A_.wtok, B_.wtok] + O_.deps_w())
                O_.rtoks.append(tok)
                A_.read(tok); B_.read(tok)
                stok = pb.dma("sp", f"s_fins{t % 3}",
                              out_ap[c * 128:(c + 1) * 128, t * TT:(t + 1) * TT],
                              O_.ap[:, sl], after=[tok])
                O_.read(stok)
            O_.wrote(O_.rtoks[-1] if O_.rtoks else None)
        pb.barrier()


def _mk_consts(ctx):
    nc, pb, es = ctx["nc"], ctx["pb"], ctx["es"]
    ones = np.ones((128, 128), dtype=np.float32)
    h = nc.inline_tensor(ones, "ones_f32")
    tile = es.enter_context(nc.sbuf_tensor("ones_r_sb", [128, 128], F32R))
    tok = pb.dma("sp", "s_const", tile[:], h.ap().bitcast(F32R))
    b = Buf(tile[:])
    b.wrote(tok)
    ctx["ones_f32r"] = b
    vals = [EPS]
    cp = np.tile(np.asarray(vals, dtype=np.float32)[None, :], (128, 1))
    hc = nc.inline_tensor(cp, "cpool")
    ct = es.enter_context(nc.sbuf_tensor("cpool_sb", [128, len(vals)], F32))
    ctok = pb.dma("sp", "s_const", ct[:], hc.ap())
    cb = Buf(ct[:])
    cb.wrote(ctok)
    ctx["cpool"] = cb
    ctx["cidx"] = {v: i for i, v in enumerate(vals)}

    def add_const(name, arr, dtype):
        hh = nc.inline_tensor(arr, name)
        tt = es.enter_context(nc.sbuf_tensor(name + "_sb", list(arr.shape), dtype))
        tk = pb.dma("sp", "s_const", tt[:], hh.ap())
        bb = Buf(tt[:])
        bb.wrote(tk)
        ctx[name] = bb
    add_const("ident_bf", np.eye(128, dtype=ml_dtypes.bfloat16), BF16)
    bd = np.zeros((128, 128), dtype=ml_dtypes.bfloat16)
    bd[0:64, 0:64] = 1
    bd[64:128, 64:128] = 1
    add_const("bdiag_bf", bd, BF16)
    add_const("trilT_f32", _tril_maskT(), F32)
    add_const("ident_f32", np.eye(128, dtype=np.float32), F32)
    add_const("ones_bf", np.ones((128, 128), dtype=ml_dtypes.bfloat16), BF16)
    add_const("rotP_bf", _rot_perm(), BF16)
    qcs, qss, kcs, kss = _rotary_tables()
    add_const("rot_qcs", qcs, BF16)
    add_const("rot_qss", qss, BF16)
    add_const("rot_kcs", kcs, BF16)
    add_const("rot_kss", kss, BF16)
    mq, mt = _locattn_masks()
    add_const("maddt", mt, BF16)
    ctx["_consts"] = ["ones_f32r", "cpool", "ident_bf", "bdiag_bf", "trilT_f32",
                      "ident_f32", "ones_bf", "rotP_bf", "rot_qcs", "rot_qss",
                      "rot_kcs", "rot_kss", "maddt"]
    fin = pb.cur("s_const")
    for nm in ctx["_consts"]:
        ctx[nm].wtok = fin


def build_program(phases, io_spec):
    nc = bass.Bass("TRN2", target_bir_lowering=False)
    es = contextlib.ExitStack()
    es.__enter__()
    dram = {}
    for name, (shape, dtype, kind) in io_spec.items():
        dram[name] = nc.dram_tensor(name, list(shape), dtype, kind=kind).ap()
    pb = PB(nc, es)
    ctx = {"nc": nc, "pb": pb, "es": es, "dram": dram}
    _mk_consts(ctx)
    for fn, kw in phases:
        fn(ctx, **{k: dram[v] if isinstance(v, str) else v for k, v in kw.items()})
    pb.emit()
    return nc, es, dram


def prep_core_x(x, core):
    xs = x[NB * core:NB * core + NB]
    return np.ascontiguousarray(xs.transpose(2, 0, 1).reshape(D, NT))


def unprep_out(outs):
    full = np.empty((B, T, D), dtype=np.float32)
    for c, o in enumerate(outs):
        full[NB * c:NB * c + NB] = np.asarray(o).reshape(D, NB, T).transpose(1, 2, 0)
    return full


def build_full(fused=True):
    D32, D16 = F32, BF16
    FM32 = ([1024, NT], F32)
    FM16 = ([1024, NT], BF16)
    inter_specs = {
        "xn0": FM16, "acc0": FM32, "y0": FM16, "x1": FM32,
        "xn1": FM16, "z": FM32, "glogE": FM32,
        "xf_r": FM16, "xf_k": FM16, "xf_v": FM16, "xf_g": FM16,
        "q1": FM16, "k1": FM16, "gate1": FM16, "o1": FM16,
        "vtm1": ([NT, 1024], BF16), "acc1": FM32,
        "o1f": ([512, NT], BF16), "xnf": FM16,
    }
    in_specs = {
        "x": FM32,
        "wqkv0": ([1024, 3072], F32), "wo0": ([1024, 1024], F32),
        "w10": ([1024, 4096], F32), "b1t0": ([128, 32], F32),
        "w20": ([4096, 1024], F32), "b2t0": ([128, 8], F32),
        "maa_x": ([128, 8], F32), "maa5": ([128, 40], F32),
        "w1maa": ([1024, 320], F32), "w2m": ([128, 5120], F32),
        "td": ([128, 8], F32), "w1dec": ([1024, 64], F32),
        "w2dec": ([64, 1024], F32),
        "wq1": ([1024, 1024], F32), "wk1": ([1024, 1024], F32),
        "wv1": ([1024, 1024], F32), "wg1": ([1024, 1024], F32),
        "wo1": ([1024, 1024], F32), "bgt": ([128, 8], F32),
        "fqkv0": ([1024, 1536], F32), "fqkv1": ([1024, 1536], F32),
        "fqs": ([128, 2], F32), "fks": ([128, 2], F32),
        "fwo0": ([512, 1024], F32), "fwo1": ([512, 1024], F32),
        "fw10": ([1024, 2 * F_HIDP], F32), "fw11": ([1024, 2 * F_HIDP], F32),
        "fb1a": ([128, 44], F32), "fb1g": ([128, 44], F32),
        "fw20": ([F_HIDP, 1024], F32), "fw21": ([F_HIDP, 1024], F32),
        "fb2": ([128, 16], F32), "flng": ([128, 16], F32),
        "flnb": ([128, 16], F32),
    }
    out_specs = {"out": FM32}

    def phase_list(ctx, d):
        xf = {"r": d["xf_r"], "k": d["xf_k"], "v": d["xf_v"], "g": d["xf_g"]}
        return [
            lambda: ph_layernorm(ctx, d["x"], d["xn0"]),
            lambda: ph_ffn0(ctx, d["xn0"], d["x"], d["w10"], d["b1t0"],
                            d["w20"], d["b2t0"], d["acc0"]),
            lambda: ph_attn0(ctx, d["xn0"], d["wqkv0"], d["y0"]),
            lambda: ph_wo0(ctx, d["y0"], d["wo0"], d["acc0"], d["x1"]),
            lambda: ph_layernorm(ctx, d["x1"], d["xn1"], z_ap=d["z"]),
            lambda: ph_mix1(ctx, d["xn1"], d["maa_x"], d["maa5"], d["w1maa"],
                            d["w2m"], d["td"], d["w1dec"], d["w2dec"], xf,
                            d["glogE"]),
            lambda: ph_qkvg1(ctx, xf, d["glogE"], d["wq1"], d["wk1"],
                             d["wv1"], d["wg1"], d["bgt"], d["q1"], d["k1"],
                             d["vtm1"], d["gate1"]),
            lambda: ph_scan1(ctx, d["q1"], d["k1"], d["vtm1"], d["glogE"],
                             d["o1"]),
            lambda: ph_l1f_attn(ctx, d["z"], d["fqkv0"], d["fqs"], d["fks"],
                                d["o1f"], 0),
            lambda: ph_l1f_wo(ctx, d["o1f"], d["fwo0"], d["z"]),
            lambda: ph_l1f_ffn_half(ctx, d["z"], d["fw10"], d["fb1a"],
                                    d["fb1g"], d["fw20"], d["fb2"], d["flng"],
                                    d["flnb"], 0, 0, d["xnf"]),
            lambda: ph_l1f_ffn_half(ctx, d["z"], d["fw10"], d["fb1a"],
                                    d["fb1g"], d["fw20"], d["fb2"], d["flng"],
                                    d["flnb"], 0, 1, d["xnf"]),
            lambda: ph_l1f_attn(ctx, d["z"], d["fqkv1"], d["fqs"], d["fks"],
                                d["o1f"], 1),
            lambda: ph_l1f_wo(ctx, d["o1f"], d["fwo1"], d["z"]),
            lambda: ph_l1f_ffn_half(ctx, d["z"], d["fw11"], d["fb1a"],
                                    d["fb1g"], d["fw21"], d["fb2"], d["flng"],
                                    d["flnb"], 1, 0, d["xnf"]),
            lambda: ph_l1f_ffn_half(ctx, d["z"], d["fw11"], d["fb1a"],
                                    d["fb1g"], d["fw21"], d["fb2"], d["flng"],
                                    d["flnb"], 1, 1, d["xnf"]),
            lambda: ph_out1(ctx, d["o1"], d["gate1"], d["wo1"], d["x1"],
                            d["z"], d["out"]),
        ]

    if fused:
        nc = bass.Bass("TRN2", target_bir_lowering=False)
        es = contextlib.ExitStack()
        es.__enter__()
        d = {}
        for nm, (shape, dt) in in_specs.items():
            d[nm] = nc.dram_tensor(nm, list(shape), dt, kind="ExternalInput").ap()
        for nm, (shape, dt) in out_specs.items():
            d[nm] = nc.dram_tensor(nm, list(shape), dt, kind="ExternalOutput").ap()
        for nm, (shape, dt) in inter_specs.items():
            d[nm] = nc.dram_tensor(nm, list(shape), dt, kind="Internal").ap()
        pb = PB(nc, es)
        ctx = {"nc": nc, "pb": pb, "es": es}
        _mk_consts(ctx)
        names = ["ln0", "ffn0", "attn0", "wo0", "ln1", "mix1", "qkvg1",
                 "scan1", "lfa0", "lfw0", "lff0a", "lff0b",
                 "lfa1", "lfw1", "lff1a", "lff1b", "out1"]
        for nm, fn in zip(names, phase_list(ctx, d)):
            pb.phase = nm
            fn()
        pb.emit()
        return [(nc, list(in_specs), list(out_specs))]
    else:
        launches = []
        all_specs = {}
        all_specs.update(in_specs)
        all_specs.update(out_specs)
        all_specs.update(inter_specs)
        pio = [
            (["x"], ["xn0"]),
            (["xn0", "x", "w10", "b1t0", "w20", "b2t0"], ["acc0"]),
            (["xn0", "wqkv0"], ["y0"]),
            (["y0", "wo0", "acc0"], ["x1"]),
            (["x1"], ["xn1", "z"]),
            (["xn1", "maa_x", "maa5", "w1maa", "w2m", "td", "w1dec", "w2dec"],
             ["xf_r", "xf_k", "xf_v", "xf_g", "glogE"]),
            (["xf_r", "xf_k", "xf_v", "xf_g", "glogE", "wq1", "wk1", "wv1",
              "wg1", "bgt"], ["q1", "k1", "vtm1", "gate1"]),
            (["q1", "k1", "vtm1", "glogE"], ["o1"]),
            (["o1", "gate1", "wo1", "x1"], ["acc1"]),
            (["z", "fqkv0", "fqs", "fks"], ["o1f"]),
            (["o1f", "fwo0", "z"], ["z"]),
            (["z", "fw10", "fb1a", "fb1g", "fw20", "fb2", "flng", "flnb"],
             ["z", "xnf"]),
            (["z", "xnf", "fw10", "fb1a", "fb1g", "fw20", "fb2", "flng",
              "flnb"], ["z"]),
            (["z", "fqkv1", "fqs", "fks"], ["o1f"]),
            (["o1f", "fwo1", "z"], ["z"]),
            (["z", "fw11", "fb1a", "fb1g", "fw21", "fb2", "flng", "flnb"],
             ["z", "xnf"]),
            (["z", "xnf", "fw11", "fb1a", "fb1g", "fw21", "fb2", "flng",
              "flnb"], ["z"]),
            (["acc1", "z"], ["out"]),
        ]
        for pi, (ins, outs) in enumerate(pio):
            nc = bass.Bass("TRN2", target_bir_lowering=False)
            es = contextlib.ExitStack()
            es.__enter__()
            d = {}
            for nm in ins:
                shape, dt = all_specs[nm]
                d[nm] = nc.dram_tensor(nm, list(shape), dt,
                                       kind="ExternalInput").ap()
            for nm in outs:
                shape, dt = all_specs[nm]
                if nm in d:
                    d[nm + "_out"] = nc.dram_tensor(
                        nm + "_out", list(shape), dt, kind="ExternalOutput").ap()
                else:
                    d[nm] = nc.dram_tensor(nm, list(shape), dt,
                                           kind="ExternalOutput").ap()
            pb = PB(nc, es)
            ctx = {"nc": nc, "pb": pb, "es": es}
            _mk_consts(ctx)
            full = phase_list(ctx, _MultiDict(d))
            full[pi]()
            pb.emit()
            launches.append((nc, ins, outs))
        return launches


class _MultiDict(dict):
    def __missing__(self, key):
        raise KeyError(key)


def prep_inputs(inputs):
    g = lambda k: np.asarray(inputs[k], dtype=np.float32)
    w = {}
    w["wqkv0"] = g("L0_attn_W_qkv")
    w["wo0"] = g("L0_attn_W_o")
    w["w10"] = g("L0_ffn_w1")
    w["b1t0"] = _perpart(g("L0_ffn_b1"))
    w["w20"] = g("L0_ffn_w2")
    w["b2t0"] = _perpart(g("L0_ffn_b2"))
    w["maa_x"] = _perpart(g("L1_attn_time_maa_x"))
    maa5 = np.zeros((128, 40), np.float32)
    for f, k in enumerate(("r", "k", "v", "w", "g")):
        maa5[:, f * 8:(f + 1) * 8] = _perpart(g(f"L1_attn_time_maa_{k}"))
    w["maa5"] = maa5
    w1maa = g("L1_attn_time_maa_w1")
    w1p = np.zeros((1024, 320), np.float32)
    for f in range(5):
        w1p[:, f * 64:f * 64 + 32] = w1maa[:, f * 32:(f + 1) * 32]
    w["w1maa"] = w1p
    w2maa = g("L1_attn_time_maa_w2")
    w2m = np.zeros((128, 5 * 1024), np.float32)
    for f in range(5):
        w2m[0:32, f * 1024:(f + 1) * 1024] = w2maa[f]
        w2m[64:96, f * 1024:(f + 1) * 1024] = w2maa[f]
    w["w2m"] = w2m
    w["td"] = _perpart(g("L1_attn_time_decay"))
    w["w1dec"] = g("L1_attn_time_decay_w1")
    w["w2dec"] = g("L1_attn_time_decay_w2")
    w["wq1"] = g("L1_attn_Wq")
    w["wk1"] = g("L1_attn_Wk")
    w["wv1"] = g("L1_attn_Wv")
    w["wg1"] = g("L1_attn_Wg")
    w["wo1"] = g("L1_attn_Wo")
    w["bgt"] = _perpart(g("L1_attn_bg"))
    fqkv = g("L1_ffn_wqkv")
    w["fqkv0"], w["fqkv1"] = (np.ascontiguousarray(fqkv[0]),
                              np.ascontiguousarray(fqkv[1]))
    qsv = np.zeros((128, 2), np.float32)
    ksv = np.zeros((128, 2), np.float32)
    for l in range(2):
        qsv[:, l] = np.tile(g("L1_ffn_q_scale")[l], 2)
        ksv[:, l] = np.tile(g("L1_ffn_k_scale")[l], 2)
    w["fqs"], w["fks"] = qsv, ksv
    fwo = g("L1_ffn_wo")
    w["fwo0"], w["fwo1"] = (np.ascontiguousarray(fwo[0]),
                            np.ascontiguousarray(fwo[1]))
    HID = F_HID
    fw1 = g("L1_ffn_w1")
    fb1 = g("L1_ffn_b1")
    b1a = np.zeros((128, 44), np.float32)
    b1g = np.zeros((128, 44), np.float32)
    for l in range(2):
        w1p_ = np.zeros((1024, 2 * F_HIDP), np.float32)
        for j in range(22):
            n = min(128, HID - j * 128)
            w1p_[:, j * 256:j * 256 + n] = fw1[l][:, j * 128:j * 128 + n]
            w1p_[:, j * 256 + 128:j * 256 + 128 + n] = \
                fw1[l][:, HID + j * 128:HID + j * 128 + n]
            b1a[0:n, j * 2 + l] = fb1[l][j * 128:j * 128 + n]
            b1g[0:n, j * 2 + l] = fb1[l][HID + j * 128:HID + j * 128 + n]
        w[f"fw1{l}"] = w1p_
    w["fb1a"], w["fb1g"] = b1a, b1g
    fw2 = g("L1_ffn_w2")
    for l in range(2):
        w2p_ = np.zeros((F_HIDP, 1024), np.float32)
        w2p_[0:HID] = fw2[l]
        w[f"fw2{l}"] = w2p_
    fb2 = g("L1_ffn_b2")
    flng = np.zeros((128, 16), np.float32)
    flnb = np.zeros((128, 16), np.float32)
    fb2t = np.zeros((128, 16), np.float32)
    for l in range(2):
        flng[:, l::2] = _perpart(g("L1_ffn_ln_g")[l])
        flnb[:, l::2] = _perpart(g("L1_ffn_ln_b")[l])
        fb2t[:, l::2] = _perpart(g("L1_ffn_b2")[l])
    w["flng"], w["flnb"], w["fb2"] = flng, flnb, fb2t
    return w


_CACHE = {}


def kernel(**inputs):
    w = prep_inputs(inputs)
    x = np.asarray(inputs["L0_attn_x"], dtype=np.float32)
    launches = build_full(fused=True)
    state = [dict(w) for _ in range(NCORE)]
    for c in range(NCORE):
        state[c]["x"] = prep_core_x(x, c)
    outs = [None] * NCORE
    for (nc, ins, outs_names) in launches:
        in_maps = []
        for c in range(NCORE):
            in_maps.append({nm: state[c][nm] for nm in ins})
        res = run_bass_kernel_spmd(nc, in_maps, core_ids=list(range(NCORE)))
        for c in range(NCORE):
            rr = res.results[c]
            for nm in outs_names:
                key = nm + "_out" if (nm + "_out") in rr else nm
                state[c][nm] = np.asarray(rr[key])
    return unprep_out([state[c]["out"] for c in range(NCORE)])
